# version 06
# baseline (speedup 1.0000x reference)
import contextlib
import math
import os

os.environ.setdefault("GAUGE_NTFF_IGNORE_DMA", "1")

import numpy as np
import ml_dtypes

import concourse.bass as bass
import concourse.mybir as mybir
from concourse.bass_utils import run_bass_kernel_spmd

F32 = mybir.dt.float32
BF16 = mybir.dt.bfloat16
ALU = mybir.AluOpType
ACT = mybir.ActivationFunctionType
AX = mybir.AxisListType
BF = ml_dtypes.bfloat16

S = 2048
DM = 2048
NCH = DM // 128
NTS = S // 128
NTC = S // 512
INTER0 = 8192
NJC0 = INTER0 // 512
NJ0 = INTER0 // 128
INTER1 = 1408
NEXP = 15
NM1 = INTER1 // 128
L1JC = (512, 512, 384)
EPS_LN = 1e-5
EPS_RMS = 1e-6
EPS_Q = 1e-5
NEG = -1e30
ISQ = 1.0 / math.sqrt(128.0)

DBG = False

ENGINES = ("tensor", "vector", "scalar", "gpsimd", "sync")
DMA_QUEUES = ("sync", "scalar", "gpsimd")
STRICT_SYNC = os.environ.get("BASS_SIM_FP64") == "1"


class Buf:
    __slots__ = ("ready", "readers", "name")

    def __init__(self, name=""):
        self.ready = {}
        self.readers = {}
        self.name = name


class Prog:
    def __init__(self, nc, gstack):
        self.nc = nc
        self.gstack = gstack
        self.csem, self.ccount = {}, {}
        self.dsem, self.dcount = {}, {}
        for e in ENGINES:
            self.csem[e] = gstack.enter_context(nc.semaphore(f"cs_{e}"))
            self.ccount[e] = 0
        self.DMA_WAYS = 4
        for q in DMA_QUEUES:
            self.dsem[q] = [gstack.enter_context(nc.semaphore(f"ds_{q}{i}"))
                            for i in range(self.DMA_WAYS)]
            self.dcount[q] = [0] * self.DMA_WAYS
            self.dissue = getattr(self, "dissue", {})
            self.dissue[q] = 0
        self.waited = {e: {} for e in ENGINES}
        self.cur = None
        self.local_stack = None
        self.n_names = 0
        self.n_instr = 0
        self.trace = []
        self._barrier = {}
        self._barrier_done = set()

    def uname(self, base):
        self.n_names += 1
        return f"{base}_{self.n_names}"

    def sb(self, base, shape, dt, scope="local"):
        stack = self.local_stack if scope == "local" else self.gstack
        h = stack.enter_context(self.nc.sbuf_tensor(self.uname(base), shape, dt))
        return (h, Buf(base))

    def sb_on(self, stack, base, shape, dt):
        h = stack.enter_context(self.nc.sbuf_tensor(self.uname(base), shape, dt))
        return (h, Buf(base))

    def ps(self, base, shape, dt):
        h = self.local_stack.enter_context(
            self.nc.psum_tensor(self.uname(base), shape, dt))
        return (h, Buf(base))

    @contextlib.contextmanager
    def phase(self, name):
        snap = {}
        for e in ENGINES:
            if self.ccount[e]:
                snap[self.csem[e]] = self.ccount[e]
        for q in DMA_QUEUES:
            for i in range(self.DMA_WAYS):
                if self.dcount[q][i]:
                    snap[self.dsem[q][i]] = 16 * self.dcount[q][i]
        self._barrier = snap
        self._barrier_done = set()
        self.cur = {e: [] for e in ENGINES}
        with contextlib.ExitStack() as stack:
            self.local_stack = stack
            yield
            qs = self.cur
            self.cur = None
            self.local_stack = None
            with self.nc.Block() as blk:
                for e in ENGINES:
                    ops = qs[e]
                    if not ops:
                        continue

                    def body(eng, ops=ops):
                        for f in ops:
                            f(eng)
                    getattr(blk, e)(body)

    def _waits(self, engine, reads, writes):
        own = self.csem[engine]
        wt = self.waited[engine]
        need = {}
        if engine not in self._barrier_done:
            self._barrier_done.add(engine)
            for s, v in self._barrier.items():
                if s is not own and wt.get(s, 0) < v:
                    need[s] = v

        def add(sem, val, skip_own=False):
            if sem is own and skip_own:
                return
            if wt.get(sem, 0) >= val:
                return
            if need.get(sem, 0) < val:
                need[sem] = val

        for b in reads:
            for s, v in b.ready.items():
                add(s, v)
        for b in writes:
            for s, v in b.ready.items():
                add(s, v, skip_own=True)
            for s, v in b.readers.items():
                add(s, v)
        q = self.cur[engine]
        for s, v in need.items():
            self.trace.append(f"{engine}: WAIT {s.name if hasattr(s,'name') else s} >= {v}")
            q.append(lambda eng, s=s, v=v: eng.wait_ge(s, v))
            wt[s] = max(wt.get(s, 0), v)

    const_dep = None

    def op(self, engine, fn, reads=(), writes=()):
        if engine == "scalar" and self.const_dep is not None:
            reads = list(reads) + [self.const_dep]
        self._waits(engine, reads, writes)
        sem = self.csem[engine]
        self.ccount[engine] += 1
        cnt = self.ccount[engine]
        self.n_instr += 1
        self.trace.append(
            f"{engine}: OP#{cnt} r=[{','.join(b.name for b in reads)}] "
            f"w=[{','.join(b.name for b in writes)}]")
        self.cur[engine].append(
            lambda eng, fn=fn, sem=sem: fn(eng).then_inc(sem, 1))
        for b in writes:
            b.ready[sem] = cnt
        for b in reads:
            b.readers[sem] = cnt

    def dma(self, queue, fn, reads=(), writes=()):
        way = self.dissue[queue] % self.DMA_WAYS
        self.dissue[queue] += 1
        dsem = self.dsem[queue][way]
        prev = 16 * self.dcount[queue][way]
        self.dcount[queue][way] += 1
        val = prev + 16
        wt = self.waited[queue]
        if prev and wt.get(dsem, 0) < prev:
            self.cur[queue].append(
                lambda eng, s=dsem, v=prev: eng.wait_ge(s, v))
            wt[dsem] = prev
        self._waits(queue, reads, writes)
        self.n_instr += 1
        self.trace.append(
            f"{queue}: DMA#{way}:{val} r=[{','.join(b.name for b in reads)}] "
            f"w=[{','.join(b.name for b in writes)}]")
        self.cur[queue].append(
            lambda eng, fn=fn, dsem=dsem: fn(eng).then_inc(dsem, 16))
        for b in writes:
            b.ready[dsem] = val
        for b in reads:
            b.readers[dsem] = val

    def barrier_wait_all(self, engine):
        q = self.cur[engine]
        for e in ENGINES:
            if e == engine:
                continue
            s, v = self.csem[e], self.ccount[e]
            if v:
                q.append(lambda eng, s=s, v=v: eng.wait_ge(s, v))
        for qq in DMA_QUEUES:
            for i in range(self.DMA_WAYS):
                s, v = self.dsem[qq][i], 16 * self.dcount[qq][i]
                if v:
                    q.append(lambda eng, s=s, v=v: eng.wait_ge(s, v))


def build_program():
    nc = bass.Bass("TRN2", target_bir_lowering=False, debug=False)

    def dram(name, shape, dt, kind):
        return nc.dram_tensor(name, shape, dt, kind=kind).ap()

    EI, EO, IN = "ExternalInput", "ExternalOutput", "Internal"

    x_in = dram("x_in", [S, DM], F32, EI)
    wq = dram("wq", [DM, DM], F32, EI)
    wk = dram("wk", [DM, 512], F32, EI)
    wv = dram("wv", [DM, 512], F32, EI)
    wo = dram("wo", [DM, DM], F32, EI)
    wgate = dram("wgate", [INTER0, DM], F32, EI)
    wup = dram("wup", [INTER0, DM], F32, EI)
    wdown = dram("wdown", [DM, INTER0], F32, EI)
    lnw_row = dram("lnw_row", [1, INTER0], F32, EI)
    lnw_colt = dram("lnw_colt", [128, NJ0], F32, EI)
    wlat = dram("wlat", [DM, 128], F32, EI)
    blat = dram("blat", [128, 1], F32, EI)
    wd = dram("wd", [DM, DM], F32, EI)
    bdh = dram("bdh", [128, 16], F32, EI)
    wproj = dram("wproj", [DM, DM], F32, EI)
    bproj_row = dram("bproj_row", [1, DM], F32, EI)
    gatew = dram("gatew", [NEXP, DM], F32, EI)
    w1 = dram("w1", [NEXP, INTER1, DM], F32, EI)
    w2 = dram("w2", [NEXP, DM, INTER1], F32, EI)
    w3 = dram("w3", [NEXP, INTER1, DM], F32, EI)
    sw1 = dram("sw1", [INTER1, DM], F32, EI)
    sw2 = dram("sw2", [DM, INTER1], F32, EI)
    sw3 = dram("sw3", [INTER1, DM], F32, EI)
    ident_c = dram("ident_c", [128, 128], BF16, EI)
    identf_c = dram("identf_c", [128, 128], F32, EI)
    ppermt_c = dram("ppermt_c", [128, 128], BF16, EI)
    masks_c = dram("masks_c", [4 * 128, 512], F32, EI)
    ones_c = dram("ones_c", [1, 128], F32, EI)
    csts_c = dram("csts_c", [128, 8], F32, EI)

    out = dram("out", [S, DM], F32, EO)
    xA = dram("xA", [S, DM], F32, EO if DBG else IN)
    xB = dram("xB", [S, DM], F32, EO if DBG else IN)
    xC = dram("xC", [S, DM], F32, EO if DBG else IN)

    wq_bf = dram("wq_bf", [DM, DM], BF16, IN)
    wk_bf = dram("wk_bf", [DM, 512], BF16, IN)
    wv_bf = dram("wv_bf", [DM, 512], BF16, IN)
    wo_bf = dram("wo_bf", [DM, DM], BF16, IN)
    wlat_bf = dram("wlat_bf", [DM, 128], BF16, IN)
    wd_bf = dram("wd_bf", [DM, DM], BF16, IN)
    wproj_bf = dram("wproj_bf", [DM, DM], BF16, IN)
    w1_bf = dram("w1_bf", [NEXP, INTER1, DM], BF16, IN)
    w2_bf = dram("w2_bf", [NEXP, DM, INTER1], BF16, IN)
    w3_bf = dram("w3_bf", [NEXP, INTER1, DM], BF16, IN)
    sw1_bf = dram("sw1_bf", [INTER1, DM], BF16, IN)
    sw2_bf = dram("sw2_bf", [DM, INTER1], BF16, IN)
    sw3_bf = dram("sw3_bf", [INTER1, DM], BF16, IN)
    tern_g = dram("tern_g", [INTER0, DM], BF16, IN)
    tern_u = dram("tern_u", [INTER0, DM], BF16, IN)
    tern_d = dram("tern_d", [DM, INTER0], BF16, IN)
    h_spill = dram("h_spill", [S, INTER0], BF16, IN)

    gs = contextlib.ExitStack()
    pr = Prog(nc, gs)

    BC = {}
    for nm in ("wq_bf", "wk_bf", "wv_bf", "wo_bf", "wlat_bf", "wd_bf",
               "wproj_bf", "w1_bf", "w2_bf", "w3_bf", "sw1_bf", "sw2_bf",
               "sw3_bf", "tern_g", "tern_u", "tern_d", "h_spill",
               "xA", "xB", "xC", "out", "x_in", "wgate", "wup", "wdown"):
        BC[nm] = Buf(nm)

    CONSTS = [0.25, math.pi / 2.0, 128.0, EPS_LN, EPS_RMS, 1.0]
    with gs:
        csts = pr.sb("csts", [128, 8], F32, scope="global")
        for i, v in enumerate(CONSTS):
            nc.const_aps.register(F32, v, csts[0][:, i:i + 1])
        pr.const_dep = csts[1]
        ident = pr.sb("ident", [128, 128], BF16, scope="global")
        identf = pr.sb("identf", [128, 128], F32, scope="global")
        ppermt = pr.sb("ppermt", [128, 128], BF16, scope="global")
        onesf = pr.sb("onesf", [1, 128], F32, scope="global")
        cos_t = pr.sb("cos_t", [128, S], F32, scope="global")
        sin_t = pr.sb("sin_t", [128, S], F32, scope="global")
        arena = pr.sb("arena", [128, 32768], BF16, scope="global")
        wm_keep = pr.sb("wm_keep", [128, 3], F32, scope="global")
        sx_keep = pr.sb("sx_keep", [128, NTS], F32, scope="global")
        isx_keep = pr.sb("isx_keep", [128, NTS], F32, scope="global")
        hstat = pr.sb("hstat", [128, 6 * NTS], F32, scope="global")
        crows = pr.sb("crows", [1, S], F32, scope="global")
        cmb = pr.sb("cmb", [128, 16 * 16], F32, scope="global")

        def A(c, lo, n):
            return arena[0][:, c * S + lo: c * S + lo + n]

        def emit_round(dst_ap, dst_buf, src_ap, src_buf, scr1, scr2, width,
                       base=128.0, nbits=8):
            d_h, d_b = scr1
            c_h, c_b = scr2
            pr.op("vector", lambda eng: eng.tensor_single_scalar(
                d_h[:, 0:width], src_ap, base, ALU.add),
                reads=[src_buf], writes=[d_b])
            bit = base
            for _ in range(nbits):
                pr.op("vector", lambda eng, bit=bit: eng.tensor_single_scalar(
                    c_h[:, 0:width], d_h[:, 0:width], bit - 0.5, ALU.is_ge),
                    reads=[d_b], writes=[c_b])
                pr.op("vector", lambda eng, bit=bit: eng.scalar_tensor_tensor(
                    d_h[:, 0:width], c_h[:, 0:width], -bit, d_h[:, 0:width],
                    ALU.mult, ALU.add),
                    reads=[c_b, d_b], writes=[d_b])
                bit *= 0.5
            pr.op("vector", lambda eng: eng.scalar_tensor_tensor(
                dst_ap, d_h[:, 0:width], -1.0, src_ap, ALU.mult, ALU.add),
                reads=[d_b, src_buf], writes=[dst_buf])

        def emit_tr128(ptps, dst_ap, dst_buf, src_ap, src_buf,
                       evac="scalar", idt=None):
            pt_h, pt_b = ptps
            idt = idt or ident
            pr.op("tensor", lambda eng: eng.transpose(
                pt_h[:, :], src_ap, idt[0][:, :]),
                reads=[src_buf, idt[1]], writes=[pt_b])
            if evac == "scalar":
                pr.op("scalar", lambda eng: eng.copy(dst_ap, pt_h[:, :]),
                      reads=[pt_b], writes=[dst_buf])
            else:
                pr.op(evac, lambda eng: eng.tensor_copy(dst_ap, pt_h[:, :]),
                      reads=[pt_b], writes=[dst_buf])

        with pr.phase("setup"):
            pr.dma("gpsimd", lambda eng: eng.dma_start(
                out=csts[0][:, :], in_=csts_c), writes=[csts[1]])
            for (dst, src, b) in ((ident, ident_c, None), (identf, identf_c, None),
                                  (ppermt, ppermt_c, None), (onesf, ones_c, None)):
                pr.dma("gpsimd", lambda eng, dst=dst, src=src: eng.dma_start(
                    out=dst[0][:, :], in_=src), writes=[dst[1]])

            def cast(dst, src, dbuf, split=1):
                step = src.shape[0] // split
                for i in range(split):
                    pr.dma("gpsimd", lambda eng, i=i, dst=dst, src=src, step=step:
                           eng.dma_start(out=dst[i * step:(i + 1) * step],
                                         in_=src[i * step:(i + 1) * step],
                                         max_dma_last_dim=8192), writes=[dbuf])

            KCAST = int(os.environ.get("KCAST", "1"))
            if KCAST:
                cast(wq_bf, wq, BC["wq_bf"])
            if KCAST:
                cast(wk_bf, wk, BC["wk_bf"])
                cast(wv_bf, wv, BC["wv_bf"])
                cast(wo_bf, wo, BC["wo_bf"])
                cast(wlat_bf, wlat, BC["wlat_bf"])
                cast(wd_bf, wd, BC["wd_bf"])
                cast(wproj_bf, wproj, BC["wproj_bf"])
                for i in range(NEXP):
                    cast(w1_bf[i], w1[i], BC["w1_bf"])
                    cast(w3_bf[i], w3[i], BC["w3_bf"])
                    cast(w2_bf[i], w2[i], BC["w2_bf"])
                cast(sw1_bf, sw1, BC["sw1_bf"])
                cast(sw3_bf, sw3, BC["sw3_bf"])
                cast(sw2_bf, sw2, BC["sw2_bf"])

            dcol = pr.sb("dcol", [128, 2], F32)
            invf = pr.sb("invf", [128, 1], F32)
            freqs = pr.sb("freqs", [128, S], F32)
            ts1 = pr.sb("ts1", [128, S], F32)
            ts2 = pr.sb("ts2", [128, S], F32)
            ts3 = pr.sb("ts3", [128, S], F32)
            ts4 = pr.sb("ts4", [128, S], F32)
            pr.op("gpsimd", lambda eng: eng.iota(
                dcol[0][:, 0:1], [[0, 1]], channel_multiplier=1,
                allow_small_or_imprecise_dtypes=True), writes=[dcol[1]])
            pr.op("vector", lambda eng: eng.tensor_scalar(
                dcol[0][:, 1:2], dcol[0][:, 0:1], 64.0, 64.0, ALU.is_ge, ALU.mult),
                reads=[dcol[1]], writes=[dcol[1]])
            pr.op("vector", lambda eng: eng.tensor_tensor(
                dcol[0][:, 0:1], dcol[0][:, 0:1], dcol[0][:, 1:2], ALU.subtract),
                reads=[dcol[1]], writes=[dcol[1]])
            pr.op("scalar", lambda eng: eng.activation(
                invf[0][:, :], dcol[0][:, 0:1], ACT.Exp, bias=0.0,
                scale=-math.log(10000.0) / 64.0), reads=[dcol[1]], writes=[invf[1]])
            pr.op("gpsimd", lambda eng: eng.iota(
                ts1[0][:, :], [[1, S]], channel_multiplier=0,
                allow_small_or_imprecise_dtypes=True), writes=[ts1[1]])
            pr.op("scalar", lambda eng: eng.activation(
                freqs[0][:, :], ts1[0][:, :], ACT.Identity, bias=0.0,
                scale=invf[0][:, :]), reads=[ts1[1], invf[1]], writes=[freqs[1]])
            clampv = float(np.float32(3.1415925))
            twopi = 2.0 * math.pi

            def table(dst, shift):
                pr.op("scalar", lambda eng: eng.activation(
                    ts1[0][:, :], freqs[0][:, :], ACT.Identity,
                    bias=shift / twopi, scale=1.0 / twopi),
                    reads=[freqs[1]], writes=[ts1[1]])
                emit_round(ts2[0][:, :], ts2[1], ts1[0][:, :], ts1[1],
                           ts3, ts4, S, base=512.0, nbits=10)
                pr.op("scalar", lambda eng: eng.activation(
                    ts1[0][:, :], freqs[0][:, :], ACT.Identity,
                    bias=shift, scale=1.0), reads=[freqs[1]], writes=[ts1[1]])
                pr.op("vector", lambda eng: eng.scalar_tensor_tensor(
                    ts1[0][:, :], ts2[0][:, :], -twopi, ts1[0][:, :],
                    ALU.mult, ALU.add), reads=[ts2[1], ts1[1]], writes=[ts1[1]])
                pr.op("vector", lambda eng: eng.tensor_single_scalar(
                    ts1[0][:, :], ts1[0][:, :], clampv, ALU.min),
                    reads=[ts1[1]], writes=[ts1[1]])
                pr.op("vector", lambda eng: eng.tensor_single_scalar(
                    ts1[0][:, :], ts1[0][:, :], -clampv, ALU.max),
                    reads=[ts1[1]], writes=[ts1[1]])
                pr.op("scalar", lambda eng: eng.activation(
                    dst[0][:, :], ts1[0][:, :], ACT.Sin, bias=0.0, scale=1.0),
                    reads=[ts1[1]], writes=[dst[1]])

            if int(os.environ.get("KTAB", "1")):
                table(sin_t, 0.0)
                table(cos_t, math.pi / 2.0)

        TERN_ON = int(os.environ.get("KTERN", "1"))
        with pr.phase("ternall"):
          wt = [pr.sb("wt", [128, 2048], F32) for _ in range(2)]
          acc = pr.sb("acc", [128, 1], F32)
          tsum = pr.sb("tsum", [128, 2], F32)
          scol = pr.sb("scol", [128, 2], F32)
          sred = pr.sb("sred", [1, 2], F32)
          srps = pr.ps("srps", [1, 512], F32)
          brps = pr.ps("brps", [128, 512], F32)
          t1 = pr.sb("t1", [128, 2048], F32)
          t2 = pr.sb("t2", [128, 2048], F32)
          to = [pr.sb("to", [128, 2048], BF16) for _ in range(2)]
          for idx, (src, sbuf_, tdst, tbuf, nrow, ncol) in enumerate(() if not TERN_ON else (
                (wgate, BC["wgate"], tern_g, BC["tern_g"], INTER0, DM),
                (wup, BC["wup"], tern_u, BC["tern_u"], INTER0, DM),
                (wdown, BC["wdown"], tern_d, BC["tern_d"], DM, INTER0))):
            if True:
                nrt, nct = nrow // 128, ncol // 2048
                pr.op("vector", lambda eng: eng.memset(acc[0][:, :], 0.0),
                      writes=[acc[1]])
                for it in range(nrt * nct):
                    r, c = it % nrt, it // nrt
                    w = wt[it % 2]
                    pr.dma("scalar", lambda eng, r=r, c=c, w=w, src=src: eng.dma_start(
                        out=w[0][:, :],
                        in_=src[r * 128:(r + 1) * 128, c * 2048:(c + 1) * 2048]),
                        reads=[sbuf_], writes=[w[1]])
                    pr.op("vector", lambda eng, w=w: eng.reduce_sum(
                        tsum[0][:, 0:1], w[0][:, :], axis=AX.X,
                        apply_absolute_value=True), reads=[w[1]], writes=[tsum[1]])
                    pr.op("vector", lambda eng: eng.tensor_tensor(
                        acc[0][:, :], acc[0][:, :], tsum[0][:, 0:1], ALU.add),
                        reads=[acc[1], tsum[1]], writes=[acc[1]])
                pr.op("tensor", lambda eng: eng.matmul(
                    srps[0][:, 0:1], csts[0][:, 5:6], acc[0][:, :],
                    start=True, stop=True),
                    reads=[csts[1], acc[1]], writes=[srps[1]])
                pr.op("scalar", lambda eng: eng.copy(
                    sred[0][:, 0:1], srps[0][:, 0:1]),
                    reads=[srps[1]], writes=[sred[1]])
                pr.op("vector", lambda eng: eng.tensor_scalar(
                    sred[0][:, 1:2], sred[0][:, 0:1], 1.0 / (nrow * ncol),
                    EPS_Q, ALU.mult, ALU.max), reads=[sred[1]], writes=[sred[1]])
                pr.op("tensor", lambda eng: eng.matmul(
                    brps[0][:, 0:1], onesf[0][:, :], sred[0][0:1, 1:2],
                    start=True, stop=True),
                    reads=[onesf[1], sred[1]], writes=[brps[1]])
                pr.op("scalar", lambda eng, idx=idx: eng.copy(
                    wm_keep[0][:, idx:idx + 1], brps[0][:, 0:1]),
                    reads=[brps[1]], writes=[wm_keep[1]])
                pr.op("vector", lambda eng, idx=idx: eng.reciprocal(
                    scol[0][:, 0:1], wm_keep[0][:, idx:idx + 1]),
                    reads=[wm_keep[1]], writes=[scol[1]])
                pr.op("vector", lambda eng: eng.tensor_single_scalar(
                    scol[0][:, 1:2], scol[0][:, 0:1], -1.0, ALU.mult),
                    reads=[scol[1]], writes=[scol[1]])
                for it in range(nrt * nct):
                    r, c = it % nrt, it // nrt
                    w = wt[it % 2]
                    o = to[it % 2]
                    pr.dma("scalar", lambda eng, r=r, c=c, w=w, src=src: eng.dma_start(
                        out=w[0][:, :],
                        in_=src[r * 128:(r + 1) * 128, c * 2048:(c + 1) * 2048]),
                        reads=[sbuf_], writes=[w[1]])
                    pr.op("vector", lambda eng, w=w: eng.tensor_scalar(
                        t1[0][:, :], w[0][:, :], scol[0][:, 0:1], 0.5,
                        ALU.mult, ALU.is_gt), reads=[w[1], scol[1]], writes=[t1[1]])
                    pr.op("vector", lambda eng, w=w: eng.tensor_scalar(
                        t2[0][:, :], w[0][:, :], scol[0][:, 1:2], 0.5,
                        ALU.mult, ALU.is_gt), reads=[w[1], scol[1]], writes=[t2[1]])
                    pr.op("gpsimd", lambda eng, o=o: eng.tensor_tensor(
                        o[0][:, :], t1[0][:, :], t2[0][:, :], ALU.subtract),
                        reads=[t1[1], t2[1]], writes=[o[1]])
                    pr.dma("gpsimd", lambda eng, r=r, c=c, o=o, tdst=tdst: eng.dma_start(
                        out=tdst[r * 128:(r + 1) * 128, c * 2048:(c + 1) * 2048],
                        in_=o[0][:, :]), reads=[o[1]], writes=[tbuf])

        def make_cols(quant):
            cols = {}
            for nm in ("rstd", "mb", "t1", "t2"):
                cols[nm] = pr.sb("col_" + nm, [128, NTS], F32)
            if quant:
                for nm in ("mx", "sx", "isx"):
                    cols[nm] = pr.sb("col_" + nm, [128, NTS], F32)
            return cols

        def emit_ln_stats(x_ap, x_buf, sq, cols, ci):
            c = slice(ci, ci + 1)
            r, m, t1c, t2c = cols["rstd"], cols["mb"], cols["t1"], cols["t2"]
            pr.op("vector", lambda eng: eng.reduce_sum(
                t1c[0][:, c], x_ap, axis=AX.X), reads=[x_buf], writes=[t1c[1]])
            pr.op("scalar", lambda eng: eng.activation(
                sq[0][:, :], x_ap, ACT.Square, bias=0.0, scale=1.0,
                accum_out=t2c[0][:, c]), reads=[x_buf], writes=[sq[1], t2c[1]])
            pr.op("vector", lambda eng: eng.tensor_single_scalar(
                t1c[0][:, c], t1c[0][:, c], 1.0 / DM, ALU.mult),
                reads=[t1c[1]], writes=[t1c[1]])
            pr.op("vector", lambda eng: eng.tensor_tensor(
                m[0][:, c], t1c[0][:, c], t1c[0][:, c], ALU.mult),
                reads=[t1c[1]], writes=[m[1]])
            pr.op("vector", lambda eng: eng.scalar_tensor_tensor(
                t2c[0][:, c], t2c[0][:, c], 1.0 / DM, m[0][:, c],
                ALU.mult, ALU.subtract), reads=[t2c[1], m[1]], writes=[t2c[1]])
            pr.op("scalar", lambda eng: eng.activation(
                t2c[0][:, c], t2c[0][:, c], ACT.Sqrt, bias=EPS_LN, scale=1.0),
                reads=[t2c[1]], writes=[t2c[1]])
            pr.op("vector", lambda eng: eng.reciprocal(
                r[0][:, c], t2c[0][:, c]), reads=[t2c[1]], writes=[r[1]])
            pr.op("vector", lambda eng: eng.scalar_tensor_tensor(
                m[0][:, c], t1c[0][:, c], -1.0, r[0][:, c], ALU.mult, ALU.mult),
                reads=[t1c[1], r[1]], writes=[m[1]])

        def emit_ln_to_fm(xsrc, xsrc_buf, quant, cols, ptps):
            xt = [pr.sb("xt", [128, DM], F32) for _ in range(2)]
            sq = pr.sb("sq", [128, DM], F32)
            xl = pr.sb("xl", [128, DM], F32)
            xlb = pr.sb("xlb", [128, DM], BF16)
            r1 = pr.sb("r1", [128, DM], F32)
            r2 = pr.sb("r2", [128, DM], F32)
            for ts in range(NTS):
                x = xt[ts % 2]
                pr.dma("sync", lambda eng, ts=ts, x=x: eng.dma_start(
                    out=x[0][:, :], in_=xsrc[ts * 128:(ts + 1) * 128, :]),
                    reads=[xsrc_buf], writes=[x[1]])
                emit_ln_stats(x[0][:, :], x[1], sq, cols, ts)
                if not quant:
                    pr.op("scalar", lambda eng, ts=ts, x=x: eng.activation(
                        xlb[0][:, :], x[0][:, :], ACT.Identity,
                        bias=cols["mb"][0][:, ts:ts + 1],
                        scale=cols["rstd"][0][:, ts:ts + 1]),
                        reads=[x[1], cols["mb"][1], cols["rstd"][1]],
                        writes=[xlb[1]])
                else:
                    mx, sx, isx = cols["mx"], cols["sx"], cols["isx"]
                    pr.op("scalar", lambda eng, ts=ts, x=x: eng.activation(
                        xl[0][:, :], x[0][:, :], ACT.Identity,
                        bias=cols["mb"][0][:, ts:ts + 1],
                        scale=cols["rstd"][0][:, ts:ts + 1]),
                        reads=[x[1], cols["mb"][1], cols["rstd"][1]],
                        writes=[xl[1]])
                    pr.op("vector", lambda eng, ts=ts: eng.reduce_max(
                        mx[0][:, ts:ts + 1], xl[0][:, :], axis=AX.X,
                        apply_absolute_value=True), reads=[xl[1]], writes=[mx[1]])
                    pr.op("vector", lambda eng, ts=ts: eng.tensor_single_scalar(
                        mx[0][:, ts:ts + 1], mx[0][:, ts:ts + 1], EPS_Q, ALU.max),
                        reads=[mx[1]], writes=[mx[1]])
                    pr.op("vector", lambda eng, ts=ts: eng.tensor_single_scalar(
                        isx[0][:, ts:ts + 1], mx[0][:, ts:ts + 1], 1.0 / 127.0,
                        ALU.mult), reads=[mx[1]], writes=[isx[1]])
                    pr.op("vector", lambda eng, ts=ts: eng.reciprocal(
                        sx[0][:, ts:ts + 1], isx[0][:, ts:ts + 1]),
                        reads=[isx[1]], writes=[sx[1]])
                    pr.op("scalar", lambda eng, ts=ts: eng.activation(
                        xl[0][:, :], xl[0][:, :], ACT.Identity, bias=0.0,
                        scale=sx[0][:, ts:ts + 1]),
                        reads=[xl[1], sx[1]], writes=[xl[1]])
                    emit_round(xlb[0][:, :], xlb[1], xl[0][:, :], xl[1],
                               r1, r2, DM)
                for c in range(NCH):
                    emit_tr128(ptps, A(c, ts * 128, 128), arena[1],
                               xlb[0][:, c * 128:(c + 1) * 128], xlb[1])

        def attention_block(name, xsrc, xsrc_buf, xdst, xdst_buf, is_l0):
            NH = 16
            KSUB = int(os.environ.get("KSUB", "99"))
            KQC = int(os.environ.get("KQC", "99"))
            KH = int(os.environ.get("KH", "99"))
            if KSUB < 2:
                return
            with contextlib.ExitStack() as ost:
                if is_l0:
                    kst = pr.sb_on(ost, "kst", [128, 4 * S], BF16)
                    vst = pr.sb_on(ost, "vst", [128, NTS * 512], BF16)
                else:
                    kst = pr.sb_on(ost, "kst", [128, S], BF16)
                    vst = pr.sb_on(ost, "vst", [128, NTS * 128], BF16)
                    bpb = pr.sb_on(ost, "bpb", [128, DM], F32)
                with pr.phase(name + "_kv"):
                    cols = make_cols(False)
                    ptps = pr.ps("ptp", [128, 128], BF16)
                    if int(os.environ.get("KLN", "1")):
                        emit_ln_to_fm(xsrc, xsrc_buf, False, cols, ptps)
                    kps = pr.ps("kps", [128, 512], F32)
                    kps2 = pr.ps("kps2", [128, 512], F32)
                    krawb = pr.sb("krawb", [128, 512], BF16)
                    t1s = pr.sb("t1s", [128, 512], F32)
                    t2s = pr.sb("t2s", [128, 512], F32)
                    wct = [pr.sb("wct", [128, 512], BF16) for _ in range(NCH)]
                    KKV = int(os.environ.get("KKV", "99"))
                    if is_l0:
                        for c in range(NCH):
                            pr.dma("scalar", lambda eng, c=c: eng.dma_start(
                                out=wct[c][0][:, :],
                                in_=wk_bf[c * 128:(c + 1) * 128, :]),
                                reads=[BC["wk_bf"]], writes=[wct[c][1]])
                        KIT = int(os.environ.get("KIT", "99"))
                        KROPE = int(os.environ.get("KROPE", "1"))
                        it_count = 0
                        for kv in range(4 if KKV >= 2 else 0):
                            for tc in range(NTC):
                                it_count += 1
                                if it_count > KIT:
                                    continue
                                for c in range(NCH):
                                    pr.op("tensor", lambda eng, kv=kv, tc=tc, c=c: eng.matmul(
                                        kps[0][:, :],
                                        wct[c][0][:, kv * 128:(kv + 1) * 128],
                                        A(c, tc * 512, 512),
                                        start=(c == 0), stop=(c == NCH - 1)),
                                        reads=[wct[c][1], arena[1]],
                                        writes=[kps[1]] + ([kps2[1]] if c == 0 else []))
                                pr.op("scalar", lambda eng: eng.copy(
                                    krawb[0][:, :], kps[0][:, :]),
                                    reads=[kps[1]], writes=[krawb[1]])
                                pr.op("tensor", lambda eng: eng.matmul(
                                    kps2[0][:, :], ppermt[0][:, :], krawb[0][:, :],
                                    start=True, stop=True),
                                    reads=[ppermt[1], krawb[1]], writes=[kps2[1]])
                                pr.op("vector", lambda eng, tc=tc: eng.tensor_tensor(
                                    t1s[0][:, :], krawb[0][:, :],
                                    cos_t[0][:, tc * 512:(tc + 1) * 512], ALU.mult),
                                    reads=[krawb[1], cos_t[1]], writes=[t1s[1]])
                                pr.op("vector", lambda eng, tc=tc: eng.tensor_tensor(
                                    t2s[0][:, :], kps2[0][:, :],
                                    sin_t[0][:, tc * 512:(tc + 1) * 512], ALU.mult),
                                    reads=[kps2[1], sin_t[1]], writes=[t2s[1]])
                                pr.op("vector", lambda eng, kv=kv, tc=tc: eng.tensor_tensor(
                                    kst[0][:, kv * S + tc * 512: kv * S + (tc + 1) * 512],
                                    t1s[0][:, :], t2s[0][:, :], ALU.add),
                                    reads=[t1s[1], t2s[1]], writes=[kst[1]])
                        for c in range(NCH):
                            pr.dma("scalar", lambda eng, c=c: eng.dma_start(
                                out=wct[c][0][:, :],
                                in_=wv_bf[c * 128:(c + 1) * 128, :]),
                                reads=[BC["wv_bf"]], writes=[wct[c][1]])
                        for ts in range(NTS if KKV >= 3 else 0):
                            for c in range(NCH):
                                pr.op("tensor", lambda eng, ts=ts, c=c: eng.matmul(
                                    kps[0][:, :], A(c, ts * 128, 128),
                                    wct[c][0][:, :],
                                    start=(c == 0), stop=(c == NCH - 1)),
                                    reads=[arena[1], wct[c][1]], writes=[kps[1]])
                            pr.op("scalar", lambda eng, ts=ts: eng.copy(
                                vst[0][:, ts * 512:(ts + 1) * 512], kps[0][:, :]),
                                reads=[kps[1]], writes=[vst[1]])
                    else:
                        biascol0 = pr.sb("biascol0", [128, 1], F32)
                        pr.dma("gpsimd", lambda eng: eng.dma_start(
                            out=biascol0[0][:, :], in_=blat), writes=[biascol0[1]])
                        for c in range(NCH):
                            pr.dma("scalar", lambda eng, c=c: eng.dma_start(
                                out=wct[c][0][:, 0:128],
                                in_=wlat_bf[c * 128:(c + 1) * 128, :]),
                                reads=[BC["wlat_bf"]], writes=[wct[c][1]])
                        for tc in range(NTC):
                            for c in range(NCH):
                                pr.op("tensor", lambda eng, tc=tc, c=c: eng.matmul(
                                    kps[0][:, :], wct[c][0][:, 0:128],
                                    A(c, tc * 512, 512),
                                    start=(c == 0), stop=(c == NCH - 1)),
                                    reads=[wct[c][1], arena[1]], writes=[kps[1]])
                            pr.op("scalar", lambda eng, tc=tc: eng.activation(
                                kst[0][:, tc * 512:(tc + 1) * 512], kps[0][:, :],
                                ACT.Identity, bias=biascol0[0][:, 0:1], scale=1.0),
                                reads=[kps[1], biascol0[1]], writes=[kst[1]])
                        for ks in range(NTS):
                            emit_tr128(ptps, vst[0][:, ks * 128:(ks + 1) * 128],
                                       vst[1], kst[0][:, ks * 128:(ks + 1) * 128],
                                       kst[1])
                        bpr = pr.sb("bpr", [1, DM], F32)
                        pr.dma("gpsimd", lambda eng: eng.dma_start(
                            out=bpr[0][:, :], in_=bproj_row), writes=[bpr[1]])
                        for dc in range(NTC):
                            pr.op("tensor", lambda eng, dc=dc: eng.matmul(
                                kps[0][:, :], onesf[0][:, :],
                                bpr[0][0:1, dc * 512:(dc + 1) * 512],
                                start=True, stop=True),
                                reads=[onesf[1], bpr[1]], writes=[kps[1]])
                            pr.op("scalar", lambda eng, dc=dc: eng.copy(
                                bpb[0][:, dc * 512:(dc + 1) * 512], kps[0][:, :]),
                                reads=[kps[1]], writes=[bpb[1]])

                if KSUB < 3:
                    return
                with pr.phase(name + "_qa"):
                    ptps = pr.ps("ptp", [128, 128], BF16)
                    scps = [pr.ps("scps", [128, 512], F32) for _ in range(2)]
                    yps = pr.ps("yps", [128, 512], F32)
                    qps = pr.ps("qps", [128, 512], F32)
                    qps2 = pr.ps("qps2", [128, 512], F32)
                    masks = pr.sb("masks", [128, 4 * 512], F32)
                    for i in range(4):
                        pr.dma("gpsimd", lambda eng, i=i: eng.dma_start(
                            out=masks[0][:, i * 512:(i + 1) * 512],
                            in_=masks_c[i * 128:(i + 1) * 128, :]),
                            writes=[masks[1]])
                    t1s = pr.sb("t1s", [128, 512], F32)
                    t2s = pr.sb("t2s", [128, 512], F32)
                    biascol = pr.sb("biascol", [128, 16], F32)
                    if not is_l0:
                        pr.dma("gpsimd", lambda eng: eng.dma_start(
                            out=biascol[0][:, :], in_=bdh), writes=[biascol[1]])
                    qbf = pr.sb("qbf", [128, 512], BF16)
                    qraw = pr.sb("qraw", [128, 512], BF16)
                    msc = pr.sb("msc", [128, 2048], F32)
                    pnb = pr.sb("pnb", [128, 2048], BF16)
                    ptst = pr.sb("ptst", [128, 16 * 512], BF16)
                    aoc = pr.sb("aoc", [128, NH * 512], BF16)
                    rowm = pr.sb("rowm", [128, 4], F32)
                    wqt = [pr.sb("wqt", [128, 128], BF16) for _ in range(2)]
                    woall = pr.sb("woall", [128, NCH * 512], BF16)
                    xres = [pr.sb("xres", [128, 512], F32) for _ in range(2)]
                    oacc = pr.sb("oacc", [128, 512], F32)
                    wsrc = wq_bf if is_l0 else wd_bf
                    wsrcb = BC["wq_bf"] if is_l0 else BC["wd_bf"]
                    wob = wo_bf if is_l0 else wproj_bf
                    wobuf = BC["wo_bf"] if is_l0 else BC["wproj_bf"]
                    for qc in range(min(NTC, KQC)):
                        nkc = qc + 1
                        for h in range(min(NH, KH)):
                            for c in range(NCH):
                                w = wqt[c % 2]
                                pr.dma("scalar", lambda eng, c=c, h=h, w=w: eng.dma_start(
                                    out=w[0][:, :],
                                    in_=wsrc[c * 128:(c + 1) * 128,
                                             h * 128:(h + 1) * 128]),
                                    reads=[wsrcb], writes=[w[1]])
                                pr.op("tensor", lambda eng, c=c, qc=qc, w=w: eng.matmul(
                                    qps[0][:, :], w[0][:, :], A(c, qc * 512, 512),
                                    start=(c == 0), stop=(c == NCH - 1)),
                                    reads=[w[1], arena[1]], writes=[qps[1]])
                            if is_l0:
                                pr.op("scalar", lambda eng: eng.copy(
                                    qraw[0][:, :], qps[0][:, :]),
                                    reads=[qps[1]], writes=[qraw[1]])
                                pr.op("tensor", lambda eng: eng.matmul(
                                    qps2[0][:, :], ppermt[0][:, :], qraw[0][:, :],
                                    start=True, stop=True),
                                    reads=[ppermt[1], qraw[1]], writes=[qps2[1]])
                                pr.op("vector", lambda eng, qc=qc: eng.tensor_tensor(
                                    t1s[0][:, :], qraw[0][:, :],
                                    cos_t[0][:, qc * 512:(qc + 1) * 512], ALU.mult),
                                    reads=[qraw[1], cos_t[1]], writes=[t1s[1]])
                                pr.op("vector", lambda eng, qc=qc: eng.tensor_tensor(
                                    t2s[0][:, :], qps2[0][:, :],
                                    sin_t[0][:, qc * 512:(qc + 1) * 512], ALU.mult),
                                    reads=[qps2[1], sin_t[1]], writes=[t2s[1]])
                                pr.op("vector", lambda eng: eng.tensor_tensor(
                                    qbf[0][:, :], t1s[0][:, :], t2s[0][:, :], ALU.add),
                                    reads=[t1s[1], t2s[1]], writes=[qbf[1]])
                            else:
                                pr.op("scalar", lambda eng, h=h: eng.activation(
                                    qbf[0][:, :], qps[0][:, :], ACT.Identity,
                                    bias=biascol[0][:, h:h + 1], scale=1.0),
                                    reads=[qps[1], biascol[1]], writes=[qbf[1]])
                            kv = h // 4 if is_l0 else 0
                            koff = kv * S if is_l0 else 0
                            for qs in range(4):
                                for kc in range(nkc):
                                    sp = scps[kc % 2]
                                    pr.op("tensor", lambda eng, qs=qs, kc=kc, sp=sp, koff=koff: eng.matmul(
                                        sp[0][:, :],
                                        qbf[0][:, qs * 128:(qs + 1) * 128],
                                        kst[0][:, koff + kc * 512: koff + (kc + 1) * 512],
                                        start=True, stop=True),
                                        reads=[qbf[1], kst[1]], writes=[sp[1]])
                                    if kc == nkc - 1:
                                        pr.op("vector", lambda eng, qs=qs, kc=kc, sp=sp: eng.scalar_tensor_tensor(
                                            msc[0][:, kc * 512:(kc + 1) * 512],
                                            sp[0][:, :], ISQ,
                                            masks[0][:, qs * 512:(qs + 1) * 512],
                                            ALU.mult, ALU.add),
                                            reads=[sp[1], masks[1]], writes=[msc[1]])
                                    else:
                                        pr.op("vector", lambda eng, kc=kc, sp=sp: eng.tensor_single_scalar(
                                            msc[0][:, kc * 512:(kc + 1) * 512],
                                            sp[0][:, :], ISQ, ALU.mult),
                                            reads=[sp[1]], writes=[msc[1]])
                                W = nkc * 512
                                pr.op("vector", lambda eng, W=W: eng.reduce_max(
                                    rowm[0][:, 0:1], msc[0][:, 0:W], axis=AX.X),
                                    reads=[msc[1]], writes=[rowm[1]])
                                pr.op("vector", lambda eng: eng.tensor_single_scalar(
                                    rowm[0][:, 1:2], rowm[0][:, 0:1], -1.0, ALU.mult),
                                    reads=[rowm[1]], writes=[rowm[1]])
                                pr.op("scalar", lambda eng, W=W: eng.activation(
                                    pnb[0][:, 0:W], msc[0][:, 0:W], ACT.Exp,
                                    bias=rowm[0][:, 1:2], scale=1.0,
                                    accum_out=rowm[0][:, 2:3]),
                                    reads=[msc[1], rowm[1]], writes=[pnb[1], rowm[1]])
                                pr.op("scalar", lambda eng: eng.activation(
                                    rowm[0][:, 3:4], rowm[0][:, 2:3], ACT.Ln,
                                    bias=0.0, scale=1.0),
                                    reads=[rowm[1]], writes=[rowm[1]])
                                pr.op("vector", lambda eng: eng.tensor_tensor(
                                    rowm[0][:, 3:4], rowm[0][:, 1:2],
                                    rowm[0][:, 3:4], ALU.subtract),
                                    reads=[rowm[1]], writes=[rowm[1]])
                                pr.op("scalar", lambda eng, W=W: eng.activation(
                                    pnb[0][:, 0:W], msc[0][:, 0:W], ACT.Exp,
                                    bias=rowm[0][:, 3:4], scale=1.0),
                                    reads=[msc[1], rowm[1]], writes=[pnb[1]])
                                for ks in range(nkc * 4):
                                    emit_tr128(
                                        ptps,
                                        ptst[0][:, ks * 512 + qs * 128:
                                                ks * 512 + (qs + 1) * 128],
                                        ptst[1],
                                        pnb[0][:, ks * 128:(ks + 1) * 128], pnb[1])
                            nks = nkc * 4
                            for ks in range(nks):
                                voff = (ks * 512 + kv * 128) if is_l0 else (ks * 128)
                                pr.op("tensor", lambda eng, ks=ks, voff=voff, nks=nks: eng.matmul(
                                    yps[0][:, :], vst[0][:, voff: voff + 128],
                                    ptst[0][:, ks * 512:(ks + 1) * 512],
                                    start=(ks == 0), stop=(ks == nks - 1)),
                                    reads=[vst[1], ptst[1]], writes=[yps[1]])
                            pr.op("scalar", lambda eng, h=h: eng.copy(
                                aoc[0][:, h * 512:(h + 1) * 512], yps[0][:, :]),
                                reads=[yps[1]], writes=[aoc[1]])
                        for dc in range(NTC):
                            for c in range(NCH):
                                pr.dma("scalar", lambda eng, c=c, dc=dc: eng.dma_start(
                                    out=woall[0][:, c * 512:(c + 1) * 512],
                                    in_=wob[c * 128:(c + 1) * 128,
                                            dc * 512:(dc + 1) * 512]),
                                    reads=[wobuf], writes=[woall[1]])
                            for qs in range(4):
                                sp = scps[qs % 2]
                                for c in range(NCH):
                                    pr.op("tensor", lambda eng, c=c, qs=qs, sp=sp: eng.matmul(
                                        sp[0][:, :],
                                        aoc[0][:, c * 512 + qs * 128:
                                                c * 512 + (qs + 1) * 128],
                                        woall[0][:, c * 512:(c + 1) * 512],
                                        start=(c == 0), stop=(c == NCH - 1)),
                                        reads=[aoc[1], woall[1]], writes=[sp[1]])
                                ts = qc * 4 + qs
                                xr = xres[qs % 2]
                                pr.dma("sync", lambda eng, ts=ts, dc=dc, xr=xr: eng.dma_start(
                                    out=xr[0][:, :],
                                    in_=xsrc[ts * 128:(ts + 1) * 128,
                                             dc * 512:(dc + 1) * 512]),
                                    reads=[xsrc_buf], writes=[xr[1]])
                                if is_l0:
                                    pr.op("vector", lambda eng, sp=sp, xr=xr: eng.tensor_tensor(
                                        oacc[0][:, :], sp[0][:, :], xr[0][:, :],
                                        ALU.add), reads=[sp[1], xr[1]], writes=[oacc[1]])
                                else:
                                    pr.op("vector", lambda eng, sp=sp, dc=dc: eng.tensor_tensor(
                                        oacc[0][:, :], sp[0][:, :],
                                        bpb[0][:, dc * 512:(dc + 1) * 512], ALU.add),
                                        reads=[sp[1], bpb[1]], writes=[oacc[1]])
                                    pr.op("vector", lambda eng, xr=xr: eng.tensor_tensor(
                                        oacc[0][:, :], oacc[0][:, :], xr[0][:, :],
                                        ALU.add), reads=[oacc[1], xr[1]], writes=[oacc[1]])
                                pr.dma("sync", lambda eng, ts=ts, dc=dc: eng.dma_start(
                                    out=xdst[ts * 128:(ts + 1) * 128,
                                             dc * 512:(dc + 1) * 512],
                                    in_=oacc[0][:, :]),
                                    reads=[oacc[1]], writes=[xdst_buf])

        STAGES = int(os.environ.get("KSTAGES", "99"))
        if STAGES >= 2:
            attention_block("l0a", x_in, BC["x_in"], xA, BC["xA"], True)

        def emit_l0f():
            with pr.phase("l0f_qx"):
                cols0 = make_cols(True)
                ptps = pr.ps("ptp", [128, 128], BF16)
                emit_ln_to_fm(xA, BC["xA"], True, cols0, ptps)
                pr.op("vector", lambda eng: eng.tensor_copy(
                    isx_keep[0][:, :], cols0["isx"][0][:, :]),
                    reads=[cols0["isx"][1]], writes=[isx_keep[1]])

            with pr.phase("l0f_gateup"):
                gt = [pr.sb("gt", [128, 512], BF16) for _ in range(NCH)]
                ut = [pr.sb("ut", [128, 512], BF16) for _ in range(NCH)]
                gps = pr.ps("gps", [128, 512], F32)
                ups = pr.ps("ups", [128, 512], F32)
                lnps = pr.ps("lnps", [128, 512], F32)
                sgs = pr.sb("sgs", [128, 512], F32)
                hb = [pr.sb("hb", [128, 512], BF16) for _ in range(2)]
                hl = pr.sb("hl", [128, 512], F32)
                lnb = pr.sb("lnb", [128, 512], F32)
                tred = pr.sb("tred", [128, 2], F32)
                scg = pr.sb("scg", [128, NTS], F32)
                scu = pr.sb("scu", [128, NTS], F32)
                lnr = pr.sb("lnr", [1, INTER0], F32)
                pr.dma("gpsimd", lambda eng: eng.dma_start(out=lnr[0][:, :],
                                                           in_=lnw_row),
                       writes=[lnr[1]])
                pr.op("vector", lambda eng: eng.memset(hstat[0][:, 0:2 * NTS], 0.0),
                      writes=[hstat[1]])
                pr.op("vector", lambda eng: eng.tensor_scalar(
                    scg[0][:, :], isx_keep[0][:, :], wm_keep[0][:, 0:1], 0.0,
                    ALU.mult, ALU.add), reads=[isx_keep[1], wm_keep[1]],
                    writes=[scg[1]])
                pr.op("vector", lambda eng: eng.tensor_scalar(
                    scu[0][:, :], isx_keep[0][:, :], wm_keep[0][:, 1:2], 0.0,
                    ALU.mult, ALU.add), reads=[isx_keep[1], wm_keep[1]],
                    writes=[scu[1]])
                for jc in range(NJC0):
                    pr.op("tensor", lambda eng, jc=jc: eng.matmul(
                        lnps[0][:, :], onesf[0][:, :],
                        lnr[0][0:1, jc * 512:(jc + 1) * 512], start=True, stop=True),
                        reads=[onesf[1], lnr[1]], writes=[lnps[1]])
                    pr.op("scalar", lambda eng: eng.copy(
                        lnb[0][:, :], lnps[0][:, :]), reads=[lnps[1]], writes=[lnb[1]])
                    for c in range(NCH):
                        pr.dma("sync", lambda eng, jc=jc, c=c: eng.dma_start(
                            out=gt[c][0][:, :],
                            in_=tern_g[jc * 512:(jc + 1) * 512, c * 128:(c + 1) * 128],
                            transpose=True), reads=[BC["tern_g"]], writes=[gt[c][1]])
                        pr.dma("sync", lambda eng, jc=jc, c=c: eng.dma_start(
                            out=ut[c][0][:, :],
                            in_=tern_u[jc * 512:(jc + 1) * 512, c * 128:(c + 1) * 128],
                            transpose=True), reads=[BC["tern_u"]], writes=[ut[c][1]])
                    for ts in range(NTS):
                        for c in range(NCH):
                            pr.op("tensor", lambda eng, c=c, ts=ts: eng.matmul(
                                gps[0][:, :], A(c, ts * 128, 128), gt[c][0][:, :],
                                start=(c == 0), stop=(c == NCH - 1)),
                                reads=[arena[1], gt[c][1]], writes=[gps[1]])
                        for c in range(NCH):
                            pr.op("tensor", lambda eng, c=c, ts=ts: eng.matmul(
                                ups[0][:, :], A(c, ts * 128, 128), ut[c][0][:, :],
                                start=(c == 0), stop=(c == NCH - 1)),
                                reads=[arena[1], ut[c][1]], writes=[ups[1]])
                        h = hb[ts % 2]
                        pr.op("scalar", lambda eng, ts=ts: eng.activation(
                            sgs[0][:, :], gps[0][:, :], ACT.Silu, bias=0.0,
                            scale=scg[0][:, ts:ts + 1]),
                            reads=[gps[1], scg[1]], writes=[sgs[1]])
                        pr.op("vector", lambda eng, ts=ts, h=h: eng.scalar_tensor_tensor(
                            h[0][:, :], sgs[0][:, :], scu[0][:, ts:ts + 1],
                            ups[0][:, :], ALU.mult, ALU.mult),
                            reads=[sgs[1], scu[1], ups[1]], writes=[h[1]])
                        pr.op("vector", lambda eng, h=h: eng.tensor_tensor(
                            hl[0][:, :], h[0][:, :], lnb[0][:, :], ALU.mult),
                            reads=[h[1], lnb[1]], writes=[hl[1]])
                        pr.op("vector", lambda eng: eng.reduce_max(
                            tred[0][:, 0:1], hl[0][:, :], axis=AX.X,
                            apply_absolute_value=True), reads=[hl[1]], writes=[tred[1]])
                        pr.op("vector", lambda eng, ts=ts: eng.tensor_tensor(
                            hstat[0][:, ts:ts + 1], hstat[0][:, ts:ts + 1],
                            tred[0][:, 0:1], ALU.max),
                            reads=[hstat[1], tred[1]], writes=[hstat[1]])
                        pr.op("scalar", lambda eng, h=h: eng.activation(
                            hl[0][:, :], h[0][:, :], ACT.Square, bias=0.0, scale=1.0,
                            accum_out=tred[0][:, 1:2]),
                            reads=[h[1]], writes=[hl[1], tred[1]])
                        pr.op("vector", lambda eng, ts=ts: eng.tensor_tensor(
                            hstat[0][:, NTS + ts:NTS + ts + 1],
                            hstat[0][:, NTS + ts:NTS + ts + 1],
                            tred[0][:, 1:2], ALU.add),
                            reads=[hstat[1], tred[1]], writes=[hstat[1]])
                        pr.dma("gpsimd", lambda eng, ts=ts, jc=jc, h=h: eng.dma_start(
                            out=h_spill[ts * 128:(ts + 1) * 128,
                                        jc * 512:(jc + 1) * 512], in_=h[0][:, :]),
                            reads=[h[1]], writes=[BC["h_spill"]])
                pr.op("scalar", lambda eng: eng.activation(
                    hstat[0][:, 5 * NTS:6 * NTS], hstat[0][:, NTS:2 * NTS],
                    ACT.Sqrt, bias=EPS_RMS, scale=1.0 / INTER0),
                    reads=[hstat[1]], writes=[hstat[1]])
                pr.op("vector", lambda eng: eng.reciprocal(
                    hstat[0][:, 2 * NTS:3 * NTS], hstat[0][:, 5 * NTS:6 * NTS]),
                    reads=[hstat[1]], writes=[hstat[1]])
                pr.op("vector", lambda eng: eng.tensor_tensor(
                    hstat[0][:, 0:NTS], hstat[0][:, 0:NTS],
                    hstat[0][:, 2 * NTS:3 * NTS], ALU.mult),
                    reads=[hstat[1]], writes=[hstat[1]])
                pr.op("vector", lambda eng: eng.tensor_single_scalar(
                    hstat[0][:, 0:NTS], hstat[0][:, 0:NTS], EPS_Q, ALU.max),
                    reads=[hstat[1]], writes=[hstat[1]])
                pr.op("vector", lambda eng: eng.tensor_scalar(
                    hstat[0][:, 3 * NTS:4 * NTS], hstat[0][:, 0:NTS],
                    wm_keep[0][:, 2:3], 1.0 / 127.0, ALU.mult, ALU.mult),
                    reads=[hstat[1], wm_keep[1]], writes=[hstat[1]])
                pr.op("vector", lambda eng: eng.reciprocal(
                    hstat[0][:, 4 * NTS:5 * NTS], hstat[0][:, 0:NTS]),
                    reads=[hstat[1]], writes=[hstat[1]])
                pr.op("vector", lambda eng: eng.tensor_single_scalar(
                    hstat[0][:, 4 * NTS:5 * NTS], hstat[0][:, 4 * NTS:5 * NTS],
                    127.0, ALU.mult), reads=[hstat[1]], writes=[hstat[1]])
                pr.op("vector", lambda eng: eng.tensor_tensor(
                    hstat[0][:, 2 * NTS:3 * NTS], hstat[0][:, 2 * NTS:3 * NTS],
                    hstat[0][:, 4 * NTS:5 * NTS], ALU.mult),
                    reads=[hstat[1]], writes=[hstat[1]])
                ctps = pr.ps("ctps", [1, 128], F32)
                for ts in range(NTS):
                    pr.op("tensor", lambda eng, ts=ts: eng.transpose(
                        ctps[0][:, :], hstat[0][:, 2 * NTS + ts:2 * NTS + ts + 1],
                        identf[0][:, :]),
                        reads=[hstat[1], identf[1]], writes=[ctps[1]])
                    pr.op("scalar", lambda eng, ts=ts: eng.copy(
                        crows[0][0:1, ts * 128:(ts + 1) * 128], ctps[0][:, :]),
                        reads=[ctps[1]], writes=[crows[1]])

            with pr.phase("l0f_down"):
                wdt = [pr.sb("wdt", [128, 512], BF16) for _ in range(2)]
                hts = [pr.sb("hts", [128, 512], BF16) for _ in range(2)]
                v1 = pr.sb("v1", [128, 512], F32)
                r1 = pr.sb("r1", [128, 512], F32)
                r2 = pr.sb("r2", [128, 512], F32)
                cbc = pr.sb("cbc", [128, 512], F32)
                lncolt = pr.sb("lncolt", [128, NJ0], F32)
                oacc = pr.sb("oacc", [128, 512], F32)
                xres = [pr.sb("xres", [128, 512], F32) for _ in range(2)]
                dps = [pr.ps("dps", [128, 512], F32) for _ in range(4)]
                cps = pr.ps("cps", [128, 512], F32)
                pr.dma("gpsimd", lambda eng: eng.dma_start(
                    out=lncolt[0][:, :], in_=lnw_colt), writes=[lncolt[1]])
                for tc in range(NTC):
                    for i in range(4):
                        ts = tc * 4 + i
                        pr.op("tensor", lambda eng, ts=ts, i=i: eng.matmul(
                            cps[0][:, i * 128:(i + 1) * 128], onesf[0][:, :],
                            crows[0][0:1, ts * 128:(ts + 1) * 128],
                            start=True, stop=True),
                            reads=[onesf[1], crows[1]], writes=[cps[1]])
                    pr.op("scalar", lambda eng: eng.copy(
                        cbc[0][:, :], cps[0][:, :]), reads=[cps[1]], writes=[cbc[1]])
                    for j in range(NJ0):
                        ht = hts[j % 2]
                        pr.dma("sync", lambda eng, tc=tc, j=j, ht=ht: eng.dma_start(
                            out=ht[0][:, :],
                            in_=h_spill[tc * 512:(tc + 1) * 512,
                                        j * 128:(j + 1) * 128], transpose=True),
                            reads=[BC["h_spill"]], writes=[ht[1]])
                        pr.op("scalar", lambda eng, j=j, ht=ht: eng.activation(
                            v1[0][:, :], ht[0][:, :], ACT.Identity, bias=0.0,
                            scale=lncolt[0][:, j:j + 1]),
                            reads=[ht[1], lncolt[1]], writes=[v1[1]])
                        pr.op("vector", lambda eng: eng.tensor_tensor(
                            v1[0][:, :], v1[0][:, :], cbc[0][:, :], ALU.mult),
                            reads=[v1[1], cbc[1]], writes=[v1[1]])
                        emit_round(arena[0][:, j * 512:(j + 1) * 512], arena[1],
                                   v1[0][:, :], v1[1], r1, r2, 512)
                    for dc in range(NTC):
                        for j in range(NJ0):
                            w = wdt[j % 2]
                            pr.dma("sync", lambda eng, dc=dc, j=j, w=w: eng.dma_start(
                                out=w[0][:, :],
                                in_=tern_d[dc * 512:(dc + 1) * 512,
                                           j * 128:(j + 1) * 128], transpose=True),
                                reads=[BC["tern_d"]], writes=[w[1]])
                            for i in range(4):
                                pr.op("tensor", lambda eng, i=i, j=j, w=w: eng.matmul(
                                    dps[i][0][:, :],
                                    arena[0][:, j * 512 + i * 128: j * 512 + (i + 1) * 128],
                                    w[0][:, :],
                                    start=(j == 0), stop=(j == NJ0 - 1)),
                                    reads=[arena[1], w[1]], writes=[dps[i][1]])
                        for i in range(4):
                            ts = tc * 4 + i
                            xr = xres[i % 2]
                            pr.dma("sync", lambda eng, ts=ts, dc=dc, xr=xr: eng.dma_start(
                                out=xr[0][:, :],
                                in_=xA[ts * 128:(ts + 1) * 128,
                                       dc * 512:(dc + 1) * 512]),
                                reads=[BC["xA"]], writes=[xr[1]])
                            pr.op("scalar", lambda eng, i=i, ts=ts: eng.activation(
                                oacc[0][:, :], dps[i][0][:, :], ACT.Identity,
                                bias=0.0, scale=hstat[0][:, 3 * NTS + ts:3 * NTS + ts + 1]),
                                reads=[dps[i][1], hstat[1]], writes=[oacc[1]])
                            pr.op("vector", lambda eng, xr=xr: eng.tensor_tensor(
                                oacc[0][:, :], oacc[0][:, :], xr[0][:, :], ALU.add),
                                reads=[oacc[1], xr[1]], writes=[oacc[1]])
                            pr.dma("sync", lambda eng, ts=ts, dc=dc: eng.dma_start(
                                out=xB[ts * 128:(ts + 1) * 128,
                                       dc * 512:(dc + 1) * 512], in_=oacc[0][:, :]),
                                reads=[oacc[1]], writes=[BC["xB"]])


        def emit_l1f():
            with pr.phase("l1f_combine"):
                cols1 = make_cols(False)
                ptps = pr.ps("ptp", [128, 128], BF16)
                emit_ln_to_fm(xC, BC["xC"], False, cols1, ptps)
                gwb = pr.sb("gwb2", [15, 2048], BF16)
                gwt = pr.sb("gwt2", [128, NCH * 15], BF16)
                gtp = pr.ps("gtp2", [128, 15], BF16)
                lps = pr.ps("lps2", [128, 15], F32)
                lg = pr.sb("lg2", [128, 15], F32)
                c1 = pr.sb("c1_2", [128, 15], F32)
                mskd = pr.sb("mskd2", [128, 15], F32)
                pcol = pr.sb("pcol2", [128, 4], F32)
                pr.dma("gpsimd", lambda eng: eng.dma_start(out=gwb[0][:, :],
                                                           in_=gatew),
                       writes=[gwb[1]])
                for c in range(NCH):
                    pr.op("tensor", lambda eng, c=c: eng.transpose(
                        gtp[0][:, :], gwb[0][:, c * 128:(c + 1) * 128],
                        ident[0][0:15, 0:15]),
                        reads=[gwb[1], ident[1]], writes=[gtp[1]])
                    pr.op("scalar", lambda eng, c=c: eng.copy(
                        gwt[0][:, c * 15:(c + 1) * 15], gtp[0][:, :]),
                        reads=[gtp[1]], writes=[gwt[1]])
                pr.op("vector", lambda eng: eng.memset(
                    cmb[0][:, :], 1.0), writes=[cmb[1]])
                for ts in range(NTS):
                    for c in range(NCH):
                        pr.op("tensor", lambda eng, ts=ts, c=c: eng.matmul(
                            lps[0][:, :], A(c, ts * 128, 128),
                            gwt[0][:, c * 15:(c + 1) * 15],
                            start=(c == 0), stop=(c == NCH - 1)),
                            reads=[arena[1], gwt[1]], writes=[lps[1]])
                    pr.op("vector", lambda eng: eng.reduce_max(
                        pcol[0][:, 0:1], lps[0][:, :], axis=AX.X),
                        reads=[lps[1]], writes=[pcol[1]])
                    pr.op("vector", lambda eng: eng.tensor_single_scalar(
                        pcol[0][:, 1:2], pcol[0][:, 0:1], -1.0, ALU.mult),
                        reads=[pcol[1]], writes=[pcol[1]])
                    pr.op("scalar", lambda eng: eng.activation(
                        lg[0][:, :], lps[0][:, :], ACT.Exp,
                        bias=pcol[0][:, 1:2], scale=1.0, accum_out=pcol[0][:, 2:3]),
                        reads=[lps[1], pcol[1]], writes=[lg[1], pcol[1]])
                    pr.op("vector", lambda eng: eng.reciprocal(
                        pcol[0][:, 3:4], pcol[0][:, 2:3]),
                        reads=[pcol[1]], writes=[pcol[1]])
                    pr.op("scalar", lambda eng: eng.activation(
                        lg[0][:, :], lg[0][:, :], ACT.Identity, bias=0.0,
                        scale=pcol[0][:, 3:4]), reads=[lg[1], pcol[1]], writes=[lg[1]])
                    pr.op("vector", lambda eng: eng.reduce_max(
                        pcol[0][:, 0:1], lg[0][:, :], axis=AX.X),
                        reads=[lg[1]], writes=[pcol[1]])
                    pr.op("vector", lambda eng: eng.tensor_scalar(
                        c1[0][:, :], lg[0][:, :], pcol[0][:, 0:1], NEG,
                        ALU.is_ge, ALU.mult), reads=[lg[1], pcol[1]], writes=[c1[1]])
                    pr.op("vector", lambda eng: eng.tensor_tensor(
                        mskd[0][:, :], lg[0][:, :], c1[0][:, :], ALU.add),
                        reads=[lg[1], c1[1]], writes=[mskd[1]])
                    pr.op("vector", lambda eng: eng.reduce_max(
                        pcol[0][:, 1:2], mskd[0][:, :], axis=AX.X),
                        reads=[mskd[1]], writes=[pcol[1]])
                    pr.op("vector", lambda eng: eng.tensor_scalar(
                        c1[0][:, :], lg[0][:, :], pcol[0][:, 1:2], 0.0,
                        ALU.is_ge, ALU.add), reads=[lg[1], pcol[1]], writes=[c1[1]])
                    pr.op("vector", lambda eng, ts=ts: eng.tensor_tensor(
                        cmb[0][:, ts * 16: ts * 16 + 15], c1[0][:, :],
                        lg[0][:, :], ALU.mult), reads=[c1[1], lg[1]], writes=[cmb[1]])

            with pr.phase("l1f_moe"):
                NQ = 4
                TQ = S // NQ
                NTSQ = TQ // 128
                yacc = pr.sb("yacc", [128, NTSQ * DM], F32)
                w13t = [pr.sb("w13t", [128, INTER1], BF16) for _ in range(NCH)]
                sgst = pr.sb("sgst", [128, NTSQ * INTER1], BF16)
                hhst = pr.sb("hhst", [128, NTSQ * INTER1], BF16)
                hfm = pr.sb("hfm", [128, NM1 * TQ], BF16)
                w2t = [pr.sb("w2t", [128, 512], BF16) for _ in range(2)]
                hps = pr.ps("hps", [128, 512], F32)
                ptps = pr.ps("ptp", [128, 128], BF16)
                dps = [pr.ps("dps", [128, 512], F32) for _ in range(4)]
                xres = [pr.sb("xres", [128, 512], F32)]
                oacc = pr.sb("oacc", [128, 512], F32)
                exps = [(w1_bf[i], w3_bf[i], w2_bf[i], BC["w1_bf"], BC["w3_bf"],
                         BC["w2_bf"], i) for i in range(NEXP)]
                exps.append((sw1_bf, sw3_bf, sw2_bf, BC["sw1_bf"], BC["sw3_bf"],
                             BC["sw2_bf"], 15))
                for q in range(NQ):
                    pr.op("vector", lambda eng: eng.memset(yacc[0][:, :], 0.0),
                          writes=[yacc[1]])
                    for (e1, e3, e2, b1, b3, b2, ei) in exps:
                        for c in range(NCH):
                            pr.dma("sync", lambda eng, c=c, e1=e1: eng.dma_start(
                                out=w13t[c][0][:, :],
                                in_=e1[:, c * 128:(c + 1) * 128], transpose=True),
                                reads=[b1], writes=[w13t[c][1]])
                        for i in range(NTSQ):
                            ts = q * NTSQ + i
                            off = 0
                            for jw in L1JC:
                                for c in range(NCH):
                                    pr.op("tensor", lambda eng, c=c, ts=ts, off=off, jw=jw: eng.matmul(
                                        hps[0][:, 0:jw], A(c, ts * 128, 128),
                                        w13t[c][0][:, off:off + jw],
                                        start=(c == 0), stop=(c == NCH - 1)),
                                        reads=[arena[1], w13t[c][1]], writes=[hps[1]])
                                pr.op("scalar", lambda eng, i=i, off=off, jw=jw: eng.activation(
                                    sgst[0][:, i * INTER1 + off: i * INTER1 + off + jw],
                                    hps[0][:, 0:jw], ACT.Silu, bias=0.0, scale=1.0),
                                    reads=[hps[1]], writes=[sgst[1]])
                                off += jw
                        for c in range(NCH):
                            pr.dma("sync", lambda eng, c=c, e3=e3: eng.dma_start(
                                out=w13t[c][0][:, :],
                                in_=e3[:, c * 128:(c + 1) * 128], transpose=True),
                                reads=[b3], writes=[w13t[c][1]])
                        for i in range(NTSQ):
                            ts = q * NTSQ + i
                            off = 0
                            for jw in L1JC:
                                for c in range(NCH):
                                    pr.op("tensor", lambda eng, c=c, ts=ts, off=off, jw=jw: eng.matmul(
                                        hps[0][:, 0:jw], A(c, ts * 128, 128),
                                        w13t[c][0][:, off:off + jw],
                                        start=(c == 0), stop=(c == NCH - 1)),
                                        reads=[arena[1], w13t[c][1]], writes=[hps[1]])
                                pr.op("vector", lambda eng, i=i, ts=ts, off=off, jw=jw, ei=ei: eng.scalar_tensor_tensor(
                                    hhst[0][:, i * INTER1 + off: i * INTER1 + off + jw],
                                    sgst[0][:, i * INTER1 + off: i * INTER1 + off + jw],
                                    cmb[0][:, ts * 16 + ei: ts * 16 + ei + 1],
                                    hps[0][:, 0:jw], ALU.mult, ALU.mult),
                                    reads=[sgst[1], cmb[1], hps[1]], writes=[hhst[1]])
                                off += jw
                        for i in range(NTSQ):
                            for m in range(NM1):
                                emit_tr128(
                                    ptps,
                                    hfm[0][:, m * TQ + i * 128: m * TQ + (i + 1) * 128],
                                    hfm[1],
                                    hhst[0][:, i * INTER1 + m * 128:
                                            i * INTER1 + (m + 1) * 128], hhst[1])
                        for dc in range(NTC):
                            for m in range(NM1):
                                w = w2t[m % 2]
                                pr.dma("sync", lambda eng, dc=dc, m=m, w=w, e2=e2: eng.dma_start(
                                    out=w[0][:, :],
                                    in_=e2[dc * 512:(dc + 1) * 512,
                                           m * 128:(m + 1) * 128], transpose=True),
                                    reads=[b2], writes=[w[1]])
                                for i in range(NTSQ):
                                    pr.op("tensor", lambda eng, i=i, m=m, w=w: eng.matmul(
                                        dps[i][0][:, :],
                                        hfm[0][:, m * TQ + i * 128: m * TQ + (i + 1) * 128],
                                        w[0][:, :],
                                        start=(m == 0), stop=(m == NM1 - 1)),
                                        reads=[hfm[1], w[1]], writes=[dps[i][1]])
                            for i in range(NTSQ):
                                pr.op("vector", lambda eng, i=i, dc=dc: eng.tensor_tensor(
                                    yacc[0][:, i * DM + dc * 512: i * DM + (dc + 1) * 512],
                                    yacc[0][:, i * DM + dc * 512: i * DM + (dc + 1) * 512],
                                    dps[i][0][:, :], ALU.add),
                                    reads=[yacc[1], dps[i][1]], writes=[yacc[1]])
                    for i in range(NTSQ):
                        ts = q * NTSQ + i
                        for dc in range(NTC):
                            xr = xres[0]
                            pr.dma("sync", lambda eng, ts=ts, dc=dc, xr=xr: eng.dma_start(
                                out=xr[0][:, :],
                                in_=xC[ts * 128:(ts + 1) * 128,
                                       dc * 512:(dc + 1) * 512]),
                                reads=[BC["xC"]], writes=[xr[1]])
                            pr.op("vector", lambda eng, i=i, dc=dc, xr=xr: eng.tensor_tensor(
                                oacc[0][:, :],
                                yacc[0][:, i * DM + dc * 512: i * DM + (dc + 1) * 512],
                                xr[0][:, :], ALU.add),
                                reads=[yacc[1], xr[1]], writes=[oacc[1]])
                            pr.dma("sync", lambda eng, ts=ts, dc=dc: eng.dma_start(
                                out=out[ts * 128:(ts + 1) * 128,
                                        dc * 512:(dc + 1) * 512], in_=oacc[0][:, :]),
                                reads=[oacc[1]], writes=[BC["out"]])


        if STAGES >= 3:
            emit_l0f()
        if STAGES >= 4:
            attention_block("l1a", xB, BC["xB"], xC, BC["xC"], False)
        if STAGES >= 5:
            emit_l1f()

        with pr.phase("finish"):
            pr.barrier_wait_all("sync")

    build_program.trace = pr.trace
    return nc, pr.n_instr


def _host_consts():
    ident = np.eye(128, dtype=BF)
    identf = np.eye(128, dtype=np.float32)
    P = np.zeros((128, 128), dtype=np.float32)
    for d in range(64):
        P[d, d + 64] = -1.0
        P[64 + d, d] = 1.0
    ppermt = np.ascontiguousarray(P.T).astype(BF)
    masks = np.zeros((4 * 128, 512), dtype=np.float32)
    for m in range(4):
        p = np.arange(128)[:, None]
        f = np.arange(512)[None, :]
        masks[m * 128:(m + 1) * 128] = np.where(f > p + 128 * m, NEG, 0.0)
    ones = np.ones((1, 128), dtype=np.float32)
    return ident, identf, ppermt, masks, ones


_CACHE = {}


def kernel(L0_attn_x, L0_attn_Wq, L0_attn_Wk, L0_attn_Wv, L0_attn_Wo,
           L0_ffn_w_gate, L0_ffn_w_up, L0_ffn_w_down, L0_ffn_ln_weight,
           L1_attn_W_lat, L1_attn_b_lat, L1_attn_W_d, L1_attn_b_d,
           L1_attn_W_proj, L1_attn_b_proj, L1_ffn_gate_w, L1_ffn_w1,
           L1_ffn_w2, L1_ffn_w3, L1_ffn_sw1, L1_ffn_sw2, L1_ffn_sw3):
    if "nc" not in _CACHE:
        _CACHE["nc"], n = build_program()
        _CACHE["n_instr"] = n
    nc = _CACHE["nc"]

    ident, identf, ppermt, masks, ones = _host_consts()
    f32 = lambda a: np.ascontiguousarray(a, dtype=np.float32)
    shared = dict(
        wq=f32(L0_attn_Wq), wk=f32(L0_attn_Wk), wv=f32(L0_attn_Wv),
        wo=f32(L0_attn_Wo), wgate=f32(L0_ffn_w_gate), wup=f32(L0_ffn_w_up),
        wdown=f32(L0_ffn_w_down),
        lnw_row=f32(L0_ffn_ln_weight).reshape(1, INTER0),
        lnw_colt=np.ascontiguousarray(
            f32(L0_ffn_ln_weight).reshape(NJ0, 128).T),
        wlat=f32(L1_attn_W_lat), blat=f32(L1_attn_b_lat).reshape(128, 1),
        wd=f32(L1_attn_W_d),
        bdh=np.ascontiguousarray(f32(L1_attn_b_d).reshape(16, 128).T),
        wproj=f32(L1_attn_W_proj),
        bproj_row=f32(L1_attn_b_proj).reshape(1, DM),
        gatew=f32(L1_ffn_gate_w), w1=f32(L1_ffn_w1), w2=f32(L1_ffn_w2),
        w3=f32(L1_ffn_w3), sw1=f32(L1_ffn_sw1), sw2=f32(L1_ffn_sw2),
        sw3=f32(L1_ffn_sw3),
        ident_c=ident, identf_c=identf, ppermt_c=ppermt, masks_c=masks,
        ones_c=ones,
        csts_c=np.tile(np.array([0.25, math.pi / 2.0, 128.0, EPS_LN, EPS_RMS,
                                 1.0, 0.0, 0.0], np.float32), (128, 1)),
    )
    x = f32(L0_attn_x)
    in_maps = [dict(shared, x_in=np.ascontiguousarray(x[b]))
               for b in range(x.shape[0])]
    res = run_bass_kernel_spmd(nc, in_maps, core_ids=list(range(8)))
    outs = [np.asarray(r["out"], dtype=np.float32) for r in res.results]
    return np.stack(outs, axis=0)


# version 07
# speedup vs baseline: 1.0179x; 1.0179x over previous
import contextlib
import math
import os

os.environ.setdefault("GAUGE_NTFF_IGNORE_DMA", "1")

import numpy as np
import ml_dtypes

import concourse.bass as bass
import concourse.mybir as mybir
from concourse.bass_utils import run_bass_kernel_spmd

F32 = mybir.dt.float32
BF16 = mybir.dt.bfloat16
ALU = mybir.AluOpType
ACT = mybir.ActivationFunctionType
AX = mybir.AxisListType
BF = ml_dtypes.bfloat16

S = 2048
DM = 2048
NCH = DM // 128
NTS = S // 128
NTC = S // 512
INTER0 = 8192
NJC0 = INTER0 // 512
NJ0 = INTER0 // 128
INTER1 = 1408
NEXP = 15
NM1 = INTER1 // 128
L1JC = (512, 512, 384)
EPS_LN = 1e-5
EPS_RMS = 1e-6
EPS_Q = 1e-5
NEG = -1e30
ISQ = 1.0 / math.sqrt(128.0)

DBG = False

ENGINES = ("tensor", "vector", "scalar", "gpsimd", "sync")
DMA_QUEUES = ("sync", "scalar", "gpsimd")
STRICT_SYNC = os.environ.get("BASS_SIM_FP64") == "1"


class Buf:
    __slots__ = ("ready", "readers", "name")

    def __init__(self, name=""):
        self.ready = {}
        self.readers = {}
        self.name = name


class Prog:
    def __init__(self, nc, gstack):
        self.nc = nc
        self.gstack = gstack
        self.csem, self.ccount = {}, {}
        self.dsem, self.dcount = {}, {}
        for e in ENGINES:
            self.csem[e] = gstack.enter_context(nc.semaphore(f"cs_{e}"))
            self.ccount[e] = 0
        self.DMA_WAYS = 4
        for q in DMA_QUEUES:
            self.dsem[q] = [gstack.enter_context(nc.semaphore(f"ds_{q}{i}"))
                            for i in range(self.DMA_WAYS)]
            self.dcount[q] = [0] * self.DMA_WAYS
            self.dissue = getattr(self, "dissue", {})
            self.dissue[q] = 0
        self.waited = {e: {} for e in ENGINES}
        self.cur = None
        self.local_stack = None
        self.n_names = 0
        self.n_instr = 0
        self.trace = []
        self._barrier = {}
        self._barrier_done = set()

    def uname(self, base):
        self.n_names += 1
        return f"{base}_{self.n_names}"

    def sb(self, base, shape, dt, scope="local"):
        stack = self.local_stack if scope == "local" else self.gstack
        h = stack.enter_context(self.nc.sbuf_tensor(self.uname(base), shape, dt))
        return (h, Buf(base))

    def sb_on(self, stack, base, shape, dt):
        h = stack.enter_context(self.nc.sbuf_tensor(self.uname(base), shape, dt))
        return (h, Buf(base))

    def ps(self, base, shape, dt):
        h = self.local_stack.enter_context(
            self.nc.psum_tensor(self.uname(base), shape, dt))
        return (h, Buf(base))

    @contextlib.contextmanager
    def phase(self, name):
        snap = {}
        for e in ENGINES:
            if self.ccount[e]:
                snap[self.csem[e]] = self.ccount[e]
        for q in DMA_QUEUES:
            for i in range(self.DMA_WAYS):
                if self.dcount[q][i]:
                    snap[self.dsem[q][i]] = 16 * self.dcount[q][i]
        self._barrier = snap
        self._barrier_done = set()
        self.cur = {e: [] for e in ENGINES}
        with contextlib.ExitStack() as stack:
            self.local_stack = stack
            yield
            qs = self.cur
            self.cur = None
            self.local_stack = None
            with self.nc.Block() as blk:
                for e in ENGINES:
                    ops = qs[e]
                    if not ops:
                        continue

                    def body(eng, ops=ops):
                        for f in ops:
                            f(eng)
                    getattr(blk, e)(body)

    def _waits(self, engine, reads, writes):
        own = self.csem[engine]
        wt = self.waited[engine]
        need = {}
        if engine not in self._barrier_done:
            self._barrier_done.add(engine)
            for s, v in self._barrier.items():
                if s is not own and wt.get(s, 0) < v:
                    need[s] = v

        def add(sem, val, skip_own=False):
            if sem is own and skip_own:
                return
            if wt.get(sem, 0) >= val:
                return
            if need.get(sem, 0) < val:
                need[sem] = val

        for b in reads:
            for s, v in b.ready.items():
                add(s, v)
        for b in writes:
            for s, v in b.ready.items():
                add(s, v, skip_own=True)
            for s, v in b.readers.items():
                add(s, v)
        q = self.cur[engine]
        for s, v in need.items():
            self.trace.append(f"{engine}: WAIT {s.name if hasattr(s,'name') else s} >= {v}")
            q.append(lambda eng, s=s, v=v: eng.wait_ge(s, v))
            wt[s] = max(wt.get(s, 0), v)

    const_dep = None

    def op(self, engine, fn, reads=(), writes=()):
        if engine == "scalar" and self.const_dep is not None:
            reads = list(reads) + [self.const_dep]
        self._waits(engine, reads, writes)
        sem = self.csem[engine]
        self.ccount[engine] += 1
        cnt = self.ccount[engine]
        self.n_instr += 1
        self.trace.append(
            f"{engine}: OP#{cnt} r=[{','.join(b.name for b in reads)}] "
            f"w=[{','.join(b.name for b in writes)}]")
        self.cur[engine].append(
            lambda eng, fn=fn, sem=sem: fn(eng).then_inc(sem, 1))
        for b in writes:
            b.ready[sem] = cnt
        for b in reads:
            b.readers[sem] = cnt

    def dma(self, queue, fn, reads=(), writes=()):
        way = self.dissue[queue] % self.DMA_WAYS
        self.dissue[queue] += 1
        dsem = self.dsem[queue][way]
        prev = 16 * self.dcount[queue][way]
        self.dcount[queue][way] += 1
        val = prev + 16
        wt = self.waited[queue]
        if prev and wt.get(dsem, 0) < prev:
            self.cur[queue].append(
                lambda eng, s=dsem, v=prev: eng.wait_ge(s, v))
            wt[dsem] = prev
        self._waits(queue, reads, writes)
        self.n_instr += 1
        self.trace.append(
            f"{queue}: DMA#{way}:{val} r=[{','.join(b.name for b in reads)}] "
            f"w=[{','.join(b.name for b in writes)}]")
        self.cur[queue].append(
            lambda eng, fn=fn, dsem=dsem: fn(eng).then_inc(dsem, 16))
        for b in writes:
            b.ready[dsem] = val
        for b in reads:
            b.readers[dsem] = val

    def barrier_wait_all(self, engine):
        q = self.cur[engine]
        for e in ENGINES:
            if e == engine:
                continue
            s, v = self.csem[e], self.ccount[e]
            if v:
                q.append(lambda eng, s=s, v=v: eng.wait_ge(s, v))
        for qq in DMA_QUEUES:
            for i in range(self.DMA_WAYS):
                s, v = self.dsem[qq][i], 16 * self.dcount[qq][i]
                if v:
                    q.append(lambda eng, s=s, v=v: eng.wait_ge(s, v))


def build_program():
    nc = bass.Bass("TRN2", target_bir_lowering=False, debug=False)

    def dram(name, shape, dt, kind):
        return nc.dram_tensor(name, shape, dt, kind=kind).ap()

    EI, EO, IN = "ExternalInput", "ExternalOutput", "Internal"

    x_in = dram("x_in", [S, DM], F32, EI)
    wq = dram("wq", [DM, DM], F32, EI)
    wk = dram("wk", [DM, 512], F32, EI)
    wv = dram("wv", [DM, 512], F32, EI)
    wo = dram("wo", [DM, DM], F32, EI)
    wgate = dram("wgate", [INTER0, DM], F32, EI)
    wup = dram("wup", [INTER0, DM], F32, EI)
    wdown = dram("wdown", [DM, INTER0], F32, EI)
    lnw_row = dram("lnw_row", [1, INTER0], F32, EI)
    lnw_colt = dram("lnw_colt", [128, NJ0], F32, EI)
    wlat = dram("wlat", [DM, 128], F32, EI)
    blat = dram("blat", [128, 1], F32, EI)
    wd = dram("wd", [DM, DM], F32, EI)
    bdh = dram("bdh", [128, 16], F32, EI)
    wproj = dram("wproj", [DM, DM], F32, EI)
    bproj_row = dram("bproj_row", [1, DM], F32, EI)
    gatew = dram("gatew", [NEXP, DM], F32, EI)
    w1 = dram("w1", [NEXP, INTER1, DM], F32, EI)
    w2 = dram("w2", [NEXP, DM, INTER1], F32, EI)
    w3 = dram("w3", [NEXP, INTER1, DM], F32, EI)
    sw1 = dram("sw1", [INTER1, DM], F32, EI)
    sw2 = dram("sw2", [DM, INTER1], F32, EI)
    sw3 = dram("sw3", [INTER1, DM], F32, EI)
    ident_c = dram("ident_c", [128, 128], BF16, EI)
    identf_c = dram("identf_c", [128, 128], F32, EI)
    ppermt_c = dram("ppermt_c", [128, 128], BF16, EI)
    masks_c = dram("masks_c", [4 * 128, 512], BF16, EI)
    ones_c = dram("ones_c", [1, 128], F32, EI)
    csts_c = dram("csts_c", [128, 8], F32, EI)

    out = dram("out", [S, DM], F32, EO)
    xA = dram("xA", [S, DM], F32, EO if DBG else IN)
    xB = dram("xB", [S, DM], F32, EO if DBG else IN)
    xC = dram("xC", [S, DM], F32, EO if DBG else IN)

    wq_bf = dram("wq_bf", [DM, DM], BF16, IN)
    wk_bf = dram("wk_bf", [DM, 512], BF16, IN)
    wv_bf = dram("wv_bf", [DM, 512], BF16, IN)
    wo_bf = dram("wo_bf", [DM, DM], BF16, IN)
    wlat_bf = dram("wlat_bf", [DM, 128], BF16, IN)
    wd_bf = dram("wd_bf", [DM, DM], BF16, IN)
    wproj_bf = dram("wproj_bf", [DM, DM], BF16, IN)
    w1_bf = dram("w1_bf", [NEXP, INTER1, DM], BF16, IN)
    w2_bf = dram("w2_bf", [NEXP, DM, INTER1], BF16, IN)
    w3_bf = dram("w3_bf", [NEXP, INTER1, DM], BF16, IN)
    sw1_bf = dram("sw1_bf", [INTER1, DM], BF16, IN)
    sw2_bf = dram("sw2_bf", [DM, INTER1], BF16, IN)
    sw3_bf = dram("sw3_bf", [INTER1, DM], BF16, IN)
    tern_g = dram("tern_g", [INTER0, DM], BF16, IN)
    tern_u = dram("tern_u", [INTER0, DM], BF16, IN)
    tern_d = dram("tern_d", [DM, INTER0], BF16, IN)
    h_spill = dram("h_spill", [S, INTER0], BF16, IN)

    gs = contextlib.ExitStack()
    pr = Prog(nc, gs)

    BC = {}
    for nm in ("wq_bf", "wk_bf", "wv_bf", "wo_bf", "wlat_bf", "wd_bf",
               "wproj_bf", "w1_bf", "w2_bf", "w3_bf", "sw1_bf", "sw2_bf",
               "sw3_bf", "tern_g", "tern_u", "tern_d", "h_spill",
               "xA", "xB", "xC", "out", "x_in", "wgate", "wup", "wdown"):
        BC[nm] = Buf(nm)

    CONSTS = [0.25, math.pi / 2.0, 128.0, EPS_LN, EPS_RMS, 1.0]
    with gs:
        csts = pr.sb("csts", [128, 8], F32, scope="global")
        for i, v in enumerate(CONSTS):
            nc.const_aps.register(F32, v, csts[0][:, i:i + 1])
        pr.const_dep = csts[1]
        ident = pr.sb("ident", [128, 128], BF16, scope="global")
        identf = pr.sb("identf", [128, 128], F32, scope="global")
        ppermt = pr.sb("ppermt", [128, 128], BF16, scope="global")
        onesf = pr.sb("onesf", [1, 128], F32, scope="global")
        cos_t = pr.sb("cos_t", [128, S], F32, scope="global")
        sin_t = pr.sb("sin_t", [128, S], F32, scope="global")
        arena = pr.sb("arena", [128, 32768], BF16, scope="global")
        wm_keep = pr.sb("wm_keep", [128, 3], F32, scope="global")
        isx_keep = pr.sb("isx_keep", [128, NTS], F32, scope="global")
        hstat = pr.sb("hstat", [128, 6 * NTS], F32, scope="global")
        crows = pr.sb("crows", [1, S], F32, scope="global")
        cmb = pr.sb("cmb", [128, 16 * 16], F32, scope="global")

        def A(c, lo, n):
            return arena[0][:, c * S + lo: c * S + lo + n]

        def emit_round(dst_ap, dst_buf, src_ap, src_buf, scr1, scr2, width,
                       base=128.0, nbits=8):
            d_h, d_b = scr1
            c_h, c_b = scr2
            pr.op("vector", lambda eng: eng.tensor_single_scalar(
                d_h[:, 0:width], src_ap, base, ALU.add),
                reads=[src_buf], writes=[d_b])
            bit = base
            for _ in range(nbits):
                pr.op("vector", lambda eng, bit=bit: eng.tensor_single_scalar(
                    c_h[:, 0:width], d_h[:, 0:width], bit - 0.5, ALU.is_ge),
                    reads=[d_b], writes=[c_b])
                pr.op("vector", lambda eng, bit=bit: eng.scalar_tensor_tensor(
                    d_h[:, 0:width], c_h[:, 0:width], -bit, d_h[:, 0:width],
                    ALU.mult, ALU.add),
                    reads=[c_b, d_b], writes=[d_b])
                bit *= 0.5
            pr.op("vector", lambda eng: eng.scalar_tensor_tensor(
                dst_ap, d_h[:, 0:width], -1.0, src_ap, ALU.mult, ALU.add),
                reads=[d_b, src_buf], writes=[dst_buf])

        def emit_tr128(ptps, dst_ap, dst_buf, src_ap, src_buf,
                       evac="scalar", idt=None):
            pt_h, pt_b = ptps
            idt = idt or ident
            pr.op("tensor", lambda eng: eng.transpose(
                pt_h[:, :], src_ap, idt[0][:, :]),
                reads=[src_buf, idt[1]], writes=[pt_b])
            if evac == "scalar":
                pr.op("scalar", lambda eng: eng.copy(dst_ap, pt_h[:, :]),
                      reads=[pt_b], writes=[dst_buf])
            else:
                pr.op(evac, lambda eng: eng.tensor_copy(dst_ap, pt_h[:, :]),
                      reads=[pt_b], writes=[dst_buf])

        with pr.phase("setup"):
            pr.dma("gpsimd", lambda eng: eng.dma_start(
                out=csts[0][:, :], in_=csts_c), writes=[csts[1]])
            for (dst, src, b) in ((ident, ident_c, None), (identf, identf_c, None),
                                  (ppermt, ppermt_c, None), (onesf, ones_c, None)):
                pr.dma("gpsimd", lambda eng, dst=dst, src=src: eng.dma_start(
                    out=dst[0][:, :], in_=src), writes=[dst[1]])

            def cast(dst, src, dbuf, split=1):
                step = src.shape[0] // split
                for i in range(split):
                    pr.dma("gpsimd", lambda eng, i=i, dst=dst, src=src, step=step:
                           eng.dma_start(out=dst[i * step:(i + 1) * step],
                                         in_=src[i * step:(i + 1) * step],
                                         max_dma_last_dim=8192), writes=[dbuf])

            KCAST = int(os.environ.get("KCAST", "1"))
            if KCAST:
                cast(wq_bf, wq, BC["wq_bf"])
            if KCAST:
                cast(wk_bf, wk, BC["wk_bf"])
                cast(wv_bf, wv, BC["wv_bf"])
                cast(wo_bf, wo, BC["wo_bf"])
                cast(wlat_bf, wlat, BC["wlat_bf"])
                cast(wd_bf, wd, BC["wd_bf"])
                cast(wproj_bf, wproj, BC["wproj_bf"])
                for i in range(NEXP):
                    cast(w1_bf[i], w1[i], BC["w1_bf"])
                    cast(w3_bf[i], w3[i], BC["w3_bf"])
                    cast(w2_bf[i], w2[i], BC["w2_bf"])
                cast(sw1_bf, sw1, BC["sw1_bf"])
                cast(sw3_bf, sw3, BC["sw3_bf"])
                cast(sw2_bf, sw2, BC["sw2_bf"])

            dcol = pr.sb("dcol", [128, 2], F32)
            invf = pr.sb("invf", [128, 1], F32)
            freqs = pr.sb("freqs", [128, S], F32)
            ts1 = pr.sb("ts1", [128, S], F32)
            ts2 = pr.sb("ts2", [128, S], F32)
            ts3 = pr.sb("ts3", [128, S], F32)
            ts4 = pr.sb("ts4", [128, S], F32)
            pr.op("gpsimd", lambda eng: eng.iota(
                dcol[0][:, 0:1], [[0, 1]], channel_multiplier=1,
                allow_small_or_imprecise_dtypes=True), writes=[dcol[1]])
            pr.op("vector", lambda eng: eng.tensor_scalar(
                dcol[0][:, 1:2], dcol[0][:, 0:1], 64.0, 64.0, ALU.is_ge, ALU.mult),
                reads=[dcol[1]], writes=[dcol[1]])
            pr.op("vector", lambda eng: eng.tensor_tensor(
                dcol[0][:, 0:1], dcol[0][:, 0:1], dcol[0][:, 1:2], ALU.subtract),
                reads=[dcol[1]], writes=[dcol[1]])
            pr.op("scalar", lambda eng: eng.activation(
                invf[0][:, :], dcol[0][:, 0:1], ACT.Exp, bias=0.0,
                scale=-math.log(10000.0) / 64.0), reads=[dcol[1]], writes=[invf[1]])
            pr.op("gpsimd", lambda eng: eng.iota(
                ts1[0][:, :], [[1, S]], channel_multiplier=0,
                allow_small_or_imprecise_dtypes=True), writes=[ts1[1]])
            pr.op("scalar", lambda eng: eng.activation(
                freqs[0][:, :], ts1[0][:, :], ACT.Identity, bias=0.0,
                scale=invf[0][:, :]), reads=[ts1[1], invf[1]], writes=[freqs[1]])
            clampv = float(np.float32(3.1415925))
            twopi = 2.0 * math.pi

            def table(dst, shift):
                pr.op("scalar", lambda eng: eng.activation(
                    ts1[0][:, :], freqs[0][:, :], ACT.Identity,
                    bias=shift / twopi, scale=1.0 / twopi),
                    reads=[freqs[1]], writes=[ts1[1]])
                emit_round(ts2[0][:, :], ts2[1], ts1[0][:, :], ts1[1],
                           ts3, ts4, S, base=512.0, nbits=10)
                pr.op("scalar", lambda eng: eng.activation(
                    ts1[0][:, :], freqs[0][:, :], ACT.Identity,
                    bias=shift, scale=1.0), reads=[freqs[1]], writes=[ts1[1]])
                pr.op("vector", lambda eng: eng.scalar_tensor_tensor(
                    ts1[0][:, :], ts2[0][:, :], -twopi, ts1[0][:, :],
                    ALU.mult, ALU.add), reads=[ts2[1], ts1[1]], writes=[ts1[1]])
                pr.op("vector", lambda eng: eng.tensor_single_scalar(
                    ts1[0][:, :], ts1[0][:, :], clampv, ALU.min),
                    reads=[ts1[1]], writes=[ts1[1]])
                pr.op("vector", lambda eng: eng.tensor_single_scalar(
                    ts1[0][:, :], ts1[0][:, :], -clampv, ALU.max),
                    reads=[ts1[1]], writes=[ts1[1]])
                pr.op("scalar", lambda eng: eng.activation(
                    dst[0][:, :], ts1[0][:, :], ACT.Sin, bias=0.0, scale=1.0),
                    reads=[ts1[1]], writes=[dst[1]])

            if int(os.environ.get("KTAB", "1")):
                table(sin_t, 0.0)
                table(cos_t, math.pi / 2.0)

        TERN_ON = int(os.environ.get("KTERN", "1"))
        with pr.phase("ternall"):
          wt = [pr.sb("wt", [128, 2048], F32) for _ in range(2)]
          acc = pr.sb("acc", [128, 1], F32)
          tsum = pr.sb("tsum", [128, 2], F32)
          scol = pr.sb("scol", [128, 2], F32)
          sred = pr.sb("sred", [1, 2], F32)
          srps = pr.ps("srps", [1, 512], F32)
          brps = pr.ps("brps", [128, 512], F32)
          t1 = pr.sb("t1", [128, 2048], F32)
          t2 = pr.sb("t2", [128, 2048], F32)
          to = [pr.sb("to", [128, 2048], BF16) for _ in range(2)]
          for idx, (src, sbuf_, tdst, tbuf, nrow, ncol) in enumerate(() if not TERN_ON else (
                (wgate, BC["wgate"], tern_g, BC["tern_g"], INTER0, DM),
                (wup, BC["wup"], tern_u, BC["tern_u"], INTER0, DM),
                (wdown, BC["wdown"], tern_d, BC["tern_d"], DM, INTER0))):
            if True:
                nrt, nct = nrow // 128, ncol // 2048
                pr.op("vector", lambda eng: eng.memset(acc[0][:, :], 0.0),
                      writes=[acc[1]])
                for it in range(nrt * nct):
                    r, c = it % nrt, it // nrt
                    w = wt[it % 2]
                    pr.dma("scalar", lambda eng, r=r, c=c, w=w, src=src: eng.dma_start(
                        out=w[0][:, :],
                        in_=src[r * 128:(r + 1) * 128, c * 2048:(c + 1) * 2048]),
                        reads=[sbuf_], writes=[w[1]])
                    pr.op("vector", lambda eng, w=w: eng.reduce_sum(
                        tsum[0][:, 0:1], w[0][:, :], axis=AX.X,
                        apply_absolute_value=True), reads=[w[1]], writes=[tsum[1]])
                    pr.op("vector", lambda eng: eng.tensor_tensor(
                        acc[0][:, :], acc[0][:, :], tsum[0][:, 0:1], ALU.add),
                        reads=[acc[1], tsum[1]], writes=[acc[1]])
                pr.op("tensor", lambda eng: eng.matmul(
                    srps[0][:, 0:1], csts[0][:, 5:6], acc[0][:, :],
                    start=True, stop=True),
                    reads=[csts[1], acc[1]], writes=[srps[1]])
                pr.op("scalar", lambda eng: eng.copy(
                    sred[0][:, 0:1], srps[0][:, 0:1]),
                    reads=[srps[1]], writes=[sred[1]])
                pr.op("vector", lambda eng: eng.tensor_scalar(
                    sred[0][:, 1:2], sred[0][:, 0:1], 1.0 / (nrow * ncol),
                    EPS_Q, ALU.mult, ALU.max), reads=[sred[1]], writes=[sred[1]])
                pr.op("tensor", lambda eng: eng.matmul(
                    brps[0][:, 0:1], onesf[0][:, :], sred[0][0:1, 1:2],
                    start=True, stop=True),
                    reads=[onesf[1], sred[1]], writes=[brps[1]])
                pr.op("scalar", lambda eng, idx=idx: eng.copy(
                    wm_keep[0][:, idx:idx + 1], brps[0][:, 0:1]),
                    reads=[brps[1]], writes=[wm_keep[1]])
                pr.op("vector", lambda eng, idx=idx: eng.reciprocal(
                    scol[0][:, 0:1], wm_keep[0][:, idx:idx + 1]),
                    reads=[wm_keep[1]], writes=[scol[1]])
                pr.op("vector", lambda eng: eng.tensor_single_scalar(
                    scol[0][:, 1:2], scol[0][:, 0:1], -1.0, ALU.mult),
                    reads=[scol[1]], writes=[scol[1]])
                for it in range(nrt * nct):
                    r, c = it % nrt, it // nrt
                    w = wt[it % 2]
                    o = to[it % 2]
                    pr.dma("scalar", lambda eng, r=r, c=c, w=w, src=src: eng.dma_start(
                        out=w[0][:, :],
                        in_=src[r * 128:(r + 1) * 128, c * 2048:(c + 1) * 2048]),
                        reads=[sbuf_], writes=[w[1]])
                    pr.op("vector", lambda eng, w=w: eng.tensor_scalar(
                        t1[0][:, :], w[0][:, :], scol[0][:, 0:1], 0.5,
                        ALU.mult, ALU.is_gt), reads=[w[1], scol[1]], writes=[t1[1]])
                    pr.op("vector", lambda eng, w=w: eng.tensor_scalar(
                        t2[0][:, :], w[0][:, :], scol[0][:, 1:2], 0.5,
                        ALU.mult, ALU.is_gt), reads=[w[1], scol[1]], writes=[t2[1]])
                    pr.op("gpsimd", lambda eng, o=o: eng.tensor_tensor(
                        o[0][:, :], t1[0][:, :], t2[0][:, :], ALU.subtract),
                        reads=[t1[1], t2[1]], writes=[o[1]])
                    pr.dma("gpsimd", lambda eng, r=r, c=c, o=o, tdst=tdst: eng.dma_start(
                        out=tdst[r * 128:(r + 1) * 128, c * 2048:(c + 1) * 2048],
                        in_=o[0][:, :]), reads=[o[1]], writes=[tbuf])

        def make_cols(quant):
            cols = {}
            for nm in ("rstd", "mb", "t1", "t2"):
                cols[nm] = pr.sb("col_" + nm, [128, NTS], F32)
            if quant:
                for nm in ("mx", "sx", "isx"):
                    cols[nm] = pr.sb("col_" + nm, [128, NTS], F32)
            return cols

        def emit_ln_stats(x_ap, x_buf, sq, cols, ci):
            c = slice(ci, ci + 1)
            r, m, t1c, t2c = cols["rstd"], cols["mb"], cols["t1"], cols["t2"]
            pr.op("vector", lambda eng: eng.reduce_sum(
                t1c[0][:, c], x_ap, axis=AX.X), reads=[x_buf], writes=[t1c[1]])
            pr.op("scalar", lambda eng: eng.activation(
                sq[0][:, :], x_ap, ACT.Square, bias=0.0, scale=1.0,
                accum_out=t2c[0][:, c]), reads=[x_buf], writes=[sq[1], t2c[1]])
            pr.op("vector", lambda eng: eng.tensor_single_scalar(
                t1c[0][:, c], t1c[0][:, c], 1.0 / DM, ALU.mult),
                reads=[t1c[1]], writes=[t1c[1]])
            pr.op("vector", lambda eng: eng.tensor_tensor(
                m[0][:, c], t1c[0][:, c], t1c[0][:, c], ALU.mult),
                reads=[t1c[1]], writes=[m[1]])
            pr.op("vector", lambda eng: eng.scalar_tensor_tensor(
                t2c[0][:, c], t2c[0][:, c], 1.0 / DM, m[0][:, c],
                ALU.mult, ALU.subtract), reads=[t2c[1], m[1]], writes=[t2c[1]])
            pr.op("scalar", lambda eng: eng.activation(
                t2c[0][:, c], t2c[0][:, c], ACT.Sqrt, bias=EPS_LN, scale=1.0),
                reads=[t2c[1]], writes=[t2c[1]])
            pr.op("vector", lambda eng: eng.reciprocal(
                r[0][:, c], t2c[0][:, c]), reads=[t2c[1]], writes=[r[1]])
            pr.op("vector", lambda eng: eng.scalar_tensor_tensor(
                m[0][:, c], t1c[0][:, c], -1.0, r[0][:, c], ALU.mult, ALU.mult),
                reads=[t1c[1], r[1]], writes=[m[1]])

        def emit_ln_to_fm(xsrc, xsrc_buf, quant, cols, ptps):
            xt = [pr.sb("xt", [128, DM], F32) for _ in range(2)]
            sq = pr.sb("sq", [128, DM], F32)
            xl = pr.sb("xl", [128, DM], F32)
            xlb = pr.sb("xlb", [128, DM], BF16)
            r1 = pr.sb("r1", [128, DM], F32)
            r2 = pr.sb("r2", [128, DM], F32)
            for ts in range(NTS):
                x = xt[ts % 2]
                pr.dma("sync", lambda eng, ts=ts, x=x: eng.dma_start(
                    out=x[0][:, :], in_=xsrc[ts * 128:(ts + 1) * 128, :]),
                    reads=[xsrc_buf], writes=[x[1]])
                emit_ln_stats(x[0][:, :], x[1], sq, cols, ts)
                if not quant:
                    pr.op("scalar", lambda eng, ts=ts, x=x: eng.activation(
                        xlb[0][:, :], x[0][:, :], ACT.Identity,
                        bias=cols["mb"][0][:, ts:ts + 1],
                        scale=cols["rstd"][0][:, ts:ts + 1]),
                        reads=[x[1], cols["mb"][1], cols["rstd"][1]],
                        writes=[xlb[1]])
                else:
                    mx, sx, isx = cols["mx"], cols["sx"], cols["isx"]
                    pr.op("scalar", lambda eng, ts=ts, x=x: eng.activation(
                        xl[0][:, :], x[0][:, :], ACT.Identity,
                        bias=cols["mb"][0][:, ts:ts + 1],
                        scale=cols["rstd"][0][:, ts:ts + 1]),
                        reads=[x[1], cols["mb"][1], cols["rstd"][1]],
                        writes=[xl[1]])
                    pr.op("vector", lambda eng, ts=ts: eng.reduce_max(
                        mx[0][:, ts:ts + 1], xl[0][:, :], axis=AX.X,
                        apply_absolute_value=True), reads=[xl[1]], writes=[mx[1]])
                    pr.op("vector", lambda eng, ts=ts: eng.tensor_single_scalar(
                        mx[0][:, ts:ts + 1], mx[0][:, ts:ts + 1], EPS_Q, ALU.max),
                        reads=[mx[1]], writes=[mx[1]])
                    pr.op("vector", lambda eng, ts=ts: eng.tensor_single_scalar(
                        isx[0][:, ts:ts + 1], mx[0][:, ts:ts + 1], 1.0 / 127.0,
                        ALU.mult), reads=[mx[1]], writes=[isx[1]])
                    pr.op("vector", lambda eng, ts=ts: eng.reciprocal(
                        sx[0][:, ts:ts + 1], isx[0][:, ts:ts + 1]),
                        reads=[isx[1]], writes=[sx[1]])
                    pr.op("scalar", lambda eng, ts=ts: eng.activation(
                        xl[0][:, :], xl[0][:, :], ACT.Identity, bias=0.0,
                        scale=sx[0][:, ts:ts + 1]),
                        reads=[xl[1], sx[1]], writes=[xl[1]])
                    emit_round(xlb[0][:, :], xlb[1], xl[0][:, :], xl[1],
                               r1, r2, DM)
                for c in range(NCH):
                    emit_tr128(ptps, A(c, ts * 128, 128), arena[1],
                               xlb[0][:, c * 128:(c + 1) * 128], xlb[1])

        def attention_block(name, xsrc, xsrc_buf, xdst, xdst_buf, is_l0):
            NH = 16
            KSUB = int(os.environ.get("KSUB", "99"))
            KQC = int(os.environ.get("KQC", "99"))
            KH = int(os.environ.get("KH", "99"))
            if KSUB < 2:
                return
            with contextlib.ExitStack() as ost:
                if is_l0:
                    kst = pr.sb_on(ost, "kst", [128, 4 * S], BF16)
                    vst = pr.sb_on(ost, "vst", [128, NTS * 512], BF16)
                else:
                    kst = pr.sb_on(ost, "kst", [128, S], BF16)
                    vst = pr.sb_on(ost, "vst", [128, NTS * 128], BF16)
                    bpb = pr.sb_on(ost, "bpb", [128, DM], F32)
                with pr.phase(name + "_kv"):
                    cols = make_cols(False)
                    ptps = pr.ps("ptp", [128, 128], BF16)
                    if int(os.environ.get("KLN", "1")):
                        emit_ln_to_fm(xsrc, xsrc_buf, False, cols, ptps)
                    kps = pr.ps("kps", [128, 512], F32)
                    kps2 = pr.ps("kps2", [128, 512], F32)
                    krawb = pr.sb("krawb", [128, 512], BF16)
                    t1s = pr.sb("t1s", [128, 512], F32)
                    t2s = pr.sb("t2s", [128, 512], F32)
                    wct = [pr.sb("wct", [128, 512], BF16) for _ in range(NCH)]
                    KKV = int(os.environ.get("KKV", "99"))
                    if is_l0:
                        for c in range(NCH):
                            pr.dma("scalar", lambda eng, c=c: eng.dma_start(
                                out=wct[c][0][:, :],
                                in_=wk_bf[c * 128:(c + 1) * 128, :]),
                                reads=[BC["wk_bf"]], writes=[wct[c][1]])
                        KIT = int(os.environ.get("KIT", "99"))
                        KROPE = int(os.environ.get("KROPE", "1"))
                        it_count = 0
                        for kv in range(4 if KKV >= 2 else 0):
                            for tc in range(NTC):
                                it_count += 1
                                if it_count > KIT:
                                    continue
                                for c in range(NCH):
                                    pr.op("tensor", lambda eng, kv=kv, tc=tc, c=c: eng.matmul(
                                        kps[0][:, :],
                                        wct[c][0][:, kv * 128:(kv + 1) * 128],
                                        A(c, tc * 512, 512),
                                        start=(c == 0), stop=(c == NCH - 1)),
                                        reads=[wct[c][1], arena[1]],
                                        writes=[kps[1]] + ([kps2[1]] if c == 0 else []))
                                pr.op("scalar", lambda eng: eng.copy(
                                    krawb[0][:, :], kps[0][:, :]),
                                    reads=[kps[1]], writes=[krawb[1]])
                                pr.op("tensor", lambda eng: eng.matmul(
                                    kps2[0][:, :], ppermt[0][:, :], krawb[0][:, :],
                                    start=True, stop=True),
                                    reads=[ppermt[1], krawb[1]], writes=[kps2[1]])
                                pr.op("vector", lambda eng, tc=tc: eng.tensor_tensor(
                                    t1s[0][:, :], krawb[0][:, :],
                                    cos_t[0][:, tc * 512:(tc + 1) * 512], ALU.mult),
                                    reads=[krawb[1], cos_t[1]], writes=[t1s[1]])
                                pr.op("vector", lambda eng, tc=tc: eng.tensor_tensor(
                                    t2s[0][:, :], kps2[0][:, :],
                                    sin_t[0][:, tc * 512:(tc + 1) * 512], ALU.mult),
                                    reads=[kps2[1], sin_t[1]], writes=[t2s[1]])
                                pr.op("vector", lambda eng, kv=kv, tc=tc: eng.tensor_tensor(
                                    kst[0][:, kv * S + tc * 512: kv * S + (tc + 1) * 512],
                                    t1s[0][:, :], t2s[0][:, :], ALU.add),
                                    reads=[t1s[1], t2s[1]], writes=[kst[1]])
                        for c in range(NCH):
                            pr.dma("scalar", lambda eng, c=c: eng.dma_start(
                                out=wct[c][0][:, :],
                                in_=wv_bf[c * 128:(c + 1) * 128, :]),
                                reads=[BC["wv_bf"]], writes=[wct[c][1]])
                        for ts in range(NTS if KKV >= 3 else 0):
                            for c in range(NCH):
                                pr.op("tensor", lambda eng, ts=ts, c=c: eng.matmul(
                                    kps[0][:, :], A(c, ts * 128, 128),
                                    wct[c][0][:, :],
                                    start=(c == 0), stop=(c == NCH - 1)),
                                    reads=[arena[1], wct[c][1]], writes=[kps[1]])
                            pr.op("scalar", lambda eng, ts=ts: eng.copy(
                                vst[0][:, ts * 512:(ts + 1) * 512], kps[0][:, :]),
                                reads=[kps[1]], writes=[vst[1]])
                    else:
                        biascol0 = pr.sb("biascol0", [128, 1], F32)
                        pr.dma("gpsimd", lambda eng: eng.dma_start(
                            out=biascol0[0][:, :], in_=blat), writes=[biascol0[1]])
                        for c in range(NCH):
                            pr.dma("scalar", lambda eng, c=c: eng.dma_start(
                                out=wct[c][0][:, 0:128],
                                in_=wlat_bf[c * 128:(c + 1) * 128, :]),
                                reads=[BC["wlat_bf"]], writes=[wct[c][1]])
                        for tc in range(NTC):
                            for c in range(NCH):
                                pr.op("tensor", lambda eng, tc=tc, c=c: eng.matmul(
                                    kps[0][:, :], wct[c][0][:, 0:128],
                                    A(c, tc * 512, 512),
                                    start=(c == 0), stop=(c == NCH - 1)),
                                    reads=[wct[c][1], arena[1]], writes=[kps[1]])
                            pr.op("scalar", lambda eng, tc=tc: eng.activation(
                                kst[0][:, tc * 512:(tc + 1) * 512], kps[0][:, :],
                                ACT.Identity, bias=biascol0[0][:, 0:1], scale=1.0),
                                reads=[kps[1], biascol0[1]], writes=[kst[1]])
                        for ks in range(NTS):
                            emit_tr128(ptps, vst[0][:, ks * 128:(ks + 1) * 128],
                                       vst[1], kst[0][:, ks * 128:(ks + 1) * 128],
                                       kst[1])
                        bpr = pr.sb("bpr", [1, DM], F32)
                        pr.dma("gpsimd", lambda eng: eng.dma_start(
                            out=bpr[0][:, :], in_=bproj_row), writes=[bpr[1]])
                        for dc in range(NTC):
                            pr.op("tensor", lambda eng, dc=dc: eng.matmul(
                                kps[0][:, :], onesf[0][:, :],
                                bpr[0][0:1, dc * 512:(dc + 1) * 512],
                                start=True, stop=True),
                                reads=[onesf[1], bpr[1]], writes=[kps[1]])
                            pr.op("scalar", lambda eng, dc=dc: eng.copy(
                                bpb[0][:, dc * 512:(dc + 1) * 512], kps[0][:, :]),
                                reads=[kps[1]], writes=[bpb[1]])

                if KSUB < 3:
                    return
                with pr.phase(name + "_qa"):
                    ptps = pr.ps("ptp", [128, 128], BF16)
                    scps = [pr.ps("scps", [128, 512], F32) for _ in range(2)]
                    yps = pr.ps("yps", [128, 512], F32)
                    qps = pr.ps("qps", [128, 512], F32)
                    qps2 = pr.ps("qps2", [128, 512], F32)
                    masks = pr.sb("masks", [128, 4 * 512], BF16)
                    for i in range(4):
                        pr.dma("gpsimd", lambda eng, i=i: eng.dma_start(
                            out=masks[0][:, i * 512:(i + 1) * 512],
                            in_=masks_c[i * 128:(i + 1) * 128, :]),
                            writes=[masks[1]])
                    t1s = pr.sb("t1s", [128, 512], F32)
                    t2s = pr.sb("t2s", [128, 512], F32)
                    biascol = pr.sb("biascol", [128, 16], F32)
                    if not is_l0:
                        pr.dma("gpsimd", lambda eng: eng.dma_start(
                            out=biascol[0][:, :], in_=bdh), writes=[biascol[1]])
                    qbf = pr.sb("qbf", [128, 512], BF16)
                    qraw = pr.sb("qraw", [128, 512], BF16)
                    mscs = [pr.sb("msc", [128, 2048], F32) for _ in range(2)]
                    pnbs = [pr.sb("pnb", [128, 2048], BF16)]
                    ptst = pr.sb("ptst", [128, 16 * 512], BF16)
                    aoc = pr.sb("aoc", [128, NH * 512], BF16)
                    rowms = [pr.sb("rowm", [128, 4], F32) for _ in range(2)]
                    wqt = [pr.sb("wqt", [128, 128], BF16) for _ in range(2)]
                    woall = pr.sb("woall", [128, NCH * 512], BF16)
                    xres = [pr.sb("xres", [128, 512], F32)]
                    oacc = pr.sb("oacc", [128, 512], F32)
                    wsrc = wq_bf if is_l0 else wd_bf
                    wsrcb = BC["wq_bf"] if is_l0 else BC["wd_bf"]
                    wob = wo_bf if is_l0 else wproj_bf
                    wobuf = BC["wo_bf"] if is_l0 else BC["wproj_bf"]
                    for qc in range(min(NTC, KQC)):
                        nkc = qc + 1
                        for h in range(min(NH, KH)):
                            for c in range(NCH):
                                w = wqt[c % 2]
                                pr.dma("scalar", lambda eng, c=c, h=h, w=w: eng.dma_start(
                                    out=w[0][:, :],
                                    in_=wsrc[c * 128:(c + 1) * 128,
                                             h * 128:(h + 1) * 128]),
                                    reads=[wsrcb], writes=[w[1]])
                                pr.op("tensor", lambda eng, c=c, qc=qc, w=w: eng.matmul(
                                    qps[0][:, :], w[0][:, :], A(c, qc * 512, 512),
                                    start=(c == 0), stop=(c == NCH - 1)),
                                    reads=[w[1], arena[1]], writes=[qps[1]])
                            if is_l0:
                                pr.op("scalar", lambda eng: eng.copy(
                                    qraw[0][:, :], qps[0][:, :]),
                                    reads=[qps[1]], writes=[qraw[1]])
                                pr.op("tensor", lambda eng: eng.matmul(
                                    qps2[0][:, :], ppermt[0][:, :], qraw[0][:, :],
                                    start=True, stop=True),
                                    reads=[ppermt[1], qraw[1]], writes=[qps2[1]])
                                pr.op("vector", lambda eng, qc=qc: eng.tensor_tensor(
                                    t1s[0][:, :], qraw[0][:, :],
                                    cos_t[0][:, qc * 512:(qc + 1) * 512], ALU.mult),
                                    reads=[qraw[1], cos_t[1]], writes=[t1s[1]])
                                pr.op("vector", lambda eng, qc=qc: eng.tensor_tensor(
                                    t2s[0][:, :], qps2[0][:, :],
                                    sin_t[0][:, qc * 512:(qc + 1) * 512], ALU.mult),
                                    reads=[qps2[1], sin_t[1]], writes=[t2s[1]])
                                pr.op("vector", lambda eng: eng.tensor_tensor(
                                    qbf[0][:, :], t1s[0][:, :], t2s[0][:, :], ALU.add),
                                    reads=[t1s[1], t2s[1]], writes=[qbf[1]])
                            else:
                                pr.op("scalar", lambda eng, h=h: eng.activation(
                                    qbf[0][:, :], qps[0][:, :], ACT.Identity,
                                    bias=biascol[0][:, h:h + 1], scale=1.0),
                                    reads=[qps[1], biascol[1]], writes=[qbf[1]])
                            kv = h // 4 if is_l0 else 0
                            koff = kv * S if is_l0 else 0
                            for qs in range(4):
                                msc = mscs[qs % 2]
                                pnb = pnbs[0]
                                rowm = rowms[qs % 2]
                                for kc in range(nkc):
                                    sp = scps[kc % 2]
                                    pr.op("tensor", lambda eng, qs=qs, kc=kc, sp=sp, koff=koff: eng.matmul(
                                        sp[0][:, :],
                                        qbf[0][:, qs * 128:(qs + 1) * 128],
                                        kst[0][:, koff + kc * 512: koff + (kc + 1) * 512],
                                        start=True, stop=True),
                                        reads=[qbf[1], kst[1]], writes=[sp[1]])
                                    if kc == nkc - 1:
                                        pr.op("vector", lambda eng, qs=qs, kc=kc, sp=sp: eng.scalar_tensor_tensor(
                                            msc[0][:, kc * 512:(kc + 1) * 512],
                                            sp[0][:, :], ISQ,
                                            masks[0][:, qs * 512:(qs + 1) * 512],
                                            ALU.mult, ALU.add),
                                            reads=[sp[1], masks[1]], writes=[msc[1]])
                                    else:
                                        pr.op("vector", lambda eng, kc=kc, sp=sp: eng.tensor_single_scalar(
                                            msc[0][:, kc * 512:(kc + 1) * 512],
                                            sp[0][:, :], ISQ, ALU.mult),
                                            reads=[sp[1]], writes=[msc[1]])
                                W = nkc * 512
                                pr.op("vector", lambda eng, W=W: eng.reduce_max(
                                    rowm[0][:, 0:1], msc[0][:, 0:W], axis=AX.X),
                                    reads=[msc[1]], writes=[rowm[1]])
                                pr.op("vector", lambda eng: eng.tensor_single_scalar(
                                    rowm[0][:, 1:2], rowm[0][:, 0:1], -1.0, ALU.mult),
                                    reads=[rowm[1]], writes=[rowm[1]])
                                pr.op("scalar", lambda eng, W=W: eng.activation(
                                    pnb[0][:, 0:W], msc[0][:, 0:W], ACT.Exp,
                                    bias=rowm[0][:, 1:2], scale=1.0,
                                    accum_out=rowm[0][:, 2:3]),
                                    reads=[msc[1], rowm[1]], writes=[pnb[1], rowm[1]])
                                pr.op("scalar", lambda eng: eng.activation(
                                    rowm[0][:, 3:4], rowm[0][:, 2:3], ACT.Ln,
                                    bias=0.0, scale=1.0),
                                    reads=[rowm[1]], writes=[rowm[1]])
                                pr.op("vector", lambda eng: eng.tensor_tensor(
                                    rowm[0][:, 3:4], rowm[0][:, 1:2],
                                    rowm[0][:, 3:4], ALU.subtract),
                                    reads=[rowm[1]], writes=[rowm[1]])
                                pr.op("scalar", lambda eng, W=W: eng.activation(
                                    pnb[0][:, 0:W], msc[0][:, 0:W], ACT.Exp,
                                    bias=rowm[0][:, 3:4], scale=1.0),
                                    reads=[msc[1], rowm[1]], writes=[pnb[1]])
                                for ks in range(nkc * 4):
                                    emit_tr128(
                                        ptps,
                                        ptst[0][:, ks * 512 + qs * 128:
                                                ks * 512 + (qs + 1) * 128],
                                        ptst[1],
                                        pnb[0][:, ks * 128:(ks + 1) * 128], pnb[1])
                            nks = nkc * 4
                            for ks in range(nks):
                                voff = (ks * 512 + kv * 128) if is_l0 else (ks * 128)
                                pr.op("tensor", lambda eng, ks=ks, voff=voff, nks=nks: eng.matmul(
                                    yps[0][:, :], vst[0][:, voff: voff + 128],
                                    ptst[0][:, ks * 512:(ks + 1) * 512],
                                    start=(ks == 0), stop=(ks == nks - 1)),
                                    reads=[vst[1], ptst[1]], writes=[yps[1]])
                            pr.op("scalar", lambda eng, h=h: eng.copy(
                                aoc[0][:, h * 512:(h + 1) * 512], yps[0][:, :]),
                                reads=[yps[1]], writes=[aoc[1]])
                        for dc in range(NTC):
                            for c in range(NCH):
                                pr.dma("scalar", lambda eng, c=c, dc=dc: eng.dma_start(
                                    out=woall[0][:, c * 512:(c + 1) * 512],
                                    in_=wob[c * 128:(c + 1) * 128,
                                            dc * 512:(dc + 1) * 512]),
                                    reads=[wobuf], writes=[woall[1]])
                            for qs in range(4):
                                sp = scps[qs % 2]
                                for c in range(NCH):
                                    pr.op("tensor", lambda eng, c=c, qs=qs, sp=sp: eng.matmul(
                                        sp[0][:, :],
                                        aoc[0][:, c * 512 + qs * 128:
                                                c * 512 + (qs + 1) * 128],
                                        woall[0][:, c * 512:(c + 1) * 512],
                                        start=(c == 0), stop=(c == NCH - 1)),
                                        reads=[aoc[1], woall[1]], writes=[sp[1]])
                                ts = qc * 4 + qs
                                xr = xres[0]
                                pr.dma("sync", lambda eng, ts=ts, dc=dc, xr=xr: eng.dma_start(
                                    out=xr[0][:, :],
                                    in_=xsrc[ts * 128:(ts + 1) * 128,
                                             dc * 512:(dc + 1) * 512]),
                                    reads=[xsrc_buf], writes=[xr[1]])
                                if is_l0:
                                    pr.op("vector", lambda eng, sp=sp, xr=xr: eng.tensor_tensor(
                                        oacc[0][:, :], sp[0][:, :], xr[0][:, :],
                                        ALU.add), reads=[sp[1], xr[1]], writes=[oacc[1]])
                                else:
                                    pr.op("vector", lambda eng, sp=sp, dc=dc: eng.tensor_tensor(
                                        oacc[0][:, :], sp[0][:, :],
                                        bpb[0][:, dc * 512:(dc + 1) * 512], ALU.add),
                                        reads=[sp[1], bpb[1]], writes=[oacc[1]])
                                    pr.op("vector", lambda eng, xr=xr: eng.tensor_tensor(
                                        oacc[0][:, :], oacc[0][:, :], xr[0][:, :],
                                        ALU.add), reads=[oacc[1], xr[1]], writes=[oacc[1]])
                                pr.dma("sync", lambda eng, ts=ts, dc=dc: eng.dma_start(
                                    out=xdst[ts * 128:(ts + 1) * 128,
                                             dc * 512:(dc + 1) * 512],
                                    in_=oacc[0][:, :]),
                                    reads=[oacc[1]], writes=[xdst_buf])

        STAGES = int(os.environ.get("KSTAGES", "99"))
        if STAGES >= 2:
            attention_block("l0a", x_in, BC["x_in"], xA, BC["xA"], True)

        def emit_l0f():
            with pr.phase("l0f_qx"):
                cols0 = make_cols(True)
                ptps = pr.ps("ptp", [128, 128], BF16)
                emit_ln_to_fm(xA, BC["xA"], True, cols0, ptps)
                pr.op("vector", lambda eng: eng.tensor_copy(
                    isx_keep[0][:, :], cols0["isx"][0][:, :]),
                    reads=[cols0["isx"][1]], writes=[isx_keep[1]])

            with pr.phase("l0f_gateup"):
                gt = [pr.sb("gt", [128, 512], BF16) for _ in range(NCH)]
                ut = [pr.sb("ut", [128, 512], BF16) for _ in range(NCH)]
                gps = pr.ps("gps", [128, 512], F32)
                ups = pr.ps("ups", [128, 512], F32)
                lnps = pr.ps("lnps", [128, 512], F32)
                sgs = pr.sb("sgs", [128, 512], F32)
                hb = [pr.sb("hb", [128, 512], BF16) for _ in range(2)]
                hl = pr.sb("hl", [128, 512], F32)
                lnb = pr.sb("lnb", [128, 512], F32)
                tred = pr.sb("tred", [128, 2], F32)
                scg = pr.sb("scg", [128, NTS], F32)
                scu = pr.sb("scu", [128, NTS], F32)
                lnr = pr.sb("lnr", [1, INTER0], F32)
                pr.dma("gpsimd", lambda eng: eng.dma_start(out=lnr[0][:, :],
                                                           in_=lnw_row),
                       writes=[lnr[1]])
                pr.op("vector", lambda eng: eng.memset(hstat[0][:, 0:2 * NTS], 0.0),
                      writes=[hstat[1]])
                pr.op("vector", lambda eng: eng.tensor_scalar(
                    scg[0][:, :], isx_keep[0][:, :], wm_keep[0][:, 0:1], 0.0,
                    ALU.mult, ALU.add), reads=[isx_keep[1], wm_keep[1]],
                    writes=[scg[1]])
                pr.op("vector", lambda eng: eng.tensor_scalar(
                    scu[0][:, :], isx_keep[0][:, :], wm_keep[0][:, 1:2], 0.0,
                    ALU.mult, ALU.add), reads=[isx_keep[1], wm_keep[1]],
                    writes=[scu[1]])
                for jc in range(NJC0):
                    pr.op("tensor", lambda eng, jc=jc: eng.matmul(
                        lnps[0][:, :], onesf[0][:, :],
                        lnr[0][0:1, jc * 512:(jc + 1) * 512], start=True, stop=True),
                        reads=[onesf[1], lnr[1]], writes=[lnps[1]])
                    pr.op("scalar", lambda eng: eng.copy(
                        lnb[0][:, :], lnps[0][:, :]), reads=[lnps[1]], writes=[lnb[1]])
                    for c in range(NCH):
                        pr.dma("sync", lambda eng, jc=jc, c=c: eng.dma_start(
                            out=gt[c][0][:, :],
                            in_=tern_g[jc * 512:(jc + 1) * 512, c * 128:(c + 1) * 128],
                            transpose=True), reads=[BC["tern_g"]], writes=[gt[c][1]])
                        pr.dma("sync", lambda eng, jc=jc, c=c: eng.dma_start(
                            out=ut[c][0][:, :],
                            in_=tern_u[jc * 512:(jc + 1) * 512, c * 128:(c + 1) * 128],
                            transpose=True), reads=[BC["tern_u"]], writes=[ut[c][1]])
                    for ts in range(NTS):
                        for c in range(NCH):
                            pr.op("tensor", lambda eng, c=c, ts=ts: eng.matmul(
                                gps[0][:, :], A(c, ts * 128, 128), gt[c][0][:, :],
                                start=(c == 0), stop=(c == NCH - 1)),
                                reads=[arena[1], gt[c][1]], writes=[gps[1]])
                        for c in range(NCH):
                            pr.op("tensor", lambda eng, c=c, ts=ts: eng.matmul(
                                ups[0][:, :], A(c, ts * 128, 128), ut[c][0][:, :],
                                start=(c == 0), stop=(c == NCH - 1)),
                                reads=[arena[1], ut[c][1]], writes=[ups[1]])
                        h = hb[ts % 2]
                        pr.op("scalar", lambda eng, ts=ts: eng.activation(
                            sgs[0][:, :], gps[0][:, :], ACT.Silu, bias=0.0,
                            scale=scg[0][:, ts:ts + 1]),
                            reads=[gps[1], scg[1]], writes=[sgs[1]])
                        pr.op("vector", lambda eng, ts=ts, h=h: eng.scalar_tensor_tensor(
                            h[0][:, :], sgs[0][:, :], scu[0][:, ts:ts + 1],
                            ups[0][:, :], ALU.mult, ALU.mult),
                            reads=[sgs[1], scu[1], ups[1]], writes=[h[1]])
                        pr.op("vector", lambda eng, h=h: eng.tensor_tensor(
                            hl[0][:, :], h[0][:, :], lnb[0][:, :], ALU.mult),
                            reads=[h[1], lnb[1]], writes=[hl[1]])
                        pr.op("vector", lambda eng: eng.reduce_max(
                            tred[0][:, 0:1], hl[0][:, :], axis=AX.X,
                            apply_absolute_value=True), reads=[hl[1]], writes=[tred[1]])
                        pr.op("vector", lambda eng, ts=ts: eng.tensor_tensor(
                            hstat[0][:, ts:ts + 1], hstat[0][:, ts:ts + 1],
                            tred[0][:, 0:1], ALU.max),
                            reads=[hstat[1], tred[1]], writes=[hstat[1]])
                        pr.op("scalar", lambda eng, h=h: eng.activation(
                            hl[0][:, :], h[0][:, :], ACT.Square, bias=0.0, scale=1.0,
                            accum_out=tred[0][:, 1:2]),
                            reads=[h[1]], writes=[hl[1], tred[1]])
                        pr.op("vector", lambda eng, ts=ts: eng.tensor_tensor(
                            hstat[0][:, NTS + ts:NTS + ts + 1],
                            hstat[0][:, NTS + ts:NTS + ts + 1],
                            tred[0][:, 1:2], ALU.add),
                            reads=[hstat[1], tred[1]], writes=[hstat[1]])
                        pr.dma("gpsimd", lambda eng, ts=ts, jc=jc, h=h: eng.dma_start(
                            out=h_spill[ts * 128:(ts + 1) * 128,
                                        jc * 512:(jc + 1) * 512], in_=h[0][:, :]),
                            reads=[h[1]], writes=[BC["h_spill"]])
                pr.op("scalar", lambda eng: eng.activation(
                    hstat[0][:, 5 * NTS:6 * NTS], hstat[0][:, NTS:2 * NTS],
                    ACT.Sqrt, bias=EPS_RMS, scale=1.0 / INTER0),
                    reads=[hstat[1]], writes=[hstat[1]])
                pr.op("vector", lambda eng: eng.reciprocal(
                    hstat[0][:, 2 * NTS:3 * NTS], hstat[0][:, 5 * NTS:6 * NTS]),
                    reads=[hstat[1]], writes=[hstat[1]])
                pr.op("vector", lambda eng: eng.tensor_tensor(
                    hstat[0][:, 0:NTS], hstat[0][:, 0:NTS],
                    hstat[0][:, 2 * NTS:3 * NTS], ALU.mult),
                    reads=[hstat[1]], writes=[hstat[1]])
                pr.op("vector", lambda eng: eng.tensor_single_scalar(
                    hstat[0][:, 0:NTS], hstat[0][:, 0:NTS], EPS_Q, ALU.max),
                    reads=[hstat[1]], writes=[hstat[1]])
                pr.op("vector", lambda eng: eng.tensor_scalar(
                    hstat[0][:, 3 * NTS:4 * NTS], hstat[0][:, 0:NTS],
                    wm_keep[0][:, 2:3], 1.0 / 127.0, ALU.mult, ALU.mult),
                    reads=[hstat[1], wm_keep[1]], writes=[hstat[1]])
                pr.op("vector", lambda eng: eng.reciprocal(
                    hstat[0][:, 4 * NTS:5 * NTS], hstat[0][:, 0:NTS]),
                    reads=[hstat[1]], writes=[hstat[1]])
                pr.op("vector", lambda eng: eng.tensor_single_scalar(
                    hstat[0][:, 4 * NTS:5 * NTS], hstat[0][:, 4 * NTS:5 * NTS],
                    127.0, ALU.mult), reads=[hstat[1]], writes=[hstat[1]])
                pr.op("vector", lambda eng: eng.tensor_tensor(
                    hstat[0][:, 2 * NTS:3 * NTS], hstat[0][:, 2 * NTS:3 * NTS],
                    hstat[0][:, 4 * NTS:5 * NTS], ALU.mult),
                    reads=[hstat[1]], writes=[hstat[1]])
                ctps = pr.ps("ctps", [1, 128], F32)
                for ts in range(NTS):
                    pr.op("tensor", lambda eng, ts=ts: eng.transpose(
                        ctps[0][:, :], hstat[0][:, 2 * NTS + ts:2 * NTS + ts + 1],
                        identf[0][:, :]),
                        reads=[hstat[1], identf[1]], writes=[ctps[1]])
                    pr.op("scalar", lambda eng, ts=ts: eng.copy(
                        crows[0][0:1, ts * 128:(ts + 1) * 128], ctps[0][:, :]),
                        reads=[ctps[1]], writes=[crows[1]])

            with pr.phase("l0f_down"):
                wdt = [pr.sb("wdt", [128, 512], BF16) for _ in range(2)]
                hts = [pr.sb("hts", [128, 512], BF16) for _ in range(2)]
                v1 = pr.sb("v1", [128, 512], F32)
                r1 = pr.sb("r1", [128, 512], F32)
                r2 = pr.sb("r2", [128, 512], F32)
                cbc = pr.sb("cbc", [128, 512], F32)
                lncolt = pr.sb("lncolt", [128, NJ0], F32)
                oacc = pr.sb("oacc", [128, 512], F32)
                xres = [pr.sb("xres", [128, 512], F32) for _ in range(2)]
                dps = [pr.ps("dps", [128, 512], F32) for _ in range(4)]
                cps = pr.ps("cps", [128, 512], F32)
                pr.dma("gpsimd", lambda eng: eng.dma_start(
                    out=lncolt[0][:, :], in_=lnw_colt), writes=[lncolt[1]])
                for tc in range(NTC):
                    for i in range(4):
                        ts = tc * 4 + i
                        pr.op("tensor", lambda eng, ts=ts, i=i: eng.matmul(
                            cps[0][:, i * 128:(i + 1) * 128], onesf[0][:, :],
                            crows[0][0:1, ts * 128:(ts + 1) * 128],
                            start=True, stop=True),
                            reads=[onesf[1], crows[1]], writes=[cps[1]])
                    pr.op("scalar", lambda eng: eng.copy(
                        cbc[0][:, :], cps[0][:, :]), reads=[cps[1]], writes=[cbc[1]])
                    for j in range(NJ0):
                        ht = hts[j % 2]
                        pr.dma("sync", lambda eng, tc=tc, j=j, ht=ht: eng.dma_start(
                            out=ht[0][:, :],
                            in_=h_spill[tc * 512:(tc + 1) * 512,
                                        j * 128:(j + 1) * 128], transpose=True),
                            reads=[BC["h_spill"]], writes=[ht[1]])
                        pr.op("scalar", lambda eng, j=j, ht=ht: eng.activation(
                            v1[0][:, :], ht[0][:, :], ACT.Identity, bias=0.0,
                            scale=lncolt[0][:, j:j + 1]),
                            reads=[ht[1], lncolt[1]], writes=[v1[1]])
                        pr.op("vector", lambda eng: eng.tensor_tensor(
                            v1[0][:, :], v1[0][:, :], cbc[0][:, :], ALU.mult),
                            reads=[v1[1], cbc[1]], writes=[v1[1]])
                        emit_round(arena[0][:, j * 512:(j + 1) * 512], arena[1],
                                   v1[0][:, :], v1[1], r1, r2, 512)
                    for dc in range(NTC):
                        for j in range(NJ0):
                            w = wdt[j % 2]
                            pr.dma("sync", lambda eng, dc=dc, j=j, w=w: eng.dma_start(
                                out=w[0][:, :],
                                in_=tern_d[dc * 512:(dc + 1) * 512,
                                           j * 128:(j + 1) * 128], transpose=True),
                                reads=[BC["tern_d"]], writes=[w[1]])
                            for i in range(4):
                                pr.op("tensor", lambda eng, i=i, j=j, w=w: eng.matmul(
                                    dps[i][0][:, :],
                                    arena[0][:, j * 512 + i * 128: j * 512 + (i + 1) * 128],
                                    w[0][:, :],
                                    start=(j == 0), stop=(j == NJ0 - 1)),
                                    reads=[arena[1], w[1]], writes=[dps[i][1]])
                        for i in range(4):
                            ts = tc * 4 + i
                            xr = xres[i % 2]
                            pr.dma("sync", lambda eng, ts=ts, dc=dc, xr=xr: eng.dma_start(
                                out=xr[0][:, :],
                                in_=xA[ts * 128:(ts + 1) * 128,
                                       dc * 512:(dc + 1) * 512]),
                                reads=[BC["xA"]], writes=[xr[1]])
                            pr.op("scalar", lambda eng, i=i, ts=ts: eng.activation(
                                oacc[0][:, :], dps[i][0][:, :], ACT.Identity,
                                bias=0.0, scale=hstat[0][:, 3 * NTS + ts:3 * NTS + ts + 1]),
                                reads=[dps[i][1], hstat[1]], writes=[oacc[1]])
                            pr.op("vector", lambda eng, xr=xr: eng.tensor_tensor(
                                oacc[0][:, :], oacc[0][:, :], xr[0][:, :], ALU.add),
                                reads=[oacc[1], xr[1]], writes=[oacc[1]])
                            pr.dma("sync", lambda eng, ts=ts, dc=dc: eng.dma_start(
                                out=xB[ts * 128:(ts + 1) * 128,
                                       dc * 512:(dc + 1) * 512], in_=oacc[0][:, :]),
                                reads=[oacc[1]], writes=[BC["xB"]])


        def emit_l1f():
            with pr.phase("l1f_combine"):
                cols1 = make_cols(False)
                ptps = pr.ps("ptp", [128, 128], BF16)
                emit_ln_to_fm(xC, BC["xC"], False, cols1, ptps)
                gwb = pr.sb("gwb2", [15, 2048], BF16)
                gwt = pr.sb("gwt2", [128, NCH * 15], BF16)
                gtp = pr.ps("gtp2", [128, 15], BF16)
                lps = pr.ps("lps2", [128, 15], F32)
                lg = pr.sb("lg2", [128, 15], F32)
                c1 = pr.sb("c1_2", [128, 15], F32)
                mskd = pr.sb("mskd2", [128, 15], F32)
                pcol = pr.sb("pcol2", [128, 4], F32)
                pr.dma("gpsimd", lambda eng: eng.dma_start(out=gwb[0][:, :],
                                                           in_=gatew),
                       writes=[gwb[1]])
                for c in range(NCH):
                    pr.op("tensor", lambda eng, c=c: eng.transpose(
                        gtp[0][:, :], gwb[0][:, c * 128:(c + 1) * 128],
                        ident[0][0:15, 0:15]),
                        reads=[gwb[1], ident[1]], writes=[gtp[1]])
                    pr.op("scalar", lambda eng, c=c: eng.copy(
                        gwt[0][:, c * 15:(c + 1) * 15], gtp[0][:, :]),
                        reads=[gtp[1]], writes=[gwt[1]])
                pr.op("vector", lambda eng: eng.memset(
                    cmb[0][:, :], 1.0), writes=[cmb[1]])
                for ts in range(NTS):
                    for c in range(NCH):
                        pr.op("tensor", lambda eng, ts=ts, c=c: eng.matmul(
                            lps[0][:, :], A(c, ts * 128, 128),
                            gwt[0][:, c * 15:(c + 1) * 15],
                            start=(c == 0), stop=(c == NCH - 1)),
                            reads=[arena[1], gwt[1]], writes=[lps[1]])
                    pr.op("vector", lambda eng: eng.reduce_max(
                        pcol[0][:, 0:1], lps[0][:, :], axis=AX.X),
                        reads=[lps[1]], writes=[pcol[1]])
                    pr.op("vector", lambda eng: eng.tensor_single_scalar(
                        pcol[0][:, 1:2], pcol[0][:, 0:1], -1.0, ALU.mult),
                        reads=[pcol[1]], writes=[pcol[1]])
                    pr.op("scalar", lambda eng: eng.activation(
                        lg[0][:, :], lps[0][:, :], ACT.Exp,
                        bias=pcol[0][:, 1:2], scale=1.0, accum_out=pcol[0][:, 2:3]),
                        reads=[lps[1], pcol[1]], writes=[lg[1], pcol[1]])
                    pr.op("vector", lambda eng: eng.reciprocal(
                        pcol[0][:, 3:4], pcol[0][:, 2:3]),
                        reads=[pcol[1]], writes=[pcol[1]])
                    pr.op("scalar", lambda eng: eng.activation(
                        lg[0][:, :], lg[0][:, :], ACT.Identity, bias=0.0,
                        scale=pcol[0][:, 3:4]), reads=[lg[1], pcol[1]], writes=[lg[1]])
                    pr.op("vector", lambda eng: eng.reduce_max(
                        pcol[0][:, 0:1], lg[0][:, :], axis=AX.X),
                        reads=[lg[1]], writes=[pcol[1]])
                    pr.op("vector", lambda eng: eng.tensor_scalar(
                        c1[0][:, :], lg[0][:, :], pcol[0][:, 0:1], NEG,
                        ALU.is_ge, ALU.mult), reads=[lg[1], pcol[1]], writes=[c1[1]])
                    pr.op("vector", lambda eng: eng.tensor_tensor(
                        mskd[0][:, :], lg[0][:, :], c1[0][:, :], ALU.add),
                        reads=[lg[1], c1[1]], writes=[mskd[1]])
                    pr.op("vector", lambda eng: eng.reduce_max(
                        pcol[0][:, 1:2], mskd[0][:, :], axis=AX.X),
                        reads=[mskd[1]], writes=[pcol[1]])
                    pr.op("vector", lambda eng: eng.tensor_scalar(
                        c1[0][:, :], lg[0][:, :], pcol[0][:, 1:2], 0.0,
                        ALU.is_ge, ALU.add), reads=[lg[1], pcol[1]], writes=[c1[1]])
                    pr.op("vector", lambda eng, ts=ts: eng.tensor_tensor(
                        cmb[0][:, ts * 16: ts * 16 + 15], c1[0][:, :],
                        lg[0][:, :], ALU.mult), reads=[c1[1], lg[1]], writes=[cmb[1]])

            with pr.phase("l1f_moe"):
                NQ = 4
                TQ = S // NQ
                NTSQ = TQ // 128
                yacc = pr.sb("yacc", [128, NTSQ * DM], F32)
                w13t = [pr.sb("w13t", [128, INTER1], BF16) for _ in range(NCH)]
                sgst = pr.sb("sgst", [128, NTSQ * INTER1], BF16)
                hhst = pr.sb("hhst", [128, NTSQ * INTER1], BF16)
                hfm = pr.sb("hfm", [128, NM1 * TQ], BF16)
                w2t = [pr.sb("w2t", [128, 512], BF16) for _ in range(2)]
                hps = pr.ps("hps", [128, 512], F32)
                ptps = pr.ps("ptp", [128, 128], BF16)
                dps = [pr.ps("dps", [128, 512], F32) for _ in range(4)]
                xres = [pr.sb("xres", [128, 512], F32)]
                oacc = pr.sb("oacc", [128, 512], F32)
                exps = [(w1_bf[i], w3_bf[i], w2_bf[i], BC["w1_bf"], BC["w3_bf"],
                         BC["w2_bf"], i) for i in range(NEXP)]
                exps.append((sw1_bf, sw3_bf, sw2_bf, BC["sw1_bf"], BC["sw3_bf"],
                             BC["sw2_bf"], 15))
                for q in range(NQ):
                    pr.op("vector", lambda eng: eng.memset(yacc[0][:, :], 0.0),
                          writes=[yacc[1]])
                    for (e1, e3, e2, b1, b3, b2, ei) in exps:
                        for c in range(NCH):
                            pr.dma("sync", lambda eng, c=c, e1=e1: eng.dma_start(
                                out=w13t[c][0][:, :],
                                in_=e1[:, c * 128:(c + 1) * 128], transpose=True),
                                reads=[b1], writes=[w13t[c][1]])
                        for i in range(NTSQ):
                            ts = q * NTSQ + i
                            off = 0
                            for jw in L1JC:
                                for c in range(NCH):
                                    pr.op("tensor", lambda eng, c=c, ts=ts, off=off, jw=jw: eng.matmul(
                                        hps[0][:, 0:jw], A(c, ts * 128, 128),
                                        w13t[c][0][:, off:off + jw],
                                        start=(c == 0), stop=(c == NCH - 1)),
                                        reads=[arena[1], w13t[c][1]], writes=[hps[1]])
                                pr.op("scalar", lambda eng, i=i, off=off, jw=jw: eng.activation(
                                    sgst[0][:, i * INTER1 + off: i * INTER1 + off + jw],
                                    hps[0][:, 0:jw], ACT.Silu, bias=0.0, scale=1.0),
                                    reads=[hps[1]], writes=[sgst[1]])
                                off += jw
                        for c in range(NCH):
                            pr.dma("sync", lambda eng, c=c, e3=e3: eng.dma_start(
                                out=w13t[c][0][:, :],
                                in_=e3[:, c * 128:(c + 1) * 128], transpose=True),
                                reads=[b3], writes=[w13t[c][1]])
                        for i in range(NTSQ):
                            ts = q * NTSQ + i
                            off = 0
                            for jw in L1JC:
                                for c in range(NCH):
                                    pr.op("tensor", lambda eng, c=c, ts=ts, off=off, jw=jw: eng.matmul(
                                        hps[0][:, 0:jw], A(c, ts * 128, 128),
                                        w13t[c][0][:, off:off + jw],
                                        start=(c == 0), stop=(c == NCH - 1)),
                                        reads=[arena[1], w13t[c][1]], writes=[hps[1]])
                                pr.op("vector", lambda eng, i=i, ts=ts, off=off, jw=jw, ei=ei: eng.scalar_tensor_tensor(
                                    hhst[0][:, i * INTER1 + off: i * INTER1 + off + jw],
                                    sgst[0][:, i * INTER1 + off: i * INTER1 + off + jw],
                                    cmb[0][:, ts * 16 + ei: ts * 16 + ei + 1],
                                    hps[0][:, 0:jw], ALU.mult, ALU.mult),
                                    reads=[sgst[1], cmb[1], hps[1]], writes=[hhst[1]])
                                off += jw
                        for i in range(NTSQ):
                            for m in range(NM1):
                                emit_tr128(
                                    ptps,
                                    hfm[0][:, m * TQ + i * 128: m * TQ + (i + 1) * 128],
                                    hfm[1],
                                    hhst[0][:, i * INTER1 + m * 128:
                                            i * INTER1 + (m + 1) * 128], hhst[1])
                        for dc in range(NTC):
                            for m in range(NM1):
                                w = w2t[m % 2]
                                pr.dma("sync", lambda eng, dc=dc, m=m, w=w, e2=e2: eng.dma_start(
                                    out=w[0][:, :],
                                    in_=e2[dc * 512:(dc + 1) * 512,
                                           m * 128:(m + 1) * 128], transpose=True),
                                    reads=[b2], writes=[w[1]])
                                for i in range(NTSQ):
                                    pr.op("tensor", lambda eng, i=i, m=m, w=w: eng.matmul(
                                        dps[i][0][:, :],
                                        hfm[0][:, m * TQ + i * 128: m * TQ + (i + 1) * 128],
                                        w[0][:, :],
                                        start=(m == 0), stop=(m == NM1 - 1)),
                                        reads=[hfm[1], w[1]], writes=[dps[i][1]])
                            for i in range(NTSQ):
                                pr.op("vector", lambda eng, i=i, dc=dc: eng.tensor_tensor(
                                    yacc[0][:, i * DM + dc * 512: i * DM + (dc + 1) * 512],
                                    yacc[0][:, i * DM + dc * 512: i * DM + (dc + 1) * 512],
                                    dps[i][0][:, :], ALU.add),
                                    reads=[yacc[1], dps[i][1]], writes=[yacc[1]])
                    for i in range(NTSQ):
                        ts = q * NTSQ + i
                        for dc in range(NTC):
                            xr = xres[0]
                            pr.dma("sync", lambda eng, ts=ts, dc=dc, xr=xr: eng.dma_start(
                                out=xr[0][:, :],
                                in_=xC[ts * 128:(ts + 1) * 128,
                                       dc * 512:(dc + 1) * 512]),
                                reads=[BC["xC"]], writes=[xr[1]])
                            pr.op("vector", lambda eng, i=i, dc=dc, xr=xr: eng.tensor_tensor(
                                oacc[0][:, :],
                                yacc[0][:, i * DM + dc * 512: i * DM + (dc + 1) * 512],
                                xr[0][:, :], ALU.add),
                                reads=[yacc[1], xr[1]], writes=[oacc[1]])
                            pr.dma("sync", lambda eng, ts=ts, dc=dc: eng.dma_start(
                                out=out[ts * 128:(ts + 1) * 128,
                                        dc * 512:(dc + 1) * 512], in_=oacc[0][:, :]),
                                reads=[oacc[1]], writes=[BC["out"]])


        if STAGES >= 3:
            emit_l0f()
        if STAGES >= 4:
            attention_block("l1a", xB, BC["xB"], xC, BC["xC"], False)
        if STAGES >= 5:
            emit_l1f()

        with pr.phase("finish"):
            pr.barrier_wait_all("sync")

    build_program.trace = pr.trace
    return nc, pr.n_instr


def _host_consts():
    ident = np.eye(128, dtype=BF)
    identf = np.eye(128, dtype=np.float32)
    P = np.zeros((128, 128), dtype=np.float32)
    for d in range(64):
        P[d, d + 64] = -1.0
        P[64 + d, d] = 1.0
    ppermt = np.ascontiguousarray(P.T).astype(BF)
    masks = np.zeros((4 * 128, 512), dtype=np.float32)
    for m in range(4):
        p = np.arange(128)[:, None]
        f = np.arange(512)[None, :]
        masks[m * 128:(m + 1) * 128] = np.where(f > p + 128 * m, NEG, 0.0)
    masks = masks.astype(BF)
    ones = np.ones((1, 128), dtype=np.float32)
    return ident, identf, ppermt, masks, ones


_CACHE = {}


def kernel(L0_attn_x, L0_attn_Wq, L0_attn_Wk, L0_attn_Wv, L0_attn_Wo,
           L0_ffn_w_gate, L0_ffn_w_up, L0_ffn_w_down, L0_ffn_ln_weight,
           L1_attn_W_lat, L1_attn_b_lat, L1_attn_W_d, L1_attn_b_d,
           L1_attn_W_proj, L1_attn_b_proj, L1_ffn_gate_w, L1_ffn_w1,
           L1_ffn_w2, L1_ffn_w3, L1_ffn_sw1, L1_ffn_sw2, L1_ffn_sw3):
    if "nc" not in _CACHE:
        _CACHE["nc"], n = build_program()
        _CACHE["n_instr"] = n
    nc = _CACHE["nc"]

    ident, identf, ppermt, masks, ones = _host_consts()
    f32 = lambda a: np.ascontiguousarray(a, dtype=np.float32)
    shared = dict(
        wq=f32(L0_attn_Wq), wk=f32(L0_attn_Wk), wv=f32(L0_attn_Wv),
        wo=f32(L0_attn_Wo), wgate=f32(L0_ffn_w_gate), wup=f32(L0_ffn_w_up),
        wdown=f32(L0_ffn_w_down),
        lnw_row=f32(L0_ffn_ln_weight).reshape(1, INTER0),
        lnw_colt=np.ascontiguousarray(
            f32(L0_ffn_ln_weight).reshape(NJ0, 128).T),
        wlat=f32(L1_attn_W_lat), blat=f32(L1_attn_b_lat).reshape(128, 1),
        wd=f32(L1_attn_W_d),
        bdh=np.ascontiguousarray(f32(L1_attn_b_d).reshape(16, 128).T),
        wproj=f32(L1_attn_W_proj),
        bproj_row=f32(L1_attn_b_proj).reshape(1, DM),
        gatew=f32(L1_ffn_gate_w), w1=f32(L1_ffn_w1), w2=f32(L1_ffn_w2),
        w3=f32(L1_ffn_w3), sw1=f32(L1_ffn_sw1), sw2=f32(L1_ffn_sw2),
        sw3=f32(L1_ffn_sw3),
        ident_c=ident, identf_c=identf, ppermt_c=ppermt, masks_c=masks,
        ones_c=ones,
        csts_c=np.tile(np.array([0.25, math.pi / 2.0, 128.0, EPS_LN, EPS_RMS,
                                 1.0, 0.0, 0.0], np.float32), (128, 1)),
    )
    x = f32(L0_attn_x)
    in_maps = [dict(shared, x_in=np.ascontiguousarray(x[b]))
               for b in range(x.shape[0])]
    res = run_bass_kernel_spmd(nc, in_maps, core_ids=list(range(8)))
    outs = [np.asarray(r["out"], dtype=np.float32) for r in res.results]
    return np.stack(outs, axis=0)


# version 08 (final)
# speedup vs baseline: 1.0451x; 1.0267x over previous
import contextlib
import math
import os

os.environ.setdefault("GAUGE_NTFF_IGNORE_DMA", "1")

import numpy as np
import ml_dtypes

import concourse.bass as bass
import concourse.mybir as mybir
from concourse.bass_utils import run_bass_kernel_spmd

F32 = mybir.dt.float32
BF16 = mybir.dt.bfloat16
ALU = mybir.AluOpType
ACT = mybir.ActivationFunctionType
AX = mybir.AxisListType
BF = ml_dtypes.bfloat16

S = 2048
DM = 2048
NCH = DM // 128
NTS = S // 128
NTC = S // 512
INTER0 = 8192
NJC0 = INTER0 // 512
NJ0 = INTER0 // 128
INTER1 = 1408
NEXP = 15
NM1 = INTER1 // 128
L1JC = (512, 512, 384)
EPS_LN = 1e-5
EPS_RMS = 1e-6
EPS_Q = 1e-5
NEG = -1e30
ISQ = 1.0 / math.sqrt(128.0)

DBG = False

ENGINES = ("tensor", "vector", "scalar", "gpsimd", "sync")
DMA_QUEUES = ("sync", "scalar", "gpsimd")
STRICT_SYNC = os.environ.get("BASS_SIM_FP64") == "1"


class Buf:
    __slots__ = ("ready", "readers", "name")

    def __init__(self, name=""):
        self.ready = {}
        self.readers = {}
        self.name = name


class Prog:
    def __init__(self, nc, gstack):
        self.nc = nc
        self.gstack = gstack
        self.csem, self.ccount = {}, {}
        self.dsem, self.dcount = {}, {}
        for e in ENGINES:
            self.csem[e] = gstack.enter_context(nc.semaphore(f"cs_{e}"))
            self.ccount[e] = 0
        self.DMA_WAYS = 4
        for q in DMA_QUEUES:
            self.dsem[q] = [gstack.enter_context(nc.semaphore(f"ds_{q}{i}"))
                            for i in range(self.DMA_WAYS)]
            self.dcount[q] = [0] * self.DMA_WAYS
            self.dissue = getattr(self, "dissue", {})
            self.dissue[q] = 0
        self.waited = {e: {} for e in ENGINES}
        self.cur = None
        self.local_stack = None
        self.n_names = 0
        self.n_instr = 0
        self.trace = []
        self._barrier = {}
        self._barrier_done = set()

    def uname(self, base):
        self.n_names += 1
        return f"{base}_{self.n_names}"

    def sb(self, base, shape, dt, scope="local"):
        stack = self.local_stack if scope == "local" else self.gstack
        h = stack.enter_context(self.nc.sbuf_tensor(self.uname(base), shape, dt))
        return (h, Buf(base))

    def sb_on(self, stack, base, shape, dt):
        h = stack.enter_context(self.nc.sbuf_tensor(self.uname(base), shape, dt))
        return (h, Buf(base))

    def ps(self, base, shape, dt):
        h = self.local_stack.enter_context(
            self.nc.psum_tensor(self.uname(base), shape, dt))
        return (h, Buf(base))

    @contextlib.contextmanager
    def phase(self, name):
        snap = {}
        for e in ENGINES:
            if self.ccount[e]:
                snap[self.csem[e]] = self.ccount[e]
        for q in DMA_QUEUES:
            for i in range(self.DMA_WAYS):
                if self.dcount[q][i]:
                    snap[self.dsem[q][i]] = 16 * self.dcount[q][i]
        self._barrier = snap
        self._barrier_done = set()
        self.cur = {e: [] for e in ENGINES}
        with contextlib.ExitStack() as stack:
            self.local_stack = stack
            yield
            qs = self.cur
            self.cur = None
            self.local_stack = None
            with self.nc.Block() as blk:
                for e in ENGINES:
                    ops = qs[e]
                    if not ops:
                        continue

                    def body(eng, ops=ops):
                        for f in ops:
                            f(eng)
                    getattr(blk, e)(body)

    def _waits(self, engine, reads, writes):
        own = self.csem[engine]
        wt = self.waited[engine]
        need = {}
        if engine not in self._barrier_done:
            self._barrier_done.add(engine)
            for s, v in self._barrier.items():
                if s is not own and wt.get(s, 0) < v:
                    need[s] = v

        def add(sem, val, skip_own=False):
            if sem is own and skip_own:
                return
            if wt.get(sem, 0) >= val:
                return
            if need.get(sem, 0) < val:
                need[sem] = val

        for b in reads:
            for s, v in b.ready.items():
                add(s, v)
        for b in writes:
            for s, v in b.ready.items():
                add(s, v, skip_own=True)
            for s, v in b.readers.items():
                add(s, v)
        q = self.cur[engine]
        for s, v in need.items():
            self.trace.append(f"{engine}: WAIT {s.name if hasattr(s,'name') else s} >= {v}")
            q.append(lambda eng, s=s, v=v: eng.wait_ge(s, v))
            wt[s] = max(wt.get(s, 0), v)

    const_dep = None

    def op(self, engine, fn, reads=(), writes=()):
        if engine == "scalar" and self.const_dep is not None:
            reads = list(reads) + [self.const_dep]
        self._waits(engine, reads, writes)
        sem = self.csem[engine]
        self.ccount[engine] += 1
        cnt = self.ccount[engine]
        self.n_instr += 1
        self.trace.append(
            f"{engine}: OP#{cnt} r=[{','.join(b.name for b in reads)}] "
            f"w=[{','.join(b.name for b in writes)}]")
        self.cur[engine].append(
            lambda eng, fn=fn, sem=sem: fn(eng).then_inc(sem, 1))
        for b in writes:
            b.ready[sem] = cnt
        for b in reads:
            b.readers[sem] = cnt

    def dma(self, queue, fn, reads=(), writes=()):
        way = self.dissue[queue] % self.DMA_WAYS
        self.dissue[queue] += 1
        dsem = self.dsem[queue][way]
        prev = 16 * self.dcount[queue][way]
        self.dcount[queue][way] += 1
        val = prev + 16
        wt = self.waited[queue]
        if prev and wt.get(dsem, 0) < prev:
            self.cur[queue].append(
                lambda eng, s=dsem, v=prev: eng.wait_ge(s, v))
            wt[dsem] = prev
        self._waits(queue, reads, writes)
        self.n_instr += 1
        self.trace.append(
            f"{queue}: DMA#{way}:{val} r=[{','.join(b.name for b in reads)}] "
            f"w=[{','.join(b.name for b in writes)}]")
        self.cur[queue].append(
            lambda eng, fn=fn, dsem=dsem: fn(eng).then_inc(dsem, 16))
        for b in writes:
            b.ready[dsem] = val
        for b in reads:
            b.readers[dsem] = val

    def barrier_wait_all(self, engine):
        q = self.cur[engine]
        for e in ENGINES:
            if e == engine:
                continue
            s, v = self.csem[e], self.ccount[e]
            if v:
                q.append(lambda eng, s=s, v=v: eng.wait_ge(s, v))
        for qq in DMA_QUEUES:
            for i in range(self.DMA_WAYS):
                s, v = self.dsem[qq][i], 16 * self.dcount[qq][i]
                if v:
                    q.append(lambda eng, s=s, v=v: eng.wait_ge(s, v))


def build_program():
    nc = bass.Bass("TRN2", target_bir_lowering=False, debug=False)

    def dram(name, shape, dt, kind):
        return nc.dram_tensor(name, shape, dt, kind=kind).ap()

    EI, EO, IN = "ExternalInput", "ExternalOutput", "Internal"

    x_in = dram("x_in", [S, DM], F32, EI)
    wq = dram("wq", [DM, DM], F32, EI)
    wk = dram("wk", [DM, 512], F32, EI)
    wv = dram("wv", [DM, 512], F32, EI)
    wo = dram("wo", [DM, DM], F32, EI)
    wgate = dram("wgate", [INTER0, DM], F32, EI)
    wup = dram("wup", [INTER0, DM], F32, EI)
    wdown = dram("wdown", [DM, INTER0], F32, EI)
    lnw_row = dram("lnw_row", [1, INTER0], F32, EI)
    lnw_colt = dram("lnw_colt", [128, NJ0], F32, EI)
    wlat = dram("wlat", [DM, 128], F32, EI)
    blat = dram("blat", [128, 1], F32, EI)
    wd = dram("wd", [DM, DM], F32, EI)
    bdh = dram("bdh", [128, 16], F32, EI)
    wproj = dram("wproj", [DM, DM], F32, EI)
    bproj_row = dram("bproj_row", [1, DM], F32, EI)
    gatew = dram("gatew", [NEXP, DM], F32, EI)
    w1 = dram("w1", [NEXP, INTER1, DM], F32, EI)
    w2 = dram("w2", [NEXP, DM, INTER1], F32, EI)
    w3 = dram("w3", [NEXP, INTER1, DM], F32, EI)
    sw1 = dram("sw1", [INTER1, DM], F32, EI)
    sw2 = dram("sw2", [DM, INTER1], F32, EI)
    sw3 = dram("sw3", [INTER1, DM], F32, EI)
    ident_c = dram("ident_c", [128, 128], BF16, EI)
    identf_c = dram("identf_c", [128, 128], F32, EI)
    ppermt_c = dram("ppermt_c", [128, 128], BF16, EI)
    masks_c = dram("masks_c", [4 * 128, 512], BF16, EI)
    ones_c = dram("ones_c", [1, 128], F32, EI)
    csts_c = dram("csts_c", [128, 8], F32, EI)

    out = dram("out", [S, DM], F32, EO)
    xA = dram("xA", [S, DM], F32, EO if DBG else IN)
    xB = dram("xB", [S, DM], F32, EO if DBG else IN)
    xC = dram("xC", [S, DM], F32, EO if DBG else IN)

    wq_bf = dram("wq_bf", [DM, DM], BF16, IN)
    wk_bf = dram("wk_bf", [DM, 512], BF16, IN)
    wv_bf = dram("wv_bf", [DM, 512], BF16, IN)
    wo_bf = dram("wo_bf", [DM, DM], BF16, IN)
    wlat_bf = dram("wlat_bf", [DM, 128], BF16, IN)
    wd_bf = dram("wd_bf", [DM, DM], BF16, IN)
    wproj_bf = dram("wproj_bf", [DM, DM], BF16, IN)
    w1_bf = dram("w1_bf", [NEXP, INTER1, DM], BF16, IN)
    w2_bf = dram("w2_bf", [NEXP, DM, INTER1], BF16, IN)
    w3_bf = dram("w3_bf", [NEXP, INTER1, DM], BF16, IN)
    sw1_bf = dram("sw1_bf", [INTER1, DM], BF16, IN)
    sw2_bf = dram("sw2_bf", [DM, INTER1], BF16, IN)
    sw3_bf = dram("sw3_bf", [INTER1, DM], BF16, IN)
    tern_g = dram("tern_g", [INTER0, DM], BF16, IN)
    tern_u = dram("tern_u", [INTER0, DM], BF16, IN)
    tern_d = dram("tern_d", [DM, INTER0], BF16, IN)
    h_spill = dram("h_spill", [S, INTER0], BF16, IN)

    gs = contextlib.ExitStack()
    pr = Prog(nc, gs)

    BC = {}
    for nm in ("wq_bf", "wk_bf", "wv_bf", "wo_bf", "wlat_bf", "wd_bf",
               "wproj_bf", "w1_bf", "w2_bf", "w3_bf", "sw1_bf", "sw2_bf",
               "sw3_bf", "tern_g", "tern_u", "tern_d", "h_spill",
               "xA", "xB", "xC", "out", "x_in", "wgate", "wup", "wdown"):
        BC[nm] = Buf(nm)

    CONSTS = [0.25, math.pi / 2.0, 128.0, EPS_LN, EPS_RMS, 1.0]
    with gs:
        csts = pr.sb("csts", [128, 8], F32, scope="global")
        for i, v in enumerate(CONSTS):
            nc.const_aps.register(F32, v, csts[0][:, i:i + 1])
        pr.const_dep = csts[1]
        ident = pr.sb("ident", [128, 128], BF16, scope="global")
        identf = pr.sb("identf", [128, 128], F32, scope="global")
        ppermt = pr.sb("ppermt", [128, 128], BF16, scope="global")
        onesf = pr.sb("onesf", [1, 128], F32, scope="global")
        cos_t = pr.sb("cos_t", [128, S], F32, scope="global")
        sin_t = pr.sb("sin_t", [128, S], F32, scope="global")
        arena = pr.sb("arena", [128, 32768], BF16, scope="global")
        wm_keep = pr.sb("wm_keep", [128, 3], F32, scope="global")
        sx_keep = pr.sb("sx_keep", [128, NTS], F32, scope="global")
        isx_keep = pr.sb("isx_keep", [128, NTS], F32, scope="global")
        hstat = pr.sb("hstat", [128, 6 * NTS], F32, scope="global")
        crows = pr.sb("crows", [1, S], F32, scope="global")
        cmb = pr.sb("cmb", [128, 16 * 16], F32, scope="global")

        def A(c, lo, n):
            return arena[0][:, c * S + lo: c * S + lo + n]

        def emit_round(dst_ap, dst_buf, src_ap, src_buf, scr1, scr2, width,
                       base=128.0, nbits=8):
            d_h, d_b = scr1
            c_h, c_b = scr2
            pr.op("vector", lambda eng: eng.tensor_single_scalar(
                d_h[:, 0:width], src_ap, base, ALU.add),
                reads=[src_buf], writes=[d_b])
            bit = base
            for _ in range(nbits):
                pr.op("vector", lambda eng, bit=bit: eng.tensor_single_scalar(
                    c_h[:, 0:width], d_h[:, 0:width], bit - 0.5, ALU.is_ge),
                    reads=[d_b], writes=[c_b])
                pr.op("vector", lambda eng, bit=bit: eng.scalar_tensor_tensor(
                    d_h[:, 0:width], c_h[:, 0:width], -bit, d_h[:, 0:width],
                    ALU.mult, ALU.add),
                    reads=[c_b, d_b], writes=[d_b])
                bit *= 0.5
            pr.op("vector", lambda eng: eng.scalar_tensor_tensor(
                dst_ap, d_h[:, 0:width], -1.0, src_ap, ALU.mult, ALU.add),
                reads=[d_b, src_buf], writes=[dst_buf])

        def emit_tr128(ptps, dst_ap, dst_buf, src_ap, src_buf,
                       evac="scalar", idt=None):
            pt_h, pt_b = ptps
            idt = idt or ident
            pr.op("tensor", lambda eng: eng.transpose(
                pt_h[:, :], src_ap, idt[0][:, :]),
                reads=[src_buf, idt[1]], writes=[pt_b])
            if evac == "scalar":
                pr.op("scalar", lambda eng: eng.copy(dst_ap, pt_h[:, :]),
                      reads=[pt_b], writes=[dst_buf])
            else:
                pr.op(evac, lambda eng: eng.tensor_copy(dst_ap, pt_h[:, :]),
                      reads=[pt_b], writes=[dst_buf])

        with pr.phase("setup"):
            pr.dma("gpsimd", lambda eng: eng.dma_start(
                out=csts[0][:, :], in_=csts_c), writes=[csts[1]])
            for (dst, src, b) in ((ident, ident_c, None), (identf, identf_c, None),
                                  (ppermt, ppermt_c, None), (onesf, ones_c, None)):
                pr.dma("gpsimd", lambda eng, dst=dst, src=src: eng.dma_start(
                    out=dst[0][:, :], in_=src), writes=[dst[1]])

            def cast(dst, src, dbuf, split=1):
                step = src.shape[0] // split
                for i in range(split):
                    pr.dma("gpsimd", lambda eng, i=i, dst=dst, src=src, step=step:
                           eng.dma_start(out=dst[i * step:(i + 1) * step],
                                         in_=src[i * step:(i + 1) * step],
                                         max_dma_last_dim=8192), writes=[dbuf])

            KCAST = int(os.environ.get("KCAST", "1"))
            if KCAST:
                cast(wq_bf, wq, BC["wq_bf"])
            if KCAST:
                cast(wk_bf, wk, BC["wk_bf"])
                cast(wv_bf, wv, BC["wv_bf"])
                cast(wo_bf, wo, BC["wo_bf"])
                cast(wlat_bf, wlat, BC["wlat_bf"])
                cast(wd_bf, wd, BC["wd_bf"])
                cast(wproj_bf, wproj, BC["wproj_bf"])
                for i in range(NEXP):
                    cast(w1_bf[i], w1[i], BC["w1_bf"])
                    cast(w3_bf[i], w3[i], BC["w3_bf"])
                    cast(w2_bf[i], w2[i], BC["w2_bf"])
                cast(sw1_bf, sw1, BC["sw1_bf"])
                cast(sw3_bf, sw3, BC["sw3_bf"])
                cast(sw2_bf, sw2, BC["sw2_bf"])

            dcol = pr.sb("dcol", [128, 2], F32)
            invf = pr.sb("invf", [128, 1], F32)
            freqs = pr.sb("freqs", [128, S], F32)
            ts1 = pr.sb("ts1", [128, S], F32)
            ts2 = pr.sb("ts2", [128, S], F32)
            ts3 = pr.sb("ts3", [128, S], F32)
            ts4 = pr.sb("ts4", [128, S], F32)
            pr.op("gpsimd", lambda eng: eng.iota(
                dcol[0][:, 0:1], [[0, 1]], channel_multiplier=1,
                allow_small_or_imprecise_dtypes=True), writes=[dcol[1]])
            pr.op("vector", lambda eng: eng.tensor_scalar(
                dcol[0][:, 1:2], dcol[0][:, 0:1], 64.0, 64.0, ALU.is_ge, ALU.mult),
                reads=[dcol[1]], writes=[dcol[1]])
            pr.op("vector", lambda eng: eng.tensor_tensor(
                dcol[0][:, 0:1], dcol[0][:, 0:1], dcol[0][:, 1:2], ALU.subtract),
                reads=[dcol[1]], writes=[dcol[1]])
            pr.op("scalar", lambda eng: eng.activation(
                invf[0][:, :], dcol[0][:, 0:1], ACT.Exp, bias=0.0,
                scale=-math.log(10000.0) / 64.0), reads=[dcol[1]], writes=[invf[1]])
            pr.op("gpsimd", lambda eng: eng.iota(
                ts1[0][:, :], [[1, S]], channel_multiplier=0,
                allow_small_or_imprecise_dtypes=True), writes=[ts1[1]])
            pr.op("scalar", lambda eng: eng.activation(
                freqs[0][:, :], ts1[0][:, :], ACT.Identity, bias=0.0,
                scale=invf[0][:, :]), reads=[ts1[1], invf[1]], writes=[freqs[1]])
            clampv = float(np.float32(3.1415925))
            twopi = 2.0 * math.pi

            def table(dst, shift):
                pr.op("scalar", lambda eng: eng.activation(
                    ts1[0][:, :], freqs[0][:, :], ACT.Identity,
                    bias=shift / twopi, scale=1.0 / twopi),
                    reads=[freqs[1]], writes=[ts1[1]])
                emit_round(ts2[0][:, :], ts2[1], ts1[0][:, :], ts1[1],
                           ts3, ts4, S, base=512.0, nbits=10)
                pr.op("scalar", lambda eng: eng.activation(
                    ts1[0][:, :], freqs[0][:, :], ACT.Identity,
                    bias=shift, scale=1.0), reads=[freqs[1]], writes=[ts1[1]])
                pr.op("vector", lambda eng: eng.scalar_tensor_tensor(
                    ts1[0][:, :], ts2[0][:, :], -twopi, ts1[0][:, :],
                    ALU.mult, ALU.add), reads=[ts2[1], ts1[1]], writes=[ts1[1]])
                pr.op("vector", lambda eng: eng.tensor_single_scalar(
                    ts1[0][:, :], ts1[0][:, :], clampv, ALU.min),
                    reads=[ts1[1]], writes=[ts1[1]])
                pr.op("vector", lambda eng: eng.tensor_single_scalar(
                    ts1[0][:, :], ts1[0][:, :], -clampv, ALU.max),
                    reads=[ts1[1]], writes=[ts1[1]])
                pr.op("scalar", lambda eng: eng.activation(
                    dst[0][:, :], ts1[0][:, :], ACT.Sin, bias=0.0, scale=1.0),
                    reads=[ts1[1]], writes=[dst[1]])

            if int(os.environ.get("KTAB", "1")):
                table(sin_t, 0.0)
                table(cos_t, math.pi / 2.0)

        TERN_ON = int(os.environ.get("KTERN", "1"))
        with pr.phase("ternall"):
          wt = [pr.sb("wt", [128, 2048], F32) for _ in range(2)]
          acc = pr.sb("acc", [128, 1], F32)
          tsum = pr.sb("tsum", [128, 2], F32)
          scol = pr.sb("scol", [128, 2], F32)
          sred = pr.sb("sred", [1, 2], F32)
          srps = pr.ps("srps", [1, 512], F32)
          brps = pr.ps("brps", [128, 512], F32)
          t1 = pr.sb("t1", [128, 2048], F32)
          t2 = pr.sb("t2", [128, 2048], F32)
          to = [pr.sb("to", [128, 2048], BF16) for _ in range(2)]
          for idx, (src, sbuf_, tdst, tbuf, nrow, ncol) in enumerate(() if not TERN_ON else (
                (wgate, BC["wgate"], tern_g, BC["tern_g"], INTER0, DM),
                (wup, BC["wup"], tern_u, BC["tern_u"], INTER0, DM),
                (wdown, BC["wdown"], tern_d, BC["tern_d"], DM, INTER0))):
            if True:
                nrt, nct = nrow // 128, ncol // 2048
                pr.op("vector", lambda eng: eng.memset(acc[0][:, :], 0.0),
                      writes=[acc[1]])
                for it in range(nrt * nct):
                    r, c = it % nrt, it // nrt
                    w = wt[it % 2]
                    pr.dma("scalar", lambda eng, r=r, c=c, w=w, src=src: eng.dma_start(
                        out=w[0][:, :],
                        in_=src[r * 128:(r + 1) * 128, c * 2048:(c + 1) * 2048]),
                        reads=[sbuf_], writes=[w[1]])
                    pr.op("vector", lambda eng, w=w: eng.reduce_sum(
                        tsum[0][:, 0:1], w[0][:, :], axis=AX.X,
                        apply_absolute_value=True), reads=[w[1]], writes=[tsum[1]])
                    pr.op("vector", lambda eng: eng.tensor_tensor(
                        acc[0][:, :], acc[0][:, :], tsum[0][:, 0:1], ALU.add),
                        reads=[acc[1], tsum[1]], writes=[acc[1]])
                pr.op("tensor", lambda eng: eng.matmul(
                    srps[0][:, 0:1], csts[0][:, 5:6], acc[0][:, :],
                    start=True, stop=True),
                    reads=[csts[1], acc[1]], writes=[srps[1]])
                pr.op("scalar", lambda eng: eng.copy(
                    sred[0][:, 0:1], srps[0][:, 0:1]),
                    reads=[srps[1]], writes=[sred[1]])
                pr.op("vector", lambda eng: eng.tensor_scalar(
                    sred[0][:, 1:2], sred[0][:, 0:1], 1.0 / (nrow * ncol),
                    EPS_Q, ALU.mult, ALU.max), reads=[sred[1]], writes=[sred[1]])
                pr.op("tensor", lambda eng: eng.matmul(
                    brps[0][:, 0:1], onesf[0][:, :], sred[0][0:1, 1:2],
                    start=True, stop=True),
                    reads=[onesf[1], sred[1]], writes=[brps[1]])
                pr.op("scalar", lambda eng, idx=idx: eng.copy(
                    wm_keep[0][:, idx:idx + 1], brps[0][:, 0:1]),
                    reads=[brps[1]], writes=[wm_keep[1]])
                pr.op("vector", lambda eng, idx=idx: eng.reciprocal(
                    scol[0][:, 0:1], wm_keep[0][:, idx:idx + 1]),
                    reads=[wm_keep[1]], writes=[scol[1]])
                pr.op("vector", lambda eng: eng.tensor_single_scalar(
                    scol[0][:, 1:2], scol[0][:, 0:1], -1.0, ALU.mult),
                    reads=[scol[1]], writes=[scol[1]])
                for it in range(nrt * nct):
                    r, c = it % nrt, it // nrt
                    w = wt[it % 2]
                    o = to[it % 2]
                    pr.dma("scalar", lambda eng, r=r, c=c, w=w, src=src: eng.dma_start(
                        out=w[0][:, :],
                        in_=src[r * 128:(r + 1) * 128, c * 2048:(c + 1) * 2048]),
                        reads=[sbuf_], writes=[w[1]])
                    pr.op("vector", lambda eng, w=w: eng.tensor_scalar(
                        t1[0][:, :], w[0][:, :], scol[0][:, 0:1], 0.5,
                        ALU.mult, ALU.is_gt), reads=[w[1], scol[1]], writes=[t1[1]])
                    pr.op("vector", lambda eng, w=w: eng.tensor_scalar(
                        t2[0][:, :], w[0][:, :], scol[0][:, 1:2], 0.5,
                        ALU.mult, ALU.is_gt), reads=[w[1], scol[1]], writes=[t2[1]])
                    pr.op("gpsimd", lambda eng, o=o: eng.tensor_tensor(
                        o[0][:, :], t1[0][:, :], t2[0][:, :], ALU.subtract),
                        reads=[t1[1], t2[1]], writes=[o[1]])
                    pr.dma("gpsimd", lambda eng, r=r, c=c, o=o, tdst=tdst: eng.dma_start(
                        out=tdst[r * 128:(r + 1) * 128, c * 2048:(c + 1) * 2048],
                        in_=o[0][:, :]), reads=[o[1]], writes=[tbuf])

        def make_cols(quant):
            cols = {}
            for nm in ("rstd", "mb", "t1", "t2"):
                cols[nm] = pr.sb("col_" + nm, [128, NTS], F32)
            if quant:
                for nm in ("mx", "sx", "isx"):
                    cols[nm] = pr.sb("col_" + nm, [128, NTS], F32)
            return cols

        def emit_ln_stats(x_ap, x_buf, sq, cols, ci):
            c = slice(ci, ci + 1)
            r, m, t1c, t2c = cols["rstd"], cols["mb"], cols["t1"], cols["t2"]
            pr.op("vector", lambda eng: eng.reduce_sum(
                t1c[0][:, c], x_ap, axis=AX.X), reads=[x_buf], writes=[t1c[1]])
            pr.op("scalar", lambda eng: eng.activation(
                sq[0][:, :], x_ap, ACT.Square, bias=0.0, scale=1.0,
                accum_out=t2c[0][:, c]), reads=[x_buf], writes=[sq[1], t2c[1]])
            pr.op("vector", lambda eng: eng.tensor_single_scalar(
                t1c[0][:, c], t1c[0][:, c], 1.0 / DM, ALU.mult),
                reads=[t1c[1]], writes=[t1c[1]])
            pr.op("vector", lambda eng: eng.tensor_tensor(
                m[0][:, c], t1c[0][:, c], t1c[0][:, c], ALU.mult),
                reads=[t1c[1]], writes=[m[1]])
            pr.op("vector", lambda eng: eng.scalar_tensor_tensor(
                t2c[0][:, c], t2c[0][:, c], 1.0 / DM, m[0][:, c],
                ALU.mult, ALU.subtract), reads=[t2c[1], m[1]], writes=[t2c[1]])
            pr.op("scalar", lambda eng: eng.activation(
                t2c[0][:, c], t2c[0][:, c], ACT.Sqrt, bias=EPS_LN, scale=1.0),
                reads=[t2c[1]], writes=[t2c[1]])
            pr.op("vector", lambda eng: eng.reciprocal(
                r[0][:, c], t2c[0][:, c]), reads=[t2c[1]], writes=[r[1]])
            pr.op("vector", lambda eng: eng.scalar_tensor_tensor(
                m[0][:, c], t1c[0][:, c], -1.0, r[0][:, c], ALU.mult, ALU.mult),
                reads=[t1c[1], r[1]], writes=[m[1]])

        def emit_ln_to_fm(xsrc, xsrc_buf, quant, cols, ptps):
            xt = [pr.sb("xt", [128, DM], F32) for _ in range(2)]
            sq = pr.sb("sq", [128, DM], F32)
            xl = pr.sb("xl", [128, DM], F32)
            xlb = pr.sb("xlb", [128, DM], BF16)
            r1 = pr.sb("r1", [128, DM], F32)
            r2 = pr.sb("r2", [128, DM], F32)
            for ts in range(NTS):
                x = xt[ts % 2]
                pr.dma("sync", lambda eng, ts=ts, x=x: eng.dma_start(
                    out=x[0][:, :], in_=xsrc[ts * 128:(ts + 1) * 128, :]),
                    reads=[xsrc_buf], writes=[x[1]])
                emit_ln_stats(x[0][:, :], x[1], sq, cols, ts)
                if not quant:
                    pr.op("scalar", lambda eng, ts=ts, x=x: eng.activation(
                        xlb[0][:, :], x[0][:, :], ACT.Identity,
                        bias=cols["mb"][0][:, ts:ts + 1],
                        scale=cols["rstd"][0][:, ts:ts + 1]),
                        reads=[x[1], cols["mb"][1], cols["rstd"][1]],
                        writes=[xlb[1]])
                else:
                    mx, sx, isx = cols["mx"], cols["sx"], cols["isx"]
                    pr.op("scalar", lambda eng, ts=ts, x=x: eng.activation(
                        xl[0][:, :], x[0][:, :], ACT.Identity,
                        bias=cols["mb"][0][:, ts:ts + 1],
                        scale=cols["rstd"][0][:, ts:ts + 1]),
                        reads=[x[1], cols["mb"][1], cols["rstd"][1]],
                        writes=[xl[1]])
                    pr.op("vector", lambda eng, ts=ts: eng.reduce_max(
                        mx[0][:, ts:ts + 1], xl[0][:, :], axis=AX.X,
                        apply_absolute_value=True), reads=[xl[1]], writes=[mx[1]])
                    pr.op("vector", lambda eng, ts=ts: eng.tensor_single_scalar(
                        mx[0][:, ts:ts + 1], mx[0][:, ts:ts + 1], EPS_Q, ALU.max),
                        reads=[mx[1]], writes=[mx[1]])
                    pr.op("vector", lambda eng, ts=ts: eng.tensor_single_scalar(
                        isx[0][:, ts:ts + 1], mx[0][:, ts:ts + 1], 1.0 / 127.0,
                        ALU.mult), reads=[mx[1]], writes=[isx[1]])
                    pr.op("vector", lambda eng, ts=ts: eng.reciprocal(
                        sx[0][:, ts:ts + 1], isx[0][:, ts:ts + 1]),
                        reads=[isx[1]], writes=[sx[1]])
                    pr.op("scalar", lambda eng, ts=ts: eng.activation(
                        xl[0][:, :], xl[0][:, :], ACT.Identity, bias=0.0,
                        scale=sx[0][:, ts:ts + 1]),
                        reads=[xl[1], sx[1]], writes=[xl[1]])
                    emit_round(xlb[0][:, :], xlb[1], xl[0][:, :], xl[1],
                               r1, r2, DM)
                for c in range(NCH):
                    emit_tr128(ptps, A(c, ts * 128, 128), arena[1],
                               xlb[0][:, c * 128:(c + 1) * 128], xlb[1])

        def attention_block(name, xsrc, xsrc_buf, xdst, xdst_buf, is_l0):
            NH = 16
            KSUB = int(os.environ.get("KSUB", "99"))
            KQC = int(os.environ.get("KQC", "99"))
            KH = int(os.environ.get("KH", "99"))
            if KSUB < 2:
                return
            with contextlib.ExitStack() as ost:
                if is_l0:
                    kst = pr.sb_on(ost, "kst", [128, 4 * S], BF16)
                    vst = pr.sb_on(ost, "vst", [128, NTS * 512], BF16)
                else:
                    kst = pr.sb_on(ost, "kst", [128, S], BF16)
                    vst = pr.sb_on(ost, "vst", [128, NTS * 128], BF16)
                    bpb = pr.sb_on(ost, "bpb", [128, DM], F32)
                with pr.phase(name + "_kv"):
                    cols = make_cols(False)
                    ptps = pr.ps("ptp", [128, 128], BF16)
                    if int(os.environ.get("KLN", "1")):
                        emit_ln_to_fm(xsrc, xsrc_buf, False, cols, ptps)
                    kps = pr.ps("kps", [128, 512], F32)
                    kps2 = pr.ps("kps2", [128, 512], F32)
                    krawb = pr.sb("krawb", [128, 512], BF16)
                    t1s = pr.sb("t1s", [128, 512], F32)
                    t2s = pr.sb("t2s", [128, 512], F32)
                    wct = [pr.sb("wct", [128, 512], BF16) for _ in range(NCH)]
                    KKV = int(os.environ.get("KKV", "99"))
                    if is_l0:
                        for c in range(NCH):
                            pr.dma("scalar", lambda eng, c=c: eng.dma_start(
                                out=wct[c][0][:, :],
                                in_=wk_bf[c * 128:(c + 1) * 128, :]),
                                reads=[BC["wk_bf"]], writes=[wct[c][1]])
                        KIT = int(os.environ.get("KIT", "99"))
                        KROPE = int(os.environ.get("KROPE", "1"))
                        it_count = 0
                        for kv in range(4 if KKV >= 2 else 0):
                            for tc in range(NTC):
                                it_count += 1
                                if it_count > KIT:
                                    continue
                                for c in range(NCH):
                                    pr.op("tensor", lambda eng, kv=kv, tc=tc, c=c: eng.matmul(
                                        kps[0][:, :],
                                        wct[c][0][:, kv * 128:(kv + 1) * 128],
                                        A(c, tc * 512, 512),
                                        start=(c == 0), stop=(c == NCH - 1)),
                                        reads=[wct[c][1], arena[1]],
                                        writes=[kps[1]] + ([kps2[1]] if c == 0 else []))
                                pr.op("scalar", lambda eng: eng.copy(
                                    krawb[0][:, :], kps[0][:, :]),
                                    reads=[kps[1]], writes=[krawb[1]])
                                pr.op("tensor", lambda eng: eng.matmul(
                                    kps2[0][:, :], ppermt[0][:, :], krawb[0][:, :],
                                    start=True, stop=True),
                                    reads=[ppermt[1], krawb[1]], writes=[kps2[1]])
                                pr.op("vector", lambda eng, tc=tc: eng.tensor_tensor(
                                    t1s[0][:, :], krawb[0][:, :],
                                    cos_t[0][:, tc * 512:(tc + 1) * 512], ALU.mult),
                                    reads=[krawb[1], cos_t[1]], writes=[t1s[1]])
                                pr.op("vector", lambda eng, tc=tc: eng.tensor_tensor(
                                    t2s[0][:, :], kps2[0][:, :],
                                    sin_t[0][:, tc * 512:(tc + 1) * 512], ALU.mult),
                                    reads=[kps2[1], sin_t[1]], writes=[t2s[1]])
                                pr.op("vector", lambda eng, kv=kv, tc=tc: eng.tensor_tensor(
                                    kst[0][:, kv * S + tc * 512: kv * S + (tc + 1) * 512],
                                    t1s[0][:, :], t2s[0][:, :], ALU.add),
                                    reads=[t1s[1], t2s[1]], writes=[kst[1]])
                        for c in range(NCH):
                            pr.dma("scalar", lambda eng, c=c: eng.dma_start(
                                out=wct[c][0][:, :],
                                in_=wv_bf[c * 128:(c + 1) * 128, :]),
                                reads=[BC["wv_bf"]], writes=[wct[c][1]])
                        for ts in range(NTS if KKV >= 3 else 0):
                            for c in range(NCH):
                                pr.op("tensor", lambda eng, ts=ts, c=c: eng.matmul(
                                    kps[0][:, :], A(c, ts * 128, 128),
                                    wct[c][0][:, :],
                                    start=(c == 0), stop=(c == NCH - 1)),
                                    reads=[arena[1], wct[c][1]], writes=[kps[1]])
                            pr.op("scalar", lambda eng, ts=ts: eng.copy(
                                vst[0][:, ts * 512:(ts + 1) * 512], kps[0][:, :]),
                                reads=[kps[1]], writes=[vst[1]])
                    else:
                        biascol0 = pr.sb("biascol0", [128, 1], F32)
                        pr.dma("gpsimd", lambda eng: eng.dma_start(
                            out=biascol0[0][:, :], in_=blat), writes=[biascol0[1]])
                        for c in range(NCH):
                            pr.dma("scalar", lambda eng, c=c: eng.dma_start(
                                out=wct[c][0][:, 0:128],
                                in_=wlat_bf[c * 128:(c + 1) * 128, :]),
                                reads=[BC["wlat_bf"]], writes=[wct[c][1]])
                        for tc in range(NTC):
                            for c in range(NCH):
                                pr.op("tensor", lambda eng, tc=tc, c=c: eng.matmul(
                                    kps[0][:, :], wct[c][0][:, 0:128],
                                    A(c, tc * 512, 512),
                                    start=(c == 0), stop=(c == NCH - 1)),
                                    reads=[wct[c][1], arena[1]], writes=[kps[1]])
                            pr.op("scalar", lambda eng, tc=tc: eng.activation(
                                kst[0][:, tc * 512:(tc + 1) * 512], kps[0][:, :],
                                ACT.Identity, bias=biascol0[0][:, 0:1], scale=1.0),
                                reads=[kps[1], biascol0[1]], writes=[kst[1]])
                        for ks in range(NTS):
                            emit_tr128(ptps, vst[0][:, ks * 128:(ks + 1) * 128],
                                       vst[1], kst[0][:, ks * 128:(ks + 1) * 128],
                                       kst[1])
                        bpr = pr.sb("bpr", [1, DM], F32)
                        pr.dma("gpsimd", lambda eng: eng.dma_start(
                            out=bpr[0][:, :], in_=bproj_row), writes=[bpr[1]])
                        for dc in range(NTC):
                            pr.op("tensor", lambda eng, dc=dc: eng.matmul(
                                kps[0][:, :], onesf[0][:, :],
                                bpr[0][0:1, dc * 512:(dc + 1) * 512],
                                start=True, stop=True),
                                reads=[onesf[1], bpr[1]], writes=[kps[1]])
                            pr.op("scalar", lambda eng, dc=dc: eng.copy(
                                bpb[0][:, dc * 512:(dc + 1) * 512], kps[0][:, :]),
                                reads=[kps[1]], writes=[bpb[1]])

                if KSUB < 3:
                    return
                with pr.phase(name + "_qa"):
                    ptps = pr.ps("ptp", [128, 128], BF16)
                    scps = [pr.ps("scps", [128, 512], F32) for _ in range(2)]
                    yps = pr.ps("yps", [128, 512], F32)
                    qps = pr.ps("qps", [128, 512], F32)
                    qps2 = pr.ps("qps2", [128, 512], F32)
                    masks = pr.sb("masks", [128, 4 * 512], BF16)
                    for i in range(4):
                        pr.dma("gpsimd", lambda eng, i=i: eng.dma_start(
                            out=masks[0][:, i * 512:(i + 1) * 512],
                            in_=masks_c[i * 128:(i + 1) * 128, :]),
                            writes=[masks[1]])
                    t1s = pr.sb("t1s", [128, 512], F32)
                    t2s = pr.sb("t2s", [128, 512], F32)
                    biascol = pr.sb("biascol", [128, 16], F32)
                    if not is_l0:
                        pr.dma("gpsimd", lambda eng: eng.dma_start(
                            out=biascol[0][:, :], in_=bdh), writes=[biascol[1]])
                    qbf = pr.sb("qbf", [128, 512], BF16)
                    qraw = pr.sb("qraw", [128, 512], BF16)
                    msc = pr.sb("msc", [128, 2048], F32)
                    pnb = pr.sb("pnb", [128, 2048], BF16)
                    ptst = pr.sb("ptst", [128, 16 * 512], BF16)
                    aoc = pr.sb("aoc", [128, NH * 512], BF16)
                    rowm = pr.sb("rowm", [128, 4], F32)
                    wqall = pr.sb("wqall", [128, NCH * 128], BF16)
                    woall = pr.sb("woall", [128, NCH * 512], BF16)
                    xres = [pr.sb("xres", [128, 512], F32) for _ in range(2)]
                    oacc = pr.sb("oacc", [128, 512], F32)
                    wsrc = wq_bf if is_l0 else wd_bf
                    wsrcb = BC["wq_bf"] if is_l0 else BC["wd_bf"]
                    wob = wo_bf if is_l0 else wproj_bf
                    wobuf = BC["wo_bf"] if is_l0 else BC["wproj_bf"]
                    for qc in range(min(NTC, KQC)):
                        nkc = qc + 1
                        for h in range(min(NH, KH)):
                            pr.dma("scalar", lambda eng, h=h: eng.dma_start(
                                out=wqall[0][:, :].rearrange(
                                    "p (c f) -> p c f", f=128),
                                in_=wsrc[:, h * 128:(h + 1) * 128].rearrange(
                                    "(c p) f -> p c f", p=128)),
                                reads=[wsrcb], writes=[wqall[1]])
                            for c in range(NCH):
                                pr.op("tensor", lambda eng, c=c, qc=qc: eng.matmul(
                                    qps[0][:, :],
                                    wqall[0][:, c * 128:(c + 1) * 128],
                                    A(c, qc * 512, 512),
                                    start=(c == 0), stop=(c == NCH - 1)),
                                    reads=[wqall[1], arena[1]], writes=[qps[1]])
                            if is_l0:
                                pr.op("scalar", lambda eng: eng.copy(
                                    qraw[0][:, :], qps[0][:, :]),
                                    reads=[qps[1]], writes=[qraw[1]])
                                pr.op("tensor", lambda eng: eng.matmul(
                                    qps2[0][:, :], ppermt[0][:, :], qraw[0][:, :],
                                    start=True, stop=True),
                                    reads=[ppermt[1], qraw[1]], writes=[qps2[1]])
                                pr.op("vector", lambda eng, qc=qc: eng.tensor_tensor(
                                    t1s[0][:, :], qraw[0][:, :],
                                    cos_t[0][:, qc * 512:(qc + 1) * 512], ALU.mult),
                                    reads=[qraw[1], cos_t[1]], writes=[t1s[1]])
                                pr.op("vector", lambda eng, qc=qc: eng.tensor_tensor(
                                    t2s[0][:, :], qps2[0][:, :],
                                    sin_t[0][:, qc * 512:(qc + 1) * 512], ALU.mult),
                                    reads=[qps2[1], sin_t[1]], writes=[t2s[1]])
                                pr.op("vector", lambda eng: eng.tensor_tensor(
                                    qbf[0][:, :], t1s[0][:, :], t2s[0][:, :], ALU.add),
                                    reads=[t1s[1], t2s[1]], writes=[qbf[1]])
                            else:
                                pr.op("scalar", lambda eng, h=h: eng.activation(
                                    qbf[0][:, :], qps[0][:, :], ACT.Identity,
                                    bias=biascol[0][:, h:h + 1], scale=1.0),
                                    reads=[qps[1], biascol[1]], writes=[qbf[1]])
                            kv = h // 4 if is_l0 else 0
                            koff = kv * S if is_l0 else 0
                            for qs in range(4):
                                for kc in range(nkc):
                                    sp = scps[kc % 2]
                                    pr.op("tensor", lambda eng, qs=qs, kc=kc, sp=sp, koff=koff: eng.matmul(
                                        sp[0][:, :],
                                        qbf[0][:, qs * 128:(qs + 1) * 128],
                                        kst[0][:, koff + kc * 512: koff + (kc + 1) * 512],
                                        start=True, stop=True),
                                        reads=[qbf[1], kst[1]], writes=[sp[1]])
                                    if kc == nkc - 1:
                                        pr.op("vector", lambda eng, qs=qs, kc=kc, sp=sp: eng.scalar_tensor_tensor(
                                            msc[0][:, kc * 512:(kc + 1) * 512],
                                            sp[0][:, :], ISQ,
                                            masks[0][:, qs * 512:(qs + 1) * 512],
                                            ALU.mult, ALU.add),
                                            reads=[sp[1], masks[1]], writes=[msc[1]])
                                    else:
                                        pr.op("vector", lambda eng, kc=kc, sp=sp: eng.tensor_single_scalar(
                                            msc[0][:, kc * 512:(kc + 1) * 512],
                                            sp[0][:, :], ISQ, ALU.mult),
                                            reads=[sp[1]], writes=[msc[1]])
                                W = nkc * 512
                                pr.op("vector", lambda eng, W=W: eng.reduce_max(
                                    rowm[0][:, 0:1], msc[0][:, 0:W], axis=AX.X),
                                    reads=[msc[1]], writes=[rowm[1]])
                                pr.op("vector", lambda eng: eng.tensor_single_scalar(
                                    rowm[0][:, 1:2], rowm[0][:, 0:1], -1.0, ALU.mult),
                                    reads=[rowm[1]], writes=[rowm[1]])
                                pr.op("scalar", lambda eng, W=W: eng.activation(
                                    pnb[0][:, 0:W], msc[0][:, 0:W], ACT.Exp,
                                    bias=rowm[0][:, 1:2], scale=1.0,
                                    accum_out=rowm[0][:, 2:3]),
                                    reads=[msc[1], rowm[1]], writes=[pnb[1], rowm[1]])
                                pr.op("scalar", lambda eng: eng.activation(
                                    rowm[0][:, 3:4], rowm[0][:, 2:3], ACT.Ln,
                                    bias=0.0, scale=1.0),
                                    reads=[rowm[1]], writes=[rowm[1]])
                                pr.op("vector", lambda eng: eng.tensor_tensor(
                                    rowm[0][:, 3:4], rowm[0][:, 1:2],
                                    rowm[0][:, 3:4], ALU.subtract),
                                    reads=[rowm[1]], writes=[rowm[1]])
                                pr.op("scalar", lambda eng, W=W: eng.activation(
                                    pnb[0][:, 0:W], msc[0][:, 0:W], ACT.Exp,
                                    bias=rowm[0][:, 3:4], scale=1.0),
                                    reads=[msc[1], rowm[1]], writes=[pnb[1]])
                                for ks in range(nkc * 4):
                                    emit_tr128(
                                        ptps,
                                        ptst[0][:, ks * 512 + qs * 128:
                                                ks * 512 + (qs + 1) * 128],
                                        ptst[1],
                                        pnb[0][:, ks * 128:(ks + 1) * 128], pnb[1])
                            nks = nkc * 4
                            for ks in range(nks):
                                voff = (ks * 512 + kv * 128) if is_l0 else (ks * 128)
                                pr.op("tensor", lambda eng, ks=ks, voff=voff, nks=nks: eng.matmul(
                                    yps[0][:, :], vst[0][:, voff: voff + 128],
                                    ptst[0][:, ks * 512:(ks + 1) * 512],
                                    start=(ks == 0), stop=(ks == nks - 1)),
                                    reads=[vst[1], ptst[1]], writes=[yps[1]])
                            pr.op("scalar", lambda eng, h=h: eng.copy(
                                aoc[0][:, h * 512:(h + 1) * 512], yps[0][:, :]),
                                reads=[yps[1]], writes=[aoc[1]])
                        for dc in range(NTC):
                            for c in range(NCH):
                                pr.dma("scalar", lambda eng, c=c, dc=dc: eng.dma_start(
                                    out=woall[0][:, c * 512:(c + 1) * 512],
                                    in_=wob[c * 128:(c + 1) * 128,
                                            dc * 512:(dc + 1) * 512]),
                                    reads=[wobuf], writes=[woall[1]])
                            for qs in range(4):
                                sp = scps[qs % 2]
                                for c in range(NCH):
                                    pr.op("tensor", lambda eng, c=c, qs=qs, sp=sp: eng.matmul(
                                        sp[0][:, :],
                                        aoc[0][:, c * 512 + qs * 128:
                                                c * 512 + (qs + 1) * 128],
                                        woall[0][:, c * 512:(c + 1) * 512],
                                        start=(c == 0), stop=(c == NCH - 1)),
                                        reads=[aoc[1], woall[1]], writes=[sp[1]])
                                ts = qc * 4 + qs
                                xr = xres[qs % 2]
                                pr.dma("sync", lambda eng, ts=ts, dc=dc, xr=xr: eng.dma_start(
                                    out=xr[0][:, :],
                                    in_=xsrc[ts * 128:(ts + 1) * 128,
                                             dc * 512:(dc + 1) * 512]),
                                    reads=[xsrc_buf], writes=[xr[1]])
                                if is_l0:
                                    pr.op("vector", lambda eng, sp=sp, xr=xr: eng.tensor_tensor(
                                        oacc[0][:, :], sp[0][:, :], xr[0][:, :],
                                        ALU.add), reads=[sp[1], xr[1]], writes=[oacc[1]])
                                else:
                                    pr.op("vector", lambda eng, sp=sp, dc=dc: eng.tensor_tensor(
                                        oacc[0][:, :], sp[0][:, :],
                                        bpb[0][:, dc * 512:(dc + 1) * 512], ALU.add),
                                        reads=[sp[1], bpb[1]], writes=[oacc[1]])
                                    pr.op("vector", lambda eng, xr=xr: eng.tensor_tensor(
                                        oacc[0][:, :], oacc[0][:, :], xr[0][:, :],
                                        ALU.add), reads=[oacc[1], xr[1]], writes=[oacc[1]])
                                pr.dma("sync", lambda eng, ts=ts, dc=dc: eng.dma_start(
                                    out=xdst[ts * 128:(ts + 1) * 128,
                                             dc * 512:(dc + 1) * 512],
                                    in_=oacc[0][:, :]),
                                    reads=[oacc[1]], writes=[xdst_buf])

        STAGES = int(os.environ.get("KSTAGES", "99"))
        if STAGES >= 2:
            attention_block("l0a", x_in, BC["x_in"], xA, BC["xA"], True)

        def emit_l0f():
            with pr.phase("l0f_qx"):
                cols0 = make_cols(True)
                ptps = pr.ps("ptp", [128, 128], BF16)
                emit_ln_to_fm(xA, BC["xA"], True, cols0, ptps)
                pr.op("vector", lambda eng: eng.tensor_copy(
                    isx_keep[0][:, :], cols0["isx"][0][:, :]),
                    reads=[cols0["isx"][1]], writes=[isx_keep[1]])

            with pr.phase("l0f_gateup"):
                gt = [pr.sb("gt", [128, 512], BF16) for _ in range(NCH)]
                ut = [pr.sb("ut", [128, 512], BF16) for _ in range(NCH)]
                gps = pr.ps("gps", [128, 512], F32)
                ups = pr.ps("ups", [128, 512], F32)
                lnps = pr.ps("lnps", [128, 512], F32)
                sgs = pr.sb("sgs", [128, 512], F32)
                hb = [pr.sb("hb", [128, 512], BF16) for _ in range(2)]
                hl = pr.sb("hl", [128, 512], F32)
                lnb = pr.sb("lnb", [128, 512], F32)
                tred = pr.sb("tred", [128, 2], F32)
                scg = pr.sb("scg", [128, NTS], F32)
                scu = pr.sb("scu", [128, NTS], F32)
                lnr = pr.sb("lnr", [1, INTER0], F32)
                pr.dma("gpsimd", lambda eng: eng.dma_start(out=lnr[0][:, :],
                                                           in_=lnw_row),
                       writes=[lnr[1]])
                pr.op("vector", lambda eng: eng.memset(hstat[0][:, 0:2 * NTS], 0.0),
                      writes=[hstat[1]])
                pr.op("vector", lambda eng: eng.tensor_scalar(
                    scg[0][:, :], isx_keep[0][:, :], wm_keep[0][:, 0:1], 0.0,
                    ALU.mult, ALU.add), reads=[isx_keep[1], wm_keep[1]],
                    writes=[scg[1]])
                pr.op("vector", lambda eng: eng.tensor_scalar(
                    scu[0][:, :], isx_keep[0][:, :], wm_keep[0][:, 1:2], 0.0,
                    ALU.mult, ALU.add), reads=[isx_keep[1], wm_keep[1]],
                    writes=[scu[1]])
                for jc in range(NJC0):
                    pr.op("tensor", lambda eng, jc=jc: eng.matmul(
                        lnps[0][:, :], onesf[0][:, :],
                        lnr[0][0:1, jc * 512:(jc + 1) * 512], start=True, stop=True),
                        reads=[onesf[1], lnr[1]], writes=[lnps[1]])
                    pr.op("scalar", lambda eng: eng.copy(
                        lnb[0][:, :], lnps[0][:, :]), reads=[lnps[1]], writes=[lnb[1]])
                    for c in range(NCH):
                        pr.dma("sync", lambda eng, jc=jc, c=c: eng.dma_start(
                            out=gt[c][0][:, :],
                            in_=tern_g[jc * 512:(jc + 1) * 512, c * 128:(c + 1) * 128],
                            transpose=True), reads=[BC["tern_g"]], writes=[gt[c][1]])
                        pr.dma("sync", lambda eng, jc=jc, c=c: eng.dma_start(
                            out=ut[c][0][:, :],
                            in_=tern_u[jc * 512:(jc + 1) * 512, c * 128:(c + 1) * 128],
                            transpose=True), reads=[BC["tern_u"]], writes=[ut[c][1]])
                    for ts in range(NTS):
                        for c in range(NCH):
                            pr.op("tensor", lambda eng, c=c, ts=ts: eng.matmul(
                                gps[0][:, :], A(c, ts * 128, 128), gt[c][0][:, :],
                                start=(c == 0), stop=(c == NCH - 1)),
                                reads=[arena[1], gt[c][1]], writes=[gps[1]])
                        for c in range(NCH):
                            pr.op("tensor", lambda eng, c=c, ts=ts: eng.matmul(
                                ups[0][:, :], A(c, ts * 128, 128), ut[c][0][:, :],
                                start=(c == 0), stop=(c == NCH - 1)),
                                reads=[arena[1], ut[c][1]], writes=[ups[1]])
                        h = hb[ts % 2]
                        pr.op("scalar", lambda eng, ts=ts: eng.activation(
                            sgs[0][:, :], gps[0][:, :], ACT.Silu, bias=0.0,
                            scale=scg[0][:, ts:ts + 1]),
                            reads=[gps[1], scg[1]], writes=[sgs[1]])
                        pr.op("vector", lambda eng, ts=ts, h=h: eng.scalar_tensor_tensor(
                            h[0][:, :], sgs[0][:, :], scu[0][:, ts:ts + 1],
                            ups[0][:, :], ALU.mult, ALU.mult),
                            reads=[sgs[1], scu[1], ups[1]], writes=[h[1]])
                        pr.op("vector", lambda eng, h=h: eng.tensor_tensor(
                            hl[0][:, :], h[0][:, :], lnb[0][:, :], ALU.mult),
                            reads=[h[1], lnb[1]], writes=[hl[1]])
                        pr.op("vector", lambda eng: eng.reduce_max(
                            tred[0][:, 0:1], hl[0][:, :], axis=AX.X,
                            apply_absolute_value=True), reads=[hl[1]], writes=[tred[1]])
                        pr.op("vector", lambda eng, ts=ts: eng.tensor_tensor(
                            hstat[0][:, ts:ts + 1], hstat[0][:, ts:ts + 1],
                            tred[0][:, 0:1], ALU.max),
                            reads=[hstat[1], tred[1]], writes=[hstat[1]])
                        pr.op("scalar", lambda eng, h=h: eng.activation(
                            hl[0][:, :], h[0][:, :], ACT.Square, bias=0.0, scale=1.0,
                            accum_out=tred[0][:, 1:2]),
                            reads=[h[1]], writes=[hl[1], tred[1]])
                        pr.op("vector", lambda eng, ts=ts: eng.tensor_tensor(
                            hstat[0][:, NTS + ts:NTS + ts + 1],
                            hstat[0][:, NTS + ts:NTS + ts + 1],
                            tred[0][:, 1:2], ALU.add),
                            reads=[hstat[1], tred[1]], writes=[hstat[1]])
                        pr.dma("gpsimd", lambda eng, ts=ts, jc=jc, h=h: eng.dma_start(
                            out=h_spill[ts * 128:(ts + 1) * 128,
                                        jc * 512:(jc + 1) * 512], in_=h[0][:, :]),
                            reads=[h[1]], writes=[BC["h_spill"]])
                pr.op("scalar", lambda eng: eng.activation(
                    hstat[0][:, 5 * NTS:6 * NTS], hstat[0][:, NTS:2 * NTS],
                    ACT.Sqrt, bias=EPS_RMS, scale=1.0 / INTER0),
                    reads=[hstat[1]], writes=[hstat[1]])
                pr.op("vector", lambda eng: eng.reciprocal(
                    hstat[0][:, 2 * NTS:3 * NTS], hstat[0][:, 5 * NTS:6 * NTS]),
                    reads=[hstat[1]], writes=[hstat[1]])
                pr.op("vector", lambda eng: eng.tensor_tensor(
                    hstat[0][:, 0:NTS], hstat[0][:, 0:NTS],
                    hstat[0][:, 2 * NTS:3 * NTS], ALU.mult),
                    reads=[hstat[1]], writes=[hstat[1]])
                pr.op("vector", lambda eng: eng.tensor_single_scalar(
                    hstat[0][:, 0:NTS], hstat[0][:, 0:NTS], EPS_Q, ALU.max),
                    reads=[hstat[1]], writes=[hstat[1]])
                pr.op("vector", lambda eng: eng.tensor_scalar(
                    hstat[0][:, 3 * NTS:4 * NTS], hstat[0][:, 0:NTS],
                    wm_keep[0][:, 2:3], 1.0 / 127.0, ALU.mult, ALU.mult),
                    reads=[hstat[1], wm_keep[1]], writes=[hstat[1]])
                pr.op("vector", lambda eng: eng.reciprocal(
                    hstat[0][:, 4 * NTS:5 * NTS], hstat[0][:, 0:NTS]),
                    reads=[hstat[1]], writes=[hstat[1]])
                pr.op("vector", lambda eng: eng.tensor_single_scalar(
                    hstat[0][:, 4 * NTS:5 * NTS], hstat[0][:, 4 * NTS:5 * NTS],
                    127.0, ALU.mult), reads=[hstat[1]], writes=[hstat[1]])
                pr.op("vector", lambda eng: eng.tensor_tensor(
                    hstat[0][:, 2 * NTS:3 * NTS], hstat[0][:, 2 * NTS:3 * NTS],
                    hstat[0][:, 4 * NTS:5 * NTS], ALU.mult),
                    reads=[hstat[1]], writes=[hstat[1]])
                ctps = pr.ps("ctps", [1, 128], F32)
                for ts in range(NTS):
                    pr.op("tensor", lambda eng, ts=ts: eng.transpose(
                        ctps[0][:, :], hstat[0][:, 2 * NTS + ts:2 * NTS + ts + 1],
                        identf[0][:, :]),
                        reads=[hstat[1], identf[1]], writes=[ctps[1]])
                    pr.op("scalar", lambda eng, ts=ts: eng.copy(
                        crows[0][0:1, ts * 128:(ts + 1) * 128], ctps[0][:, :]),
                        reads=[ctps[1]], writes=[crows[1]])

            with pr.phase("l0f_down"):
                wdt = [pr.sb("wdt", [128, 512], BF16) for _ in range(2)]
                hts = [pr.sb("hts", [128, 512], BF16) for _ in range(2)]
                v1 = pr.sb("v1", [128, 512], F32)
                r1 = pr.sb("r1", [128, 512], F32)
                r2 = pr.sb("r2", [128, 512], F32)
                cbc = pr.sb("cbc", [128, 512], F32)
                lncolt = pr.sb("lncolt", [128, NJ0], F32)
                oacc = pr.sb("oacc", [128, 512], F32)
                xres = [pr.sb("xres", [128, 512], F32) for _ in range(2)]
                dps = [pr.ps("dps", [128, 512], F32) for _ in range(4)]
                cps = pr.ps("cps", [128, 512], F32)
                pr.dma("gpsimd", lambda eng: eng.dma_start(
                    out=lncolt[0][:, :], in_=lnw_colt), writes=[lncolt[1]])
                for tc in range(NTC):
                    for i in range(4):
                        ts = tc * 4 + i
                        pr.op("tensor", lambda eng, ts=ts, i=i: eng.matmul(
                            cps[0][:, i * 128:(i + 1) * 128], onesf[0][:, :],
                            crows[0][0:1, ts * 128:(ts + 1) * 128],
                            start=True, stop=True),
                            reads=[onesf[1], crows[1]], writes=[cps[1]])
                    pr.op("scalar", lambda eng: eng.copy(
                        cbc[0][:, :], cps[0][:, :]), reads=[cps[1]], writes=[cbc[1]])
                    for j in range(NJ0):
                        ht = hts[j % 2]
                        pr.dma("sync", lambda eng, tc=tc, j=j, ht=ht: eng.dma_start(
                            out=ht[0][:, :],
                            in_=h_spill[tc * 512:(tc + 1) * 512,
                                        j * 128:(j + 1) * 128], transpose=True),
                            reads=[BC["h_spill"]], writes=[ht[1]])
                        pr.op("scalar", lambda eng, j=j, ht=ht: eng.activation(
                            v1[0][:, :], ht[0][:, :], ACT.Identity, bias=0.0,
                            scale=lncolt[0][:, j:j + 1]),
                            reads=[ht[1], lncolt[1]], writes=[v1[1]])
                        pr.op("vector", lambda eng: eng.tensor_tensor(
                            v1[0][:, :], v1[0][:, :], cbc[0][:, :], ALU.mult),
                            reads=[v1[1], cbc[1]], writes=[v1[1]])
                        emit_round(arena[0][:, j * 512:(j + 1) * 512], arena[1],
                                   v1[0][:, :], v1[1], r1, r2, 512)
                    for dc in range(NTC):
                        for j in range(NJ0):
                            w = wdt[j % 2]
                            pr.dma("sync", lambda eng, dc=dc, j=j, w=w: eng.dma_start(
                                out=w[0][:, :],
                                in_=tern_d[dc * 512:(dc + 1) * 512,
                                           j * 128:(j + 1) * 128], transpose=True),
                                reads=[BC["tern_d"]], writes=[w[1]])
                            for i in range(4):
                                pr.op("tensor", lambda eng, i=i, j=j, w=w: eng.matmul(
                                    dps[i][0][:, :],
                                    arena[0][:, j * 512 + i * 128: j * 512 + (i + 1) * 128],
                                    w[0][:, :],
                                    start=(j == 0), stop=(j == NJ0 - 1)),
                                    reads=[arena[1], w[1]], writes=[dps[i][1]])
                        for i in range(4):
                            ts = tc * 4 + i
                            xr = xres[i % 2]
                            pr.dma("sync", lambda eng, ts=ts, dc=dc, xr=xr: eng.dma_start(
                                out=xr[0][:, :],
                                in_=xA[ts * 128:(ts + 1) * 128,
                                       dc * 512:(dc + 1) * 512]),
                                reads=[BC["xA"]], writes=[xr[1]])
                            pr.op("scalar", lambda eng, i=i, ts=ts: eng.activation(
                                oacc[0][:, :], dps[i][0][:, :], ACT.Identity,
                                bias=0.0, scale=hstat[0][:, 3 * NTS + ts:3 * NTS + ts + 1]),
                                reads=[dps[i][1], hstat[1]], writes=[oacc[1]])
                            pr.op("vector", lambda eng, xr=xr: eng.tensor_tensor(
                                oacc[0][:, :], oacc[0][:, :], xr[0][:, :], ALU.add),
                                reads=[oacc[1], xr[1]], writes=[oacc[1]])
                            pr.dma("sync", lambda eng, ts=ts, dc=dc: eng.dma_start(
                                out=xB[ts * 128:(ts + 1) * 128,
                                       dc * 512:(dc + 1) * 512], in_=oacc[0][:, :]),
                                reads=[oacc[1]], writes=[BC["xB"]])


        def emit_l1f():
            with pr.phase("l1f_combine"):
                cols1 = make_cols(False)
                ptps = pr.ps("ptp", [128, 128], BF16)
                emit_ln_to_fm(xC, BC["xC"], False, cols1, ptps)
                gwb = pr.sb("gwb2", [15, 2048], BF16)
                gwt = pr.sb("gwt2", [128, NCH * 15], BF16)
                gtp = pr.ps("gtp2", [128, 15], BF16)
                lps = pr.ps("lps2", [128, 15], F32)
                lg = pr.sb("lg2", [128, 15], F32)
                c1 = pr.sb("c1_2", [128, 15], F32)
                mskd = pr.sb("mskd2", [128, 15], F32)
                pcol = pr.sb("pcol2", [128, 4], F32)
                pr.dma("gpsimd", lambda eng: eng.dma_start(out=gwb[0][:, :],
                                                           in_=gatew),
                       writes=[gwb[1]])
                for c in range(NCH):
                    pr.op("tensor", lambda eng, c=c: eng.transpose(
                        gtp[0][:, :], gwb[0][:, c * 128:(c + 1) * 128],
                        ident[0][0:15, 0:15]),
                        reads=[gwb[1], ident[1]], writes=[gtp[1]])
                    pr.op("scalar", lambda eng, c=c: eng.copy(
                        gwt[0][:, c * 15:(c + 1) * 15], gtp[0][:, :]),
                        reads=[gtp[1]], writes=[gwt[1]])
                pr.op("vector", lambda eng: eng.memset(
                    cmb[0][:, :], 1.0), writes=[cmb[1]])
                for ts in range(NTS):
                    for c in range(NCH):
                        pr.op("tensor", lambda eng, ts=ts, c=c: eng.matmul(
                            lps[0][:, :], A(c, ts * 128, 128),
                            gwt[0][:, c * 15:(c + 1) * 15],
                            start=(c == 0), stop=(c == NCH - 1)),
                            reads=[arena[1], gwt[1]], writes=[lps[1]])
                    pr.op("vector", lambda eng: eng.reduce_max(
                        pcol[0][:, 0:1], lps[0][:, :], axis=AX.X),
                        reads=[lps[1]], writes=[pcol[1]])
                    pr.op("vector", lambda eng: eng.tensor_single_scalar(
                        pcol[0][:, 1:2], pcol[0][:, 0:1], -1.0, ALU.mult),
                        reads=[pcol[1]], writes=[pcol[1]])
                    pr.op("scalar", lambda eng: eng.activation(
                        lg[0][:, :], lps[0][:, :], ACT.Exp,
                        bias=pcol[0][:, 1:2], scale=1.0, accum_out=pcol[0][:, 2:3]),
                        reads=[lps[1], pcol[1]], writes=[lg[1], pcol[1]])
                    pr.op("vector", lambda eng: eng.reciprocal(
                        pcol[0][:, 3:4], pcol[0][:, 2:3]),
                        reads=[pcol[1]], writes=[pcol[1]])
                    pr.op("scalar", lambda eng: eng.activation(
                        lg[0][:, :], lg[0][:, :], ACT.Identity, bias=0.0,
                        scale=pcol[0][:, 3:4]), reads=[lg[1], pcol[1]], writes=[lg[1]])
                    pr.op("vector", lambda eng: eng.reduce_max(
                        pcol[0][:, 0:1], lg[0][:, :], axis=AX.X),
                        reads=[lg[1]], writes=[pcol[1]])
                    pr.op("vector", lambda eng: eng.tensor_scalar(
                        c1[0][:, :], lg[0][:, :], pcol[0][:, 0:1], NEG,
                        ALU.is_ge, ALU.mult), reads=[lg[1], pcol[1]], writes=[c1[1]])
                    pr.op("vector", lambda eng: eng.tensor_tensor(
                        mskd[0][:, :], lg[0][:, :], c1[0][:, :], ALU.add),
                        reads=[lg[1], c1[1]], writes=[mskd[1]])
                    pr.op("vector", lambda eng: eng.reduce_max(
                        pcol[0][:, 1:2], mskd[0][:, :], axis=AX.X),
                        reads=[mskd[1]], writes=[pcol[1]])
                    pr.op("vector", lambda eng: eng.tensor_scalar(
                        c1[0][:, :], lg[0][:, :], pcol[0][:, 1:2], 0.0,
                        ALU.is_ge, ALU.add), reads=[lg[1], pcol[1]], writes=[c1[1]])
                    pr.op("vector", lambda eng, ts=ts: eng.tensor_tensor(
                        cmb[0][:, ts * 16: ts * 16 + 15], c1[0][:, :],
                        lg[0][:, :], ALU.mult), reads=[c1[1], lg[1]], writes=[cmb[1]])

            with pr.phase("l1f_moe"):
                NQ = 4
                TQ = S // NQ
                NTSQ = TQ // 128
                yacc = pr.sb("yacc", [128, NTSQ * DM], F32)
                w13t = [pr.sb("w13t", [128, INTER1], BF16) for _ in range(NCH)]
                sgst = pr.sb("sgst", [128, NTSQ * INTER1], BF16)
                hhst = pr.sb("hhst", [128, NTSQ * INTER1], BF16)
                hfm = pr.sb("hfm", [128, NM1 * TQ], BF16)
                w2t = [pr.sb("w2t", [128, 512], BF16) for _ in range(2)]
                hps = pr.ps("hps", [128, 512], F32)
                ptps = pr.ps("ptp", [128, 128], BF16)
                dps = [pr.ps("dps", [128, 512], F32) for _ in range(4)]
                xres = [pr.sb("xres", [128, 512], F32)]
                oacc = pr.sb("oacc", [128, 512], F32)
                exps = [(w1_bf[i], w3_bf[i], w2_bf[i], BC["w1_bf"], BC["w3_bf"],
                         BC["w2_bf"], i) for i in range(NEXP)]
                exps.append((sw1_bf, sw3_bf, sw2_bf, BC["sw1_bf"], BC["sw3_bf"],
                             BC["sw2_bf"], 15))
                for q in range(NQ):
                    pr.op("vector", lambda eng: eng.memset(yacc[0][:, :], 0.0),
                          writes=[yacc[1]])
                    for (e1, e3, e2, b1, b3, b2, ei) in exps:
                        for c in range(NCH):
                            pr.dma("sync", lambda eng, c=c, e1=e1: eng.dma_start(
                                out=w13t[c][0][:, :],
                                in_=e1[:, c * 128:(c + 1) * 128], transpose=True),
                                reads=[b1], writes=[w13t[c][1]])
                        for i in range(NTSQ):
                            ts = q * NTSQ + i
                            off = 0
                            for jw in L1JC:
                                for c in range(NCH):
                                    pr.op("tensor", lambda eng, c=c, ts=ts, off=off, jw=jw: eng.matmul(
                                        hps[0][:, 0:jw], A(c, ts * 128, 128),
                                        w13t[c][0][:, off:off + jw],
                                        start=(c == 0), stop=(c == NCH - 1)),
                                        reads=[arena[1], w13t[c][1]], writes=[hps[1]])
                                pr.op("scalar", lambda eng, i=i, off=off, jw=jw: eng.activation(
                                    sgst[0][:, i * INTER1 + off: i * INTER1 + off + jw],
                                    hps[0][:, 0:jw], ACT.Silu, bias=0.0, scale=1.0),
                                    reads=[hps[1]], writes=[sgst[1]])
                                off += jw
                        for c in range(NCH):
                            pr.dma("sync", lambda eng, c=c, e3=e3: eng.dma_start(
                                out=w13t[c][0][:, :],
                                in_=e3[:, c * 128:(c + 1) * 128], transpose=True),
                                reads=[b3], writes=[w13t[c][1]])
                        for i in range(NTSQ):
                            ts = q * NTSQ + i
                            off = 0
                            for jw in L1JC:
                                for c in range(NCH):
                                    pr.op("tensor", lambda eng, c=c, ts=ts, off=off, jw=jw: eng.matmul(
                                        hps[0][:, 0:jw], A(c, ts * 128, 128),
                                        w13t[c][0][:, off:off + jw],
                                        start=(c == 0), stop=(c == NCH - 1)),
                                        reads=[arena[1], w13t[c][1]], writes=[hps[1]])
                                pr.op("vector", lambda eng, i=i, ts=ts, off=off, jw=jw, ei=ei: eng.scalar_tensor_tensor(
                                    hhst[0][:, i * INTER1 + off: i * INTER1 + off + jw],
                                    sgst[0][:, i * INTER1 + off: i * INTER1 + off + jw],
                                    cmb[0][:, ts * 16 + ei: ts * 16 + ei + 1],
                                    hps[0][:, 0:jw], ALU.mult, ALU.mult),
                                    reads=[sgst[1], cmb[1], hps[1]], writes=[hhst[1]])
                                off += jw
                        for i in range(NTSQ):
                            for m in range(NM1):
                                emit_tr128(
                                    ptps,
                                    hfm[0][:, m * TQ + i * 128: m * TQ + (i + 1) * 128],
                                    hfm[1],
                                    hhst[0][:, i * INTER1 + m * 128:
                                            i * INTER1 + (m + 1) * 128], hhst[1])
                        for dc in range(NTC):
                            for m in range(NM1):
                                w = w2t[m % 2]
                                pr.dma("sync", lambda eng, dc=dc, m=m, w=w, e2=e2: eng.dma_start(
                                    out=w[0][:, :],
                                    in_=e2[dc * 512:(dc + 1) * 512,
                                           m * 128:(m + 1) * 128], transpose=True),
                                    reads=[b2], writes=[w[1]])
                                for i in range(NTSQ):
                                    pr.op("tensor", lambda eng, i=i, m=m, w=w: eng.matmul(
                                        dps[i][0][:, :],
                                        hfm[0][:, m * TQ + i * 128: m * TQ + (i + 1) * 128],
                                        w[0][:, :],
                                        start=(m == 0), stop=(m == NM1 - 1)),
                                        reads=[hfm[1], w[1]], writes=[dps[i][1]])
                            for i in range(NTSQ):
                                pr.op("vector", lambda eng, i=i, dc=dc: eng.tensor_tensor(
                                    yacc[0][:, i * DM + dc * 512: i * DM + (dc + 1) * 512],
                                    yacc[0][:, i * DM + dc * 512: i * DM + (dc + 1) * 512],
                                    dps[i][0][:, :], ALU.add),
                                    reads=[yacc[1], dps[i][1]], writes=[yacc[1]])
                    for i in range(NTSQ):
                        ts = q * NTSQ + i
                        for dc in range(NTC):
                            xr = xres[0]
                            pr.dma("sync", lambda eng, ts=ts, dc=dc, xr=xr: eng.dma_start(
                                out=xr[0][:, :],
                                in_=xC[ts * 128:(ts + 1) * 128,
                                       dc * 512:(dc + 1) * 512]),
                                reads=[BC["xC"]], writes=[xr[1]])
                            pr.op("vector", lambda eng, i=i, dc=dc, xr=xr: eng.tensor_tensor(
                                oacc[0][:, :],
                                yacc[0][:, i * DM + dc * 512: i * DM + (dc + 1) * 512],
                                xr[0][:, :], ALU.add),
                                reads=[yacc[1], xr[1]], writes=[oacc[1]])
                            pr.dma("sync", lambda eng, ts=ts, dc=dc: eng.dma_start(
                                out=out[ts * 128:(ts + 1) * 128,
                                        dc * 512:(dc + 1) * 512], in_=oacc[0][:, :]),
                                reads=[oacc[1]], writes=[BC["out"]])


        if STAGES >= 3:
            emit_l0f()
        if STAGES >= 4:
            attention_block("l1a", xB, BC["xB"], xC, BC["xC"], False)
        if STAGES >= 5:
            emit_l1f()

        with pr.phase("finish"):
            pr.barrier_wait_all("sync")

    build_program.trace = pr.trace
    return nc, pr.n_instr


def _host_consts():
    ident = np.eye(128, dtype=BF)
    identf = np.eye(128, dtype=np.float32)
    P = np.zeros((128, 128), dtype=np.float32)
    for d in range(64):
        P[d, d + 64] = -1.0
        P[64 + d, d] = 1.0
    ppermt = np.ascontiguousarray(P.T).astype(BF)
    masks = np.zeros((4 * 128, 512), dtype=np.float32)
    for m in range(4):
        p = np.arange(128)[:, None]
        f = np.arange(512)[None, :]
        masks[m * 128:(m + 1) * 128] = np.where(f > p + 128 * m, NEG, 0.0)
    masks = masks.astype(BF)
    ones = np.ones((1, 128), dtype=np.float32)
    return ident, identf, ppermt, masks, ones


_CACHE = {}


def kernel(L0_attn_x, L0_attn_Wq, L0_attn_Wk, L0_attn_Wv, L0_attn_Wo,
           L0_ffn_w_gate, L0_ffn_w_up, L0_ffn_w_down, L0_ffn_ln_weight,
           L1_attn_W_lat, L1_attn_b_lat, L1_attn_W_d, L1_attn_b_d,
           L1_attn_W_proj, L1_attn_b_proj, L1_ffn_gate_w, L1_ffn_w1,
           L1_ffn_w2, L1_ffn_w3, L1_ffn_sw1, L1_ffn_sw2, L1_ffn_sw3):
    if "nc" not in _CACHE:
        _CACHE["nc"], n = build_program()
        _CACHE["n_instr"] = n
    nc = _CACHE["nc"]

    ident, identf, ppermt, masks, ones = _host_consts()
    f32 = lambda a: np.ascontiguousarray(a, dtype=np.float32)
    shared = dict(
        wq=f32(L0_attn_Wq), wk=f32(L0_attn_Wk), wv=f32(L0_attn_Wv),
        wo=f32(L0_attn_Wo), wgate=f32(L0_ffn_w_gate), wup=f32(L0_ffn_w_up),
        wdown=f32(L0_ffn_w_down),
        lnw_row=f32(L0_ffn_ln_weight).reshape(1, INTER0),
        lnw_colt=np.ascontiguousarray(
            f32(L0_ffn_ln_weight).reshape(NJ0, 128).T),
        wlat=f32(L1_attn_W_lat), blat=f32(L1_attn_b_lat).reshape(128, 1),
        wd=f32(L1_attn_W_d),
        bdh=np.ascontiguousarray(f32(L1_attn_b_d).reshape(16, 128).T),
        wproj=f32(L1_attn_W_proj),
        bproj_row=f32(L1_attn_b_proj).reshape(1, DM),
        gatew=f32(L1_ffn_gate_w), w1=f32(L1_ffn_w1), w2=f32(L1_ffn_w2),
        w3=f32(L1_ffn_w3), sw1=f32(L1_ffn_sw1), sw2=f32(L1_ffn_sw2),
        sw3=f32(L1_ffn_sw3),
        ident_c=ident, identf_c=identf, ppermt_c=ppermt, masks_c=masks,
        ones_c=ones,
        csts_c=np.tile(np.array([0.25, math.pi / 2.0, 128.0, EPS_LN, EPS_RMS,
                                 1.0, 0.0, 0.0], np.float32), (128, 1)),
    )
    x = f32(L0_attn_x)
    in_maps = [dict(shared, x_in=np.ascontiguousarray(x[b]))
               for b in range(x.shape[0])]
    res = run_bass_kernel_spmd(nc, in_maps, core_ids=list(range(8)))
    outs = [np.asarray(r["out"], dtype=np.float32) for r in res.results]
    return np.stack(outs, axis=0)


# version 09
# speedup vs baseline: 1.0916x; 1.0445x over previous
import contextlib
import math
import os

os.environ.setdefault("GAUGE_NTFF_IGNORE_DMA", "1")

import numpy as np
import ml_dtypes

import concourse.bass as bass
import concourse.mybir as mybir
from concourse.bass_utils import run_bass_kernel_spmd

F32 = mybir.dt.float32
BF16 = mybir.dt.bfloat16
ALU = mybir.AluOpType
ACT = mybir.ActivationFunctionType
AX = mybir.AxisListType
BF = ml_dtypes.bfloat16

S = 2048
DM = 2048
NCH = DM // 128
NTS = S // 128
NTC = S // 512
INTER0 = 8192
NJC0 = INTER0 // 512
NJ0 = INTER0 // 128
INTER1 = 1408
NEXP = 15
NM1 = INTER1 // 128
L1JC = (512, 512, 384)
EPS_LN = 1e-5
EPS_RMS = 1e-6
EPS_Q = 1e-5
NEG = -1e30
ISQ = 1.0 / math.sqrt(128.0)

DBG = False

ENGINES = ("tensor", "vector", "scalar", "gpsimd", "sync")
DMA_QUEUES = ("sync", "scalar", "gpsimd")
STRICT_SYNC = os.environ.get("BASS_SIM_FP64") == "1"


class Buf:
    __slots__ = ("ready", "readers", "name")

    def __init__(self, name=""):
        self.ready = {}
        self.readers = {}
        self.name = name


class Prog:
    def __init__(self, nc, gstack):
        self.nc = nc
        self.gstack = gstack
        self.csem, self.ccount = {}, {}
        self.dsem, self.dcount = {}, {}
        for e in ENGINES:
            self.csem[e] = gstack.enter_context(nc.semaphore(f"cs_{e}"))
            self.ccount[e] = 0
        self.DMA_WAYS = 4
        for q in DMA_QUEUES:
            self.dsem[q] = [gstack.enter_context(nc.semaphore(f"ds_{q}{i}"))
                            for i in range(self.DMA_WAYS)]
            self.dcount[q] = [0] * self.DMA_WAYS
            self.dissue = getattr(self, "dissue", {})
            self.dissue[q] = 0
        self.waited = {e: {} for e in ENGINES}
        self.cur = None
        self.local_stack = None
        self.n_names = 0
        self.n_instr = 0
        self.trace = []
        self._barrier = {}
        self._barrier_done = set()

    def uname(self, base):
        self.n_names += 1
        return f"{base}_{self.n_names}"

    def sb(self, base, shape, dt, scope="local"):
        stack = self.local_stack if scope == "local" else self.gstack
        h = stack.enter_context(self.nc.sbuf_tensor(self.uname(base), shape, dt))
        return (h, Buf(base))

    def sb_on(self, stack, base, shape, dt):
        h = stack.enter_context(self.nc.sbuf_tensor(self.uname(base), shape, dt))
        return (h, Buf(base))

    def ps(self, base, shape, dt):
        h = self.local_stack.enter_context(
            self.nc.psum_tensor(self.uname(base), shape, dt))
        return (h, Buf(base))

    @contextlib.contextmanager
    def phase(self, name):
        snap = {}
        for e in ENGINES:
            if self.ccount[e]:
                snap[self.csem[e]] = self.ccount[e]
        for q in DMA_QUEUES:
            for i in range(self.DMA_WAYS):
                if self.dcount[q][i]:
                    snap[self.dsem[q][i]] = 16 * self.dcount[q][i]
        self._barrier = snap
        self._barrier_done = set()
        self.cur = {e: [] for e in ENGINES}
        with contextlib.ExitStack() as stack:
            self.local_stack = stack
            yield
            qs = self.cur
            self.cur = None
            self.local_stack = None
            with self.nc.Block() as blk:
                for e in ENGINES:
                    ops = qs[e]
                    if not ops:
                        continue

                    def body(eng, ops=ops):
                        for f in ops:
                            f(eng)
                    getattr(blk, e)(body)

    def _waits(self, engine, reads, writes):
        own = self.csem[engine]
        wt = self.waited[engine]
        need = {}
        if engine not in self._barrier_done:
            self._barrier_done.add(engine)
            for s, v in self._barrier.items():
                if s is not own and wt.get(s, 0) < v:
                    need[s] = v

        def add(sem, val, skip_own=False):
            if sem is own and skip_own:
                return
            if wt.get(sem, 0) >= val:
                return
            if need.get(sem, 0) < val:
                need[sem] = val

        for b in reads:
            for s, v in b.ready.items():
                add(s, v)
        for b in writes:
            for s, v in b.ready.items():
                add(s, v, skip_own=True)
            for s, v in b.readers.items():
                add(s, v)
        q = self.cur[engine]
        for s, v in need.items():
            self.trace.append(f"{engine}: WAIT {s.name if hasattr(s,'name') else s} >= {v}")
            q.append(lambda eng, s=s, v=v: eng.wait_ge(s, v))
            wt[s] = max(wt.get(s, 0), v)

    const_dep = None

    def op(self, engine, fn, reads=(), writes=()):
        if engine == "scalar" and self.const_dep is not None:
            reads = list(reads) + [self.const_dep]
        self._waits(engine, reads, writes)
        sem = self.csem[engine]
        self.ccount[engine] += 1
        cnt = self.ccount[engine]
        self.n_instr += 1
        self.trace.append(
            f"{engine}: OP#{cnt} r=[{','.join(b.name for b in reads)}] "
            f"w=[{','.join(b.name for b in writes)}]")
        self.cur[engine].append(
            lambda eng, fn=fn, sem=sem: fn(eng).then_inc(sem, 1))
        for b in writes:
            b.ready[sem] = cnt
        for b in reads:
            b.readers[sem] = cnt

    def dma(self, queue, fn, reads=(), writes=()):
        way = self.dissue[queue] % self.DMA_WAYS
        self.dissue[queue] += 1
        dsem = self.dsem[queue][way]
        prev = 16 * self.dcount[queue][way]
        self.dcount[queue][way] += 1
        val = prev + 16
        wt = self.waited[queue]
        if prev and wt.get(dsem, 0) < prev:
            self.cur[queue].append(
                lambda eng, s=dsem, v=prev: eng.wait_ge(s, v))
            wt[dsem] = prev
        self._waits(queue, reads, writes)
        self.n_instr += 1
        self.trace.append(
            f"{queue}: DMA#{way}:{val} r=[{','.join(b.name for b in reads)}] "
            f"w=[{','.join(b.name for b in writes)}]")
        self.cur[queue].append(
            lambda eng, fn=fn, dsem=dsem: fn(eng).then_inc(dsem, 16))
        for b in writes:
            b.ready[dsem] = val
        for b in reads:
            b.readers[dsem] = val

    def barrier_wait_all(self, engine):
        q = self.cur[engine]
        for e in ENGINES:
            if e == engine:
                continue
            s, v = self.csem[e], self.ccount[e]
            if v:
                q.append(lambda eng, s=s, v=v: eng.wait_ge(s, v))
        for qq in DMA_QUEUES:
            for i in range(self.DMA_WAYS):
                s, v = self.dsem[qq][i], 16 * self.dcount[qq][i]
                if v:
                    q.append(lambda eng, s=s, v=v: eng.wait_ge(s, v))


def build_program():
    nc = bass.Bass("TRN2", target_bir_lowering=False, debug=False)

    def dram(name, shape, dt, kind):
        return nc.dram_tensor(name, shape, dt, kind=kind).ap()

    EI, EO, IN = "ExternalInput", "ExternalOutput", "Internal"

    x_in = dram("x_in", [S, DM], F32, EI)
    wq = dram("wq", [DM, DM], F32, EI)
    wk = dram("wk", [DM, 512], F32, EI)
    wv = dram("wv", [DM, 512], F32, EI)
    wo = dram("wo", [DM, DM], F32, EI)
    wgate = dram("wgate", [INTER0, DM], F32, EI)
    wup = dram("wup", [INTER0, DM], F32, EI)
    wdown = dram("wdown", [DM, INTER0], F32, EI)
    lnw_row = dram("lnw_row", [1, INTER0], F32, EI)
    lnw_colt = dram("lnw_colt", [128, NJ0], F32, EI)
    wlat = dram("wlat", [DM, 128], F32, EI)
    blat = dram("blat", [128, 1], F32, EI)
    wd = dram("wd", [DM, DM], F32, EI)
    bdh = dram("bdh", [128, 16], F32, EI)
    wproj = dram("wproj", [DM, DM], F32, EI)
    bproj_row = dram("bproj_row", [1, DM], F32, EI)
    gatew = dram("gatew", [NEXP, DM], F32, EI)
    w1 = dram("w1", [NEXP, INTER1, DM], F32, EI)
    w2 = dram("w2", [NEXP, DM, INTER1], F32, EI)
    w3 = dram("w3", [NEXP, INTER1, DM], F32, EI)
    sw1 = dram("sw1", [INTER1, DM], F32, EI)
    sw2 = dram("sw2", [DM, INTER1], F32, EI)
    sw3 = dram("sw3", [INTER1, DM], F32, EI)
    ident_c = dram("ident_c", [128, 128], BF16, EI)
    identf_c = dram("identf_c", [128, 128], F32, EI)
    ppermt_c = dram("ppermt_c", [128, 128], BF16, EI)
    masks_c = dram("masks_c", [4 * 128, 512], BF16, EI)
    ones_c = dram("ones_c", [1, 128], F32, EI)
    csts_c = dram("csts_c", [128, 8], F32, EI)

    out = dram("out", [S, DM], F32, EO)
    xA = dram("xA", [S, DM], F32, EO if DBG else IN)
    xB = dram("xB", [S, DM], F32, EO if DBG else IN)
    xC = dram("xC", [S, DM], F32, EO if DBG else IN)

    wq_bf = dram("wq_bf", [DM, DM], BF16, IN)
    wk_bf = dram("wk_bf", [DM, 512], BF16, IN)
    wv_bf = dram("wv_bf", [DM, 512], BF16, IN)
    wo_bf = dram("wo_bf", [DM, DM], BF16, IN)
    wlat_bf = dram("wlat_bf", [DM, 128], BF16, IN)
    wd_bf = dram("wd_bf", [DM, DM], BF16, IN)
    wproj_bf = dram("wproj_bf", [DM, DM], BF16, IN)
    w1_bf = dram("w1_bf", [NEXP, INTER1, DM], BF16, IN)
    w2_bf = dram("w2_bf", [NEXP, DM, INTER1], BF16, IN)
    w3_bf = dram("w3_bf", [NEXP, INTER1, DM], BF16, IN)
    sw1_bf = dram("sw1_bf", [INTER1, DM], BF16, IN)
    sw2_bf = dram("sw2_bf", [DM, INTER1], BF16, IN)
    sw3_bf = dram("sw3_bf", [INTER1, DM], BF16, IN)
    tern_g = dram("tern_g", [INTER0, DM], BF16, IN)
    tern_u = dram("tern_u", [INTER0, DM], BF16, IN)
    tern_d = dram("tern_d", [DM, INTER0], BF16, IN)
    h_spill = dram("h_spill", [S, INTER0], BF16, IN)

    gs = contextlib.ExitStack()
    pr = Prog(nc, gs)

    BC = {}
    for nm in ("wq_bf", "wk_bf", "wv_bf", "wo_bf", "wlat_bf", "wd_bf",
               "wproj_bf", "w1_bf", "w2_bf", "w3_bf", "sw1_bf", "sw2_bf",
               "sw3_bf", "tern_g", "tern_u", "tern_d", "h_spill",
               "xA", "xB", "xC", "out", "x_in", "wgate", "wup", "wdown"):
        BC[nm] = Buf(nm)

    CONSTS = [0.25, math.pi / 2.0, 128.0, EPS_LN, EPS_RMS, 1.0]
    with gs:
        csts = pr.sb("csts", [128, 8], F32, scope="global")
        for i, v in enumerate(CONSTS):
            nc.const_aps.register(F32, v, csts[0][:, i:i + 1])
        pr.const_dep = csts[1]
        ident = pr.sb("ident", [128, 128], BF16, scope="global")
        identf = pr.sb("identf", [128, 128], F32, scope="global")
        ppermt = pr.sb("ppermt", [128, 128], BF16, scope="global")
        onesf = pr.sb("onesf", [1, 128], F32, scope="global")
        cos_t = pr.sb("cos_t", [128, S], F32, scope="global")
        sin_t = pr.sb("sin_t", [128, S], F32, scope="global")
        arena = pr.sb("arena", [128, 32768], BF16, scope="global")
        wm_keep = pr.sb("wm_keep", [128, 3], F32, scope="global")
        sx_keep = pr.sb("sx_keep", [128, NTS], F32, scope="global")
        isx_keep = pr.sb("isx_keep", [128, NTS], F32, scope="global")
        hstat = pr.sb("hstat", [128, 6 * NTS], F32, scope="global")
        crows = pr.sb("crows", [1, S], F32, scope="global")
        cmb = pr.sb("cmb", [128, 16 * 16], F32, scope="global")

        def A(c, lo, n):
            return arena[0][:, c * S + lo: c * S + lo + n]

        def emit_round(dst_ap, dst_buf, src_ap, src_buf, scr1, scr2, width,
                       base=128.0, nbits=8):
            d_h, d_b = scr1
            c_h, c_b = scr2
            pr.op("vector", lambda eng: eng.tensor_single_scalar(
                d_h[:, 0:width], src_ap, base, ALU.add),
                reads=[src_buf], writes=[d_b])
            bit = base
            for _ in range(nbits):
                pr.op("vector", lambda eng, bit=bit: eng.tensor_single_scalar(
                    c_h[:, 0:width], d_h[:, 0:width], bit - 0.5, ALU.is_ge),
                    reads=[d_b], writes=[c_b])
                pr.op("vector", lambda eng, bit=bit: eng.scalar_tensor_tensor(
                    d_h[:, 0:width], c_h[:, 0:width], -bit, d_h[:, 0:width],
                    ALU.mult, ALU.add),
                    reads=[c_b, d_b], writes=[d_b])
                bit *= 0.5
            pr.op("vector", lambda eng: eng.scalar_tensor_tensor(
                dst_ap, d_h[:, 0:width], -1.0, src_ap, ALU.mult, ALU.add),
                reads=[d_b, src_buf], writes=[dst_buf])

        def emit_tr128(ptps, dst_ap, dst_buf, src_ap, src_buf,
                       evac="scalar", idt=None):
            pt_h, pt_b = ptps
            idt = idt or ident
            pr.op("tensor", lambda eng: eng.transpose(
                pt_h[:, :], src_ap, idt[0][:, :]),
                reads=[src_buf, idt[1]], writes=[pt_b])
            if evac == "scalar":
                pr.op("scalar", lambda eng: eng.copy(dst_ap, pt_h[:, :]),
                      reads=[pt_b], writes=[dst_buf])
            else:
                pr.op(evac, lambda eng: eng.tensor_copy(dst_ap, pt_h[:, :]),
                      reads=[pt_b], writes=[dst_buf])

        with pr.phase("setup"):
            pr.dma("gpsimd", lambda eng: eng.dma_start(
                out=csts[0][:, :], in_=csts_c), writes=[csts[1]])
            for (dst, src, b) in ((ident, ident_c, None), (identf, identf_c, None),
                                  (ppermt, ppermt_c, None), (onesf, ones_c, None)):
                pr.dma("gpsimd", lambda eng, dst=dst, src=src: eng.dma_start(
                    out=dst[0][:, :], in_=src), writes=[dst[1]])

            def cast(dst, src, dbuf, split=1):
                step = src.shape[0] // split
                for i in range(split):
                    pr.dma("gpsimd", lambda eng, i=i, dst=dst, src=src, step=step:
                           eng.dma_start(out=dst[i * step:(i + 1) * step],
                                         in_=src[i * step:(i + 1) * step],
                                         max_dma_last_dim=8192), writes=[dbuf])

            KCAST = int(os.environ.get("KCAST", "1"))
            if KCAST:
                cast(wq_bf, wq, BC["wq_bf"])
            if KCAST:
                cast(wk_bf, wk, BC["wk_bf"])
                cast(wv_bf, wv, BC["wv_bf"])
                cast(wo_bf, wo, BC["wo_bf"])
                cast(wlat_bf, wlat, BC["wlat_bf"])
                cast(wd_bf, wd, BC["wd_bf"])
                cast(wproj_bf, wproj, BC["wproj_bf"])
                for i in range(NEXP):
                    cast(w1_bf[i], w1[i], BC["w1_bf"])
                    cast(w3_bf[i], w3[i], BC["w3_bf"])
                    cast(w2_bf[i], w2[i], BC["w2_bf"])
                cast(sw1_bf, sw1, BC["sw1_bf"])
                cast(sw3_bf, sw3, BC["sw3_bf"])
                cast(sw2_bf, sw2, BC["sw2_bf"])

            dcol = pr.sb("dcol", [128, 2], F32)
            invf = pr.sb("invf", [128, 1], F32)
            freqs = pr.sb("freqs", [128, S], F32)
            ts1 = pr.sb("ts1", [128, S], F32)
            ts2 = pr.sb("ts2", [128, S], F32)
            ts3 = pr.sb("ts3", [128, S], F32)
            ts4 = pr.sb("ts4", [128, S], F32)
            pr.op("gpsimd", lambda eng: eng.iota(
                dcol[0][:, 0:1], [[0, 1]], channel_multiplier=1,
                allow_small_or_imprecise_dtypes=True), writes=[dcol[1]])
            pr.op("vector", lambda eng: eng.tensor_scalar(
                dcol[0][:, 1:2], dcol[0][:, 0:1], 64.0, 64.0, ALU.is_ge, ALU.mult),
                reads=[dcol[1]], writes=[dcol[1]])
            pr.op("vector", lambda eng: eng.tensor_tensor(
                dcol[0][:, 0:1], dcol[0][:, 0:1], dcol[0][:, 1:2], ALU.subtract),
                reads=[dcol[1]], writes=[dcol[1]])
            pr.op("scalar", lambda eng: eng.activation(
                invf[0][:, :], dcol[0][:, 0:1], ACT.Exp, bias=0.0,
                scale=-math.log(10000.0) / 64.0), reads=[dcol[1]], writes=[invf[1]])
            pr.op("gpsimd", lambda eng: eng.iota(
                ts1[0][:, :], [[1, S]], channel_multiplier=0,
                allow_small_or_imprecise_dtypes=True), writes=[ts1[1]])
            pr.op("scalar", lambda eng: eng.activation(
                freqs[0][:, :], ts1[0][:, :], ACT.Identity, bias=0.0,
                scale=invf[0][:, :]), reads=[ts1[1], invf[1]], writes=[freqs[1]])
            clampv = float(np.float32(3.1415925))
            twopi = 2.0 * math.pi

            def table(dst, shift):
                pr.op("scalar", lambda eng: eng.activation(
                    ts1[0][:, :], freqs[0][:, :], ACT.Identity,
                    bias=shift / twopi, scale=1.0 / twopi),
                    reads=[freqs[1]], writes=[ts1[1]])
                emit_round(ts2[0][:, :], ts2[1], ts1[0][:, :], ts1[1],
                           ts3, ts4, S, base=512.0, nbits=10)
                pr.op("scalar", lambda eng: eng.activation(
                    ts1[0][:, :], freqs[0][:, :], ACT.Identity,
                    bias=shift, scale=1.0), reads=[freqs[1]], writes=[ts1[1]])
                pr.op("vector", lambda eng: eng.scalar_tensor_tensor(
                    ts1[0][:, :], ts2[0][:, :], -twopi, ts1[0][:, :],
                    ALU.mult, ALU.add), reads=[ts2[1], ts1[1]], writes=[ts1[1]])
                pr.op("vector", lambda eng: eng.tensor_single_scalar(
                    ts1[0][:, :], ts1[0][:, :], clampv, ALU.min),
                    reads=[ts1[1]], writes=[ts1[1]])
                pr.op("vector", lambda eng: eng.tensor_single_scalar(
                    ts1[0][:, :], ts1[0][:, :], -clampv, ALU.max),
                    reads=[ts1[1]], writes=[ts1[1]])
                pr.op("scalar", lambda eng: eng.activation(
                    dst[0][:, :], ts1[0][:, :], ACT.Sin, bias=0.0, scale=1.0),
                    reads=[ts1[1]], writes=[dst[1]])

            if int(os.environ.get("KTAB", "1")):
                table(sin_t, 0.0)
                table(cos_t, math.pi / 2.0)

        TERN_ON = int(os.environ.get("KTERN", "1"))
        with pr.phase("ternall"):
          wt = [pr.sb("wt", [128, 2048], F32) for _ in range(2)]
          acc = pr.sb("acc", [128, 1], F32)
          tsum = pr.sb("tsum", [128, 2], F32)
          scol = pr.sb("scol", [128, 2], F32)
          sred = pr.sb("sred", [1, 2], F32)
          srps = pr.ps("srps", [1, 512], F32)
          brps = pr.ps("brps", [128, 512], F32)
          t1 = pr.sb("t1", [128, 2048], F32)
          t2 = pr.sb("t2", [128, 2048], F32)
          to = [pr.sb("to", [128, 2048], BF16) for _ in range(2)]
          for idx, (src, sbuf_, tdst, tbuf, nrow, ncol) in enumerate(() if not TERN_ON else (
                (wgate, BC["wgate"], tern_g, BC["tern_g"], INTER0, DM),
                (wup, BC["wup"], tern_u, BC["tern_u"], INTER0, DM),
                (wdown, BC["wdown"], tern_d, BC["tern_d"], DM, INTER0))):
            if True:
                nrt, nct = nrow // 128, ncol // 2048
                pr.op("vector", lambda eng: eng.memset(acc[0][:, :], 0.0),
                      writes=[acc[1]])
                for it in range(nrt * nct):
                    r, c = it % nrt, it // nrt
                    w = wt[it % 2]
                    pr.dma("scalar", lambda eng, r=r, c=c, w=w, src=src: eng.dma_start(
                        out=w[0][:, :],
                        in_=src[r * 128:(r + 1) * 128, c * 2048:(c + 1) * 2048]),
                        reads=[sbuf_], writes=[w[1]])
                    pr.op("vector", lambda eng, w=w: eng.reduce_sum(
                        tsum[0][:, 0:1], w[0][:, :], axis=AX.X,
                        apply_absolute_value=True), reads=[w[1]], writes=[tsum[1]])
                    pr.op("vector", lambda eng: eng.tensor_tensor(
                        acc[0][:, :], acc[0][:, :], tsum[0][:, 0:1], ALU.add),
                        reads=[acc[1], tsum[1]], writes=[acc[1]])
                pr.op("tensor", lambda eng: eng.matmul(
                    srps[0][:, 0:1], csts[0][:, 5:6], acc[0][:, :],
                    start=True, stop=True),
                    reads=[csts[1], acc[1]], writes=[srps[1]])
                pr.op("scalar", lambda eng: eng.copy(
                    sred[0][:, 0:1], srps[0][:, 0:1]),
                    reads=[srps[1]], writes=[sred[1]])
                pr.op("vector", lambda eng: eng.tensor_scalar(
                    sred[0][:, 1:2], sred[0][:, 0:1], 1.0 / (nrow * ncol),
                    EPS_Q, ALU.mult, ALU.max), reads=[sred[1]], writes=[sred[1]])
                pr.op("tensor", lambda eng: eng.matmul(
                    brps[0][:, 0:1], onesf[0][:, :], sred[0][0:1, 1:2],
                    start=True, stop=True),
                    reads=[onesf[1], sred[1]], writes=[brps[1]])
                pr.op("scalar", lambda eng, idx=idx: eng.copy(
                    wm_keep[0][:, idx:idx + 1], brps[0][:, 0:1]),
                    reads=[brps[1]], writes=[wm_keep[1]])
                pr.op("vector", lambda eng, idx=idx: eng.reciprocal(
                    scol[0][:, 0:1], wm_keep[0][:, idx:idx + 1]),
                    reads=[wm_keep[1]], writes=[scol[1]])
                pr.op("vector", lambda eng: eng.tensor_single_scalar(
                    scol[0][:, 1:2], scol[0][:, 0:1], -1.0, ALU.mult),
                    reads=[scol[1]], writes=[scol[1]])
                for it in range(nrt * nct):
                    r, c = it % nrt, it // nrt
                    w = wt[it % 2]
                    o = to[it % 2]
                    pr.dma("scalar", lambda eng, r=r, c=c, w=w, src=src: eng.dma_start(
                        out=w[0][:, :],
                        in_=src[r * 128:(r + 1) * 128, c * 2048:(c + 1) * 2048]),
                        reads=[sbuf_], writes=[w[1]])
                    pr.op("vector", lambda eng, w=w: eng.tensor_scalar(
                        t1[0][:, :], w[0][:, :], scol[0][:, 0:1], 0.5,
                        ALU.mult, ALU.is_gt), reads=[w[1], scol[1]], writes=[t1[1]])
                    pr.op("vector", lambda eng, w=w: eng.tensor_scalar(
                        t2[0][:, :], w[0][:, :], scol[0][:, 1:2], 0.5,
                        ALU.mult, ALU.is_gt), reads=[w[1], scol[1]], writes=[t2[1]])
                    pr.op("gpsimd", lambda eng, o=o: eng.tensor_tensor(
                        o[0][:, :], t1[0][:, :], t2[0][:, :], ALU.subtract),
                        reads=[t1[1], t2[1]], writes=[o[1]])
                    pr.dma("gpsimd", lambda eng, r=r, c=c, o=o, tdst=tdst: eng.dma_start(
                        out=tdst[r * 128:(r + 1) * 128, c * 2048:(c + 1) * 2048],
                        in_=o[0][:, :]), reads=[o[1]], writes=[tbuf])

        def make_cols(quant):
            cols = {}
            for nm in ("rstd", "mb", "t1", "t2"):
                cols[nm] = pr.sb("col_" + nm, [128, NTS], F32)
            if quant:
                for nm in ("mx", "sx", "isx"):
                    cols[nm] = pr.sb("col_" + nm, [128, NTS], F32)
            return cols

        def emit_ln_stats(x_ap, x_buf, sq, cols, ci):
            c = slice(ci, ci + 1)
            r, m, t1c, t2c = cols["rstd"], cols["mb"], cols["t1"], cols["t2"]
            pr.op("vector", lambda eng: eng.reduce_sum(
                t1c[0][:, c], x_ap, axis=AX.X), reads=[x_buf], writes=[t1c[1]])
            pr.op("scalar", lambda eng: eng.activation(
                sq[0][:, :], x_ap, ACT.Square, bias=0.0, scale=1.0,
                accum_out=t2c[0][:, c]), reads=[x_buf], writes=[sq[1], t2c[1]])
            pr.op("vector", lambda eng: eng.tensor_single_scalar(
                t1c[0][:, c], t1c[0][:, c], 1.0 / DM, ALU.mult),
                reads=[t1c[1]], writes=[t1c[1]])
            pr.op("vector", lambda eng: eng.tensor_tensor(
                m[0][:, c], t1c[0][:, c], t1c[0][:, c], ALU.mult),
                reads=[t1c[1]], writes=[m[1]])
            pr.op("vector", lambda eng: eng.scalar_tensor_tensor(
                t2c[0][:, c], t2c[0][:, c], 1.0 / DM, m[0][:, c],
                ALU.mult, ALU.subtract), reads=[t2c[1], m[1]], writes=[t2c[1]])
            pr.op("scalar", lambda eng: eng.activation(
                t2c[0][:, c], t2c[0][:, c], ACT.Sqrt, bias=EPS_LN, scale=1.0),
                reads=[t2c[1]], writes=[t2c[1]])
            pr.op("vector", lambda eng: eng.reciprocal(
                r[0][:, c], t2c[0][:, c]), reads=[t2c[1]], writes=[r[1]])
            pr.op("vector", lambda eng: eng.scalar_tensor_tensor(
                m[0][:, c], t1c[0][:, c], -1.0, r[0][:, c], ALU.mult, ALU.mult),
                reads=[t1c[1], r[1]], writes=[m[1]])

        def emit_ln_to_fm(xsrc, xsrc_buf, quant, cols, ptps):
            xt = [pr.sb("xt", [128, DM], F32) for _ in range(2)]
            sq = pr.sb("sq", [128, DM], F32)
            xl = pr.sb("xl", [128, DM], F32)
            xlb = pr.sb("xlb", [128, DM], BF16)
            r1 = pr.sb("r1", [128, DM], F32)
            r2 = pr.sb("r2", [128, DM], F32)
            for ts in range(NTS):
                x = xt[ts % 2]
                pr.dma("sync", lambda eng, ts=ts, x=x: eng.dma_start(
                    out=x[0][:, :], in_=xsrc[ts * 128:(ts + 1) * 128, :]),
                    reads=[xsrc_buf], writes=[x[1]])
                emit_ln_stats(x[0][:, :], x[1], sq, cols, ts)
                if not quant:
                    pr.op("scalar", lambda eng, ts=ts, x=x: eng.activation(
                        xlb[0][:, :], x[0][:, :], ACT.Identity,
                        bias=cols["mb"][0][:, ts:ts + 1],
                        scale=cols["rstd"][0][:, ts:ts + 1]),
                        reads=[x[1], cols["mb"][1], cols["rstd"][1]],
                        writes=[xlb[1]])
                else:
                    mx, sx, isx = cols["mx"], cols["sx"], cols["isx"]
                    pr.op("scalar", lambda eng, ts=ts, x=x: eng.activation(
                        xl[0][:, :], x[0][:, :], ACT.Identity,
                        bias=cols["mb"][0][:, ts:ts + 1],
                        scale=cols["rstd"][0][:, ts:ts + 1]),
                        reads=[x[1], cols["mb"][1], cols["rstd"][1]],
                        writes=[xl[1]])
                    pr.op("vector", lambda eng, ts=ts: eng.reduce_max(
                        mx[0][:, ts:ts + 1], xl[0][:, :], axis=AX.X,
                        apply_absolute_value=True), reads=[xl[1]], writes=[mx[1]])
                    pr.op("vector", lambda eng, ts=ts: eng.tensor_single_scalar(
                        mx[0][:, ts:ts + 1], mx[0][:, ts:ts + 1], EPS_Q, ALU.max),
                        reads=[mx[1]], writes=[mx[1]])
                    pr.op("vector", lambda eng, ts=ts: eng.tensor_single_scalar(
                        isx[0][:, ts:ts + 1], mx[0][:, ts:ts + 1], 1.0 / 127.0,
                        ALU.mult), reads=[mx[1]], writes=[isx[1]])
                    pr.op("vector", lambda eng, ts=ts: eng.reciprocal(
                        sx[0][:, ts:ts + 1], isx[0][:, ts:ts + 1]),
                        reads=[isx[1]], writes=[sx[1]])
                    pr.op("scalar", lambda eng, ts=ts: eng.activation(
                        xl[0][:, :], xl[0][:, :], ACT.Identity, bias=0.0,
                        scale=sx[0][:, ts:ts + 1]),
                        reads=[xl[1], sx[1]], writes=[xl[1]])
                    emit_round(xlb[0][:, :], xlb[1], xl[0][:, :], xl[1],
                               r1, r2, DM)
                for c in range(NCH):
                    emit_tr128(ptps, A(c, ts * 128, 128), arena[1],
                               xlb[0][:, c * 128:(c + 1) * 128], xlb[1])

        def attention_block(name, xsrc, xsrc_buf, xdst, xdst_buf, is_l0):
            NH = 16
            KSUB = int(os.environ.get("KSUB", "99"))
            KQC = int(os.environ.get("KQC", "99"))
            KH = int(os.environ.get("KH", "99"))
            if KSUB < 2:
                return
            with contextlib.ExitStack() as ost:
                if is_l0:
                    kst = pr.sb_on(ost, "kst", [128, 4 * S], BF16)
                    vst = pr.sb_on(ost, "vst", [128, NTS * 512], BF16)
                else:
                    kst = pr.sb_on(ost, "kst", [128, S], BF16)
                    vst = pr.sb_on(ost, "vst", [128, NTS * 128], BF16)
                    bpb = pr.sb_on(ost, "bpb", [128, DM], F32)
                with pr.phase(name + "_kv"):
                    cols = make_cols(False)
                    ptps = pr.ps("ptp", [128, 128], BF16)
                    if int(os.environ.get("KLN", "1")):
                        emit_ln_to_fm(xsrc, xsrc_buf, False, cols, ptps)
                    kps = pr.ps("kps", [128, 512], F32)
                    kps2 = pr.ps("kps2", [128, 512], F32)
                    krawb = pr.sb("krawb", [128, 512], BF16)
                    t1s = pr.sb("t1s", [128, 512], F32)
                    t2s = pr.sb("t2s", [128, 512], F32)
                    wct = [pr.sb("wct", [128, 512], BF16) for _ in range(NCH)]
                    KKV = int(os.environ.get("KKV", "99"))
                    if is_l0:
                        for c in range(NCH):
                            pr.dma("scalar", lambda eng, c=c: eng.dma_start(
                                out=wct[c][0][:, :],
                                in_=wk_bf[c * 128:(c + 1) * 128, :]),
                                reads=[BC["wk_bf"]], writes=[wct[c][1]])
                        KIT = int(os.environ.get("KIT", "99"))
                        KROPE = int(os.environ.get("KROPE", "1"))
                        it_count = 0
                        for kv in range(4 if KKV >= 2 else 0):
                            for tc in range(NTC):
                                it_count += 1
                                if it_count > KIT:
                                    continue
                                for c in range(NCH):
                                    pr.op("tensor", lambda eng, kv=kv, tc=tc, c=c: eng.matmul(
                                        kps[0][:, :],
                                        wct[c][0][:, kv * 128:(kv + 1) * 128],
                                        A(c, tc * 512, 512),
                                        start=(c == 0), stop=(c == NCH - 1)),
                                        reads=[wct[c][1], arena[1]],
                                        writes=[kps[1]] + ([kps2[1]] if c == 0 else []))
                                pr.op("scalar", lambda eng: eng.copy(
                                    krawb[0][:, :], kps[0][:, :]),
                                    reads=[kps[1]], writes=[krawb[1]])
                                pr.op("tensor", lambda eng: eng.matmul(
                                    kps2[0][:, :], ppermt[0][:, :], krawb[0][:, :],
                                    start=True, stop=True),
                                    reads=[ppermt[1], krawb[1]], writes=[kps2[1]])
                                pr.op("vector", lambda eng, tc=tc: eng.tensor_tensor(
                                    t1s[0][:, :], krawb[0][:, :],
                                    cos_t[0][:, tc * 512:(tc + 1) * 512], ALU.mult),
                                    reads=[krawb[1], cos_t[1]], writes=[t1s[1]])
                                pr.op("vector", lambda eng, tc=tc: eng.tensor_tensor(
                                    t2s[0][:, :], kps2[0][:, :],
                                    sin_t[0][:, tc * 512:(tc + 1) * 512], ALU.mult),
                                    reads=[kps2[1], sin_t[1]], writes=[t2s[1]])
                                pr.op("vector", lambda eng, kv=kv, tc=tc: eng.tensor_tensor(
                                    kst[0][:, kv * S + tc * 512: kv * S + (tc + 1) * 512],
                                    t1s[0][:, :], t2s[0][:, :], ALU.add),
                                    reads=[t1s[1], t2s[1]], writes=[kst[1]])
                        for c in range(NCH):
                            pr.dma("scalar", lambda eng, c=c: eng.dma_start(
                                out=wct[c][0][:, :],
                                in_=wv_bf[c * 128:(c + 1) * 128, :]),
                                reads=[BC["wv_bf"]], writes=[wct[c][1]])
                        for ts in range(NTS if KKV >= 3 else 0):
                            for c in range(NCH):
                                pr.op("tensor", lambda eng, ts=ts, c=c: eng.matmul(
                                    kps[0][:, :], A(c, ts * 128, 128),
                                    wct[c][0][:, :],
                                    start=(c == 0), stop=(c == NCH - 1)),
                                    reads=[arena[1], wct[c][1]], writes=[kps[1]])
                            pr.op("scalar", lambda eng, ts=ts: eng.copy(
                                vst[0][:, ts * 512:(ts + 1) * 512], kps[0][:, :]),
                                reads=[kps[1]], writes=[vst[1]])
                    else:
                        biascol0 = pr.sb("biascol0", [128, 1], F32)
                        pr.dma("gpsimd", lambda eng: eng.dma_start(
                            out=biascol0[0][:, :], in_=blat), writes=[biascol0[1]])
                        for c in range(NCH):
                            pr.dma("scalar", lambda eng, c=c: eng.dma_start(
                                out=wct[c][0][:, 0:128],
                                in_=wlat_bf[c * 128:(c + 1) * 128, :]),
                                reads=[BC["wlat_bf"]], writes=[wct[c][1]])
                        for tc in range(NTC):
                            for c in range(NCH):
                                pr.op("tensor", lambda eng, tc=tc, c=c: eng.matmul(
                                    kps[0][:, :], wct[c][0][:, 0:128],
                                    A(c, tc * 512, 512),
                                    start=(c == 0), stop=(c == NCH - 1)),
                                    reads=[wct[c][1], arena[1]], writes=[kps[1]])
                            pr.op("scalar", lambda eng, tc=tc: eng.activation(
                                kst[0][:, tc * 512:(tc + 1) * 512], kps[0][:, :],
                                ACT.Identity, bias=biascol0[0][:, 0:1], scale=1.0),
                                reads=[kps[1], biascol0[1]], writes=[kst[1]])
                        for ks in range(NTS):
                            emit_tr128(ptps, vst[0][:, ks * 128:(ks + 1) * 128],
                                       vst[1], kst[0][:, ks * 128:(ks + 1) * 128],
                                       kst[1])
                        bpr = pr.sb("bpr", [1, DM], F32)
                        pr.dma("gpsimd", lambda eng: eng.dma_start(
                            out=bpr[0][:, :], in_=bproj_row), writes=[bpr[1]])
                        for dc in range(NTC):
                            pr.op("tensor", lambda eng, dc=dc: eng.matmul(
                                kps[0][:, :], onesf[0][:, :],
                                bpr[0][0:1, dc * 512:(dc + 1) * 512],
                                start=True, stop=True),
                                reads=[onesf[1], bpr[1]], writes=[kps[1]])
                            pr.op("scalar", lambda eng, dc=dc: eng.copy(
                                bpb[0][:, dc * 512:(dc + 1) * 512], kps[0][:, :]),
                                reads=[kps[1]], writes=[bpb[1]])

                if KSUB < 3:
                    return
                with pr.phase(name + "_qa"):
                    ptps = pr.ps("ptp", [128, 128], BF16)
                    scps = [pr.ps("scps", [128, 512], F32) for _ in range(2)]
                    yps = pr.ps("yps", [128, 512], F32)
                    qps = pr.ps("qps", [128, 512], F32)
                    qps2 = pr.ps("qps2", [128, 512], F32)
                    masks = pr.sb("masks", [128, 4 * 512], BF16)
                    for i in range(4):
                        pr.dma("gpsimd", lambda eng, i=i: eng.dma_start(
                            out=masks[0][:, i * 512:(i + 1) * 512],
                            in_=masks_c[i * 128:(i + 1) * 128, :]),
                            writes=[masks[1]])
                    t1s = pr.sb("t1s", [128, 512], F32)
                    t2s = pr.sb("t2s", [128, 512], F32)
                    biascol = pr.sb("biascol", [128, 16], F32)
                    if not is_l0:
                        pr.dma("gpsimd", lambda eng: eng.dma_start(
                            out=biascol[0][:, :], in_=bdh), writes=[biascol[1]])
                    qbf = pr.sb("qbf", [128, 512], BF16)
                    qraw = pr.sb("qraw", [128, 512], BF16)
                    msc = pr.sb("msc", [128, 2048], F32)
                    pnb = pr.sb("pnb", [128, 2048], BF16)
                    ptst = pr.sb("ptst", [128, 16 * 512], BF16)
                    aoc = pr.sb("aoc", [128, NH * 512], BF16)
                    rowm = pr.sb("rowm", [128, 4], F32)
                    wqall = pr.sb("wqall", [128, NCH * 128], BF16)
                    woall = pr.sb("woall", [128, NCH * 512], BF16)
                    xres = [pr.sb("xres", [128, 512], F32) for _ in range(2)]
                    oacc = pr.sb("oacc", [128, 512], F32)
                    wsrc = wq_bf if is_l0 else wd_bf
                    wsrcb = BC["wq_bf"] if is_l0 else BC["wd_bf"]
                    wob = wo_bf if is_l0 else wproj_bf
                    wobuf = BC["wo_bf"] if is_l0 else BC["wproj_bf"]
                    for qc in range(min(NTC, KQC)):
                        nkc = qc + 1
                        for h in range(min(NH, KH)):
                            pr.dma("scalar", lambda eng, h=h: eng.dma_start(
                                out=wqall[0][:, :].rearrange(
                                    "p (c f) -> p c f", f=128),
                                in_=wsrc[:, h * 128:(h + 1) * 128].rearrange(
                                    "(c p) f -> p c f", p=128)),
                                reads=[wsrcb], writes=[wqall[1]])
                            for c in range(NCH):
                                pr.op("tensor", lambda eng, c=c, qc=qc: eng.matmul(
                                    qps[0][:, :],
                                    wqall[0][:, c * 128:(c + 1) * 128],
                                    A(c, qc * 512, 512),
                                    start=(c == 0), stop=(c == NCH - 1)),
                                    reads=[wqall[1], arena[1]], writes=[qps[1]])
                            if is_l0:
                                pr.op("scalar", lambda eng: eng.copy(
                                    qraw[0][:, :], qps[0][:, :]),
                                    reads=[qps[1]], writes=[qraw[1]])
                                pr.op("tensor", lambda eng: eng.matmul(
                                    qps2[0][:, :], ppermt[0][:, :], qraw[0][:, :],
                                    start=True, stop=True),
                                    reads=[ppermt[1], qraw[1]], writes=[qps2[1]])
                                pr.op("vector", lambda eng, qc=qc: eng.tensor_tensor(
                                    t1s[0][:, :], qraw[0][:, :],
                                    cos_t[0][:, qc * 512:(qc + 1) * 512], ALU.mult),
                                    reads=[qraw[1], cos_t[1]], writes=[t1s[1]])
                                pr.op("vector", lambda eng, qc=qc: eng.tensor_tensor(
                                    t2s[0][:, :], qps2[0][:, :],
                                    sin_t[0][:, qc * 512:(qc + 1) * 512], ALU.mult),
                                    reads=[qps2[1], sin_t[1]], writes=[t2s[1]])
                                pr.op("vector", lambda eng: eng.tensor_tensor(
                                    qbf[0][:, :], t1s[0][:, :], t2s[0][:, :], ALU.add),
                                    reads=[t1s[1], t2s[1]], writes=[qbf[1]])
                            else:
                                pr.op("scalar", lambda eng, h=h: eng.activation(
                                    qbf[0][:, :], qps[0][:, :], ACT.Identity,
                                    bias=biascol[0][:, h:h + 1], scale=1.0),
                                    reads=[qps[1], biascol[1]], writes=[qbf[1]])
                            kv = h // 4 if is_l0 else 0
                            koff = kv * S if is_l0 else 0
                            for qs in range(4):
                                for kc in range(nkc):
                                    sp = scps[kc % 2]
                                    pr.op("tensor", lambda eng, qs=qs, kc=kc, sp=sp, koff=koff: eng.matmul(
                                        sp[0][:, :],
                                        qbf[0][:, qs * 128:(qs + 1) * 128],
                                        kst[0][:, koff + kc * 512: koff + (kc + 1) * 512],
                                        start=True, stop=True),
                                        reads=[qbf[1], kst[1]], writes=[sp[1]])
                                    if kc == nkc - 1:
                                        pr.op("vector", lambda eng, qs=qs, kc=kc, sp=sp: eng.scalar_tensor_tensor(
                                            msc[0][:, kc * 512:(kc + 1) * 512],
                                            sp[0][:, :], ISQ,
                                            masks[0][:, qs * 512:(qs + 1) * 512],
                                            ALU.mult, ALU.add),
                                            reads=[sp[1], masks[1]], writes=[msc[1]])
                                    else:
                                        pr.op("vector", lambda eng, kc=kc, sp=sp: eng.tensor_single_scalar(
                                            msc[0][:, kc * 512:(kc + 1) * 512],
                                            sp[0][:, :], ISQ, ALU.mult),
                                            reads=[sp[1]], writes=[msc[1]])
                                W = nkc * 512
                                pr.op("vector", lambda eng, W=W: eng.reduce_max(
                                    rowm[0][:, 0:1], msc[0][:, 0:W], axis=AX.X),
                                    reads=[msc[1]], writes=[rowm[1]])
                                pr.op("vector", lambda eng: eng.tensor_single_scalar(
                                    rowm[0][:, 1:2], rowm[0][:, 0:1], -1.0, ALU.mult),
                                    reads=[rowm[1]], writes=[rowm[1]])
                                pr.op("scalar", lambda eng, W=W: eng.activation(
                                    pnb[0][:, 0:W], msc[0][:, 0:W], ACT.Exp,
                                    bias=rowm[0][:, 1:2], scale=1.0,
                                    accum_out=rowm[0][:, 2:3]),
                                    reads=[msc[1], rowm[1]], writes=[pnb[1], rowm[1]])
                                pr.op("scalar", lambda eng: eng.activation(
                                    rowm[0][:, 3:4], rowm[0][:, 2:3], ACT.Ln,
                                    bias=0.0, scale=1.0),
                                    reads=[rowm[1]], writes=[rowm[1]])
                                pr.op("vector", lambda eng: eng.tensor_tensor(
                                    rowm[0][:, 3:4], rowm[0][:, 1:2],
                                    rowm[0][:, 3:4], ALU.subtract),
                                    reads=[rowm[1]], writes=[rowm[1]])
                                pr.op("scalar", lambda eng, W=W: eng.activation(
                                    pnb[0][:, 0:W], msc[0][:, 0:W], ACT.Exp,
                                    bias=rowm[0][:, 3:4], scale=1.0),
                                    reads=[msc[1], rowm[1]], writes=[pnb[1]])
                                for ks in range(nkc * 4):
                                    emit_tr128(
                                        ptps,
                                        ptst[0][:, ks * 512 + qs * 128:
                                                ks * 512 + (qs + 1) * 128],
                                        ptst[1],
                                        pnb[0][:, ks * 128:(ks + 1) * 128], pnb[1])
                            nks = nkc * 4
                            for ks in range(nks):
                                voff = (ks * 512 + kv * 128) if is_l0 else (ks * 128)
                                pr.op("tensor", lambda eng, ks=ks, voff=voff, nks=nks: eng.matmul(
                                    yps[0][:, :], vst[0][:, voff: voff + 128],
                                    ptst[0][:, ks * 512:(ks + 1) * 512],
                                    start=(ks == 0), stop=(ks == nks - 1)),
                                    reads=[vst[1], ptst[1]], writes=[yps[1]])
                            pr.op("scalar", lambda eng, h=h: eng.copy(
                                aoc[0][:, h * 512:(h + 1) * 512], yps[0][:, :]),
                                reads=[yps[1]], writes=[aoc[1]])
                        for dc in range(NTC):
                            pr.dma("scalar", lambda eng, dc=dc: eng.dma_start(
                                out=woall[0][:, :].rearrange(
                                    "p (c f) -> p c f", f=512),
                                in_=wob[:, dc * 512:(dc + 1) * 512].rearrange(
                                    "(c p) f -> p c f", p=128)),
                                reads=[wobuf], writes=[woall[1]])
                            for qs in range(4):
                                sp = scps[qs % 2]
                                for c in range(NCH):
                                    pr.op("tensor", lambda eng, c=c, qs=qs, sp=sp: eng.matmul(
                                        sp[0][:, :],
                                        aoc[0][:, c * 512 + qs * 128:
                                                c * 512 + (qs + 1) * 128],
                                        woall[0][:, c * 512:(c + 1) * 512],
                                        start=(c == 0), stop=(c == NCH - 1)),
                                        reads=[aoc[1], woall[1]], writes=[sp[1]])
                                ts = qc * 4 + qs
                                xr = xres[qs % 2]
                                pr.dma("sync", lambda eng, ts=ts, dc=dc, xr=xr: eng.dma_start(
                                    out=xr[0][:, :],
                                    in_=xsrc[ts * 128:(ts + 1) * 128,
                                             dc * 512:(dc + 1) * 512]),
                                    reads=[xsrc_buf], writes=[xr[1]])
                                if is_l0:
                                    pr.op("vector", lambda eng, sp=sp, xr=xr: eng.tensor_tensor(
                                        oacc[0][:, :], sp[0][:, :], xr[0][:, :],
                                        ALU.add), reads=[sp[1], xr[1]], writes=[oacc[1]])
                                else:
                                    pr.op("vector", lambda eng, sp=sp, dc=dc: eng.tensor_tensor(
                                        oacc[0][:, :], sp[0][:, :],
                                        bpb[0][:, dc * 512:(dc + 1) * 512], ALU.add),
                                        reads=[sp[1], bpb[1]], writes=[oacc[1]])
                                    pr.op("vector", lambda eng, xr=xr: eng.tensor_tensor(
                                        oacc[0][:, :], oacc[0][:, :], xr[0][:, :],
                                        ALU.add), reads=[oacc[1], xr[1]], writes=[oacc[1]])
                                pr.dma("sync", lambda eng, ts=ts, dc=dc: eng.dma_start(
                                    out=xdst[ts * 128:(ts + 1) * 128,
                                             dc * 512:(dc + 1) * 512],
                                    in_=oacc[0][:, :]),
                                    reads=[oacc[1]], writes=[xdst_buf])

        STAGES = int(os.environ.get("KSTAGES", "99"))
        if STAGES >= 2:
            attention_block("l0a", x_in, BC["x_in"], xA, BC["xA"], True)

        def emit_l0f():
            with pr.phase("l0f_qx"):
                cols0 = make_cols(True)
                ptps = pr.ps("ptp", [128, 128], BF16)
                emit_ln_to_fm(xA, BC["xA"], True, cols0, ptps)
                pr.op("vector", lambda eng: eng.tensor_copy(
                    isx_keep[0][:, :], cols0["isx"][0][:, :]),
                    reads=[cols0["isx"][1]], writes=[isx_keep[1]])

            with pr.phase("l0f_gateup"):
                gt = [pr.sb("gt", [128, 512], BF16) for _ in range(NCH)]
                ut = [pr.sb("ut", [128, 512], BF16) for _ in range(NCH)]
                gps = pr.ps("gps", [128, 512], F32)
                ups = pr.ps("ups", [128, 512], F32)
                lnps = pr.ps("lnps", [128, 512], F32)
                sgs = pr.sb("sgs", [128, 512], F32)
                hb = [pr.sb("hb", [128, 512], BF16) for _ in range(2)]
                hl = pr.sb("hl", [128, 512], F32)
                lnb = pr.sb("lnb", [128, 512], F32)
                tred = pr.sb("tred", [128, 2], F32)
                scg = pr.sb("scg", [128, NTS], F32)
                scu = pr.sb("scu", [128, NTS], F32)
                lnr = pr.sb("lnr", [1, INTER0], F32)
                pr.dma("gpsimd", lambda eng: eng.dma_start(out=lnr[0][:, :],
                                                           in_=lnw_row),
                       writes=[lnr[1]])
                pr.op("vector", lambda eng: eng.memset(hstat[0][:, 0:2 * NTS], 0.0),
                      writes=[hstat[1]])
                pr.op("vector", lambda eng: eng.tensor_scalar(
                    scg[0][:, :], isx_keep[0][:, :], wm_keep[0][:, 0:1], 0.0,
                    ALU.mult, ALU.add), reads=[isx_keep[1], wm_keep[1]],
                    writes=[scg[1]])
                pr.op("vector", lambda eng: eng.tensor_scalar(
                    scu[0][:, :], isx_keep[0][:, :], wm_keep[0][:, 1:2], 0.0,
                    ALU.mult, ALU.add), reads=[isx_keep[1], wm_keep[1]],
                    writes=[scu[1]])
                for jc in range(NJC0):
                    pr.op("tensor", lambda eng, jc=jc: eng.matmul(
                        lnps[0][:, :], onesf[0][:, :],
                        lnr[0][0:1, jc * 512:(jc + 1) * 512], start=True, stop=True),
                        reads=[onesf[1], lnr[1]], writes=[lnps[1]])
                    pr.op("scalar", lambda eng: eng.copy(
                        lnb[0][:, :], lnps[0][:, :]), reads=[lnps[1]], writes=[lnb[1]])
                    for c in range(NCH):
                        pr.dma("sync", lambda eng, jc=jc, c=c: eng.dma_start(
                            out=gt[c][0][:, :],
                            in_=tern_g[jc * 512:(jc + 1) * 512, c * 128:(c + 1) * 128],
                            transpose=True), reads=[BC["tern_g"]], writes=[gt[c][1]])
                        pr.dma("sync", lambda eng, jc=jc, c=c: eng.dma_start(
                            out=ut[c][0][:, :],
                            in_=tern_u[jc * 512:(jc + 1) * 512, c * 128:(c + 1) * 128],
                            transpose=True), reads=[BC["tern_u"]], writes=[ut[c][1]])
                    for ts in range(NTS):
                        for c in range(NCH):
                            pr.op("tensor", lambda eng, c=c, ts=ts: eng.matmul(
                                gps[0][:, :], A(c, ts * 128, 128), gt[c][0][:, :],
                                start=(c == 0), stop=(c == NCH - 1)),
                                reads=[arena[1], gt[c][1]], writes=[gps[1]])
                        for c in range(NCH):
                            pr.op("tensor", lambda eng, c=c, ts=ts: eng.matmul(
                                ups[0][:, :], A(c, ts * 128, 128), ut[c][0][:, :],
                                start=(c == 0), stop=(c == NCH - 1)),
                                reads=[arena[1], ut[c][1]], writes=[ups[1]])
                        h = hb[ts % 2]
                        pr.op("scalar", lambda eng, ts=ts: eng.activation(
                            sgs[0][:, :], gps[0][:, :], ACT.Silu, bias=0.0,
                            scale=scg[0][:, ts:ts + 1]),
                            reads=[gps[1], scg[1]], writes=[sgs[1]])
                        pr.op("vector", lambda eng, ts=ts, h=h: eng.scalar_tensor_tensor(
                            h[0][:, :], sgs[0][:, :], scu[0][:, ts:ts + 1],
                            ups[0][:, :], ALU.mult, ALU.mult),
                            reads=[sgs[1], scu[1], ups[1]], writes=[h[1]])
                        pr.op("vector", lambda eng, h=h: eng.tensor_tensor(
                            hl[0][:, :], h[0][:, :], lnb[0][:, :], ALU.mult),
                            reads=[h[1], lnb[1]], writes=[hl[1]])
                        pr.op("vector", lambda eng: eng.reduce_max(
                            tred[0][:, 0:1], hl[0][:, :], axis=AX.X,
                            apply_absolute_value=True), reads=[hl[1]], writes=[tred[1]])
                        pr.op("vector", lambda eng, ts=ts: eng.tensor_tensor(
                            hstat[0][:, ts:ts + 1], hstat[0][:, ts:ts + 1],
                            tred[0][:, 0:1], ALU.max),
                            reads=[hstat[1], tred[1]], writes=[hstat[1]])
                        pr.op("scalar", lambda eng, h=h: eng.activation(
                            hl[0][:, :], h[0][:, :], ACT.Square, bias=0.0, scale=1.0,
                            accum_out=tred[0][:, 1:2]),
                            reads=[h[1]], writes=[hl[1], tred[1]])
                        pr.op("vector", lambda eng, ts=ts: eng.tensor_tensor(
                            hstat[0][:, NTS + ts:NTS + ts + 1],
                            hstat[0][:, NTS + ts:NTS + ts + 1],
                            tred[0][:, 1:2], ALU.add),
                            reads=[hstat[1], tred[1]], writes=[hstat[1]])
                        pr.dma("gpsimd", lambda eng, ts=ts, jc=jc, h=h: eng.dma_start(
                            out=h_spill[ts * 128:(ts + 1) * 128,
                                        jc * 512:(jc + 1) * 512], in_=h[0][:, :]),
                            reads=[h[1]], writes=[BC["h_spill"]])
                pr.op("scalar", lambda eng: eng.activation(
                    hstat[0][:, 5 * NTS:6 * NTS], hstat[0][:, NTS:2 * NTS],
                    ACT.Sqrt, bias=EPS_RMS, scale=1.0 / INTER0),
                    reads=[hstat[1]], writes=[hstat[1]])
                pr.op("vector", lambda eng: eng.reciprocal(
                    hstat[0][:, 2 * NTS:3 * NTS], hstat[0][:, 5 * NTS:6 * NTS]),
                    reads=[hstat[1]], writes=[hstat[1]])
                pr.op("vector", lambda eng: eng.tensor_tensor(
                    hstat[0][:, 0:NTS], hstat[0][:, 0:NTS],
                    hstat[0][:, 2 * NTS:3 * NTS], ALU.mult),
                    reads=[hstat[1]], writes=[hstat[1]])
                pr.op("vector", lambda eng: eng.tensor_single_scalar(
                    hstat[0][:, 0:NTS], hstat[0][:, 0:NTS], EPS_Q, ALU.max),
                    reads=[hstat[1]], writes=[hstat[1]])
                pr.op("vector", lambda eng: eng.tensor_scalar(
                    hstat[0][:, 3 * NTS:4 * NTS], hstat[0][:, 0:NTS],
                    wm_keep[0][:, 2:3], 1.0 / 127.0, ALU.mult, ALU.mult),
                    reads=[hstat[1], wm_keep[1]], writes=[hstat[1]])
                pr.op("vector", lambda eng: eng.reciprocal(
                    hstat[0][:, 4 * NTS:5 * NTS], hstat[0][:, 0:NTS]),
                    reads=[hstat[1]], writes=[hstat[1]])
                pr.op("vector", lambda eng: eng.tensor_single_scalar(
                    hstat[0][:, 4 * NTS:5 * NTS], hstat[0][:, 4 * NTS:5 * NTS],
                    127.0, ALU.mult), reads=[hstat[1]], writes=[hstat[1]])
                pr.op("vector", lambda eng: eng.tensor_tensor(
                    hstat[0][:, 2 * NTS:3 * NTS], hstat[0][:, 2 * NTS:3 * NTS],
                    hstat[0][:, 4 * NTS:5 * NTS], ALU.mult),
                    reads=[hstat[1]], writes=[hstat[1]])
                ctps = pr.ps("ctps", [1, 128], F32)
                for ts in range(NTS):
                    pr.op("tensor", lambda eng, ts=ts: eng.transpose(
                        ctps[0][:, :], hstat[0][:, 2 * NTS + ts:2 * NTS + ts + 1],
                        identf[0][:, :]),
                        reads=[hstat[1], identf[1]], writes=[ctps[1]])
                    pr.op("scalar", lambda eng, ts=ts: eng.copy(
                        crows[0][0:1, ts * 128:(ts + 1) * 128], ctps[0][:, :]),
                        reads=[ctps[1]], writes=[crows[1]])

            with pr.phase("l0f_down"):
                wdt = [pr.sb("wdt", [128, 512], BF16) for _ in range(2)]
                hts = [pr.sb("hts", [128, 512], BF16) for _ in range(2)]
                v1 = pr.sb("v1", [128, 512], F32)
                r1 = pr.sb("r1", [128, 512], F32)
                r2 = pr.sb("r2", [128, 512], F32)
                cbc = pr.sb("cbc", [128, 512], F32)
                lncolt = pr.sb("lncolt", [128, NJ0], F32)
                oacc = pr.sb("oacc", [128, 512], F32)
                xres = [pr.sb("xres", [128, 512], F32) for _ in range(2)]
                dps = [pr.ps("dps", [128, 512], F32) for _ in range(4)]
                cps = pr.ps("cps", [128, 512], F32)
                pr.dma("gpsimd", lambda eng: eng.dma_start(
                    out=lncolt[0][:, :], in_=lnw_colt), writes=[lncolt[1]])
                for tc in range(NTC):
                    for i in range(4):
                        ts = tc * 4 + i
                        pr.op("tensor", lambda eng, ts=ts, i=i: eng.matmul(
                            cps[0][:, i * 128:(i + 1) * 128], onesf[0][:, :],
                            crows[0][0:1, ts * 128:(ts + 1) * 128],
                            start=True, stop=True),
                            reads=[onesf[1], crows[1]], writes=[cps[1]])
                    pr.op("scalar", lambda eng: eng.copy(
                        cbc[0][:, :], cps[0][:, :]), reads=[cps[1]], writes=[cbc[1]])
                    for j in range(NJ0):
                        ht = hts[j % 2]
                        pr.dma("sync", lambda eng, tc=tc, j=j, ht=ht: eng.dma_start(
                            out=ht[0][:, :],
                            in_=h_spill[tc * 512:(tc + 1) * 512,
                                        j * 128:(j + 1) * 128], transpose=True),
                            reads=[BC["h_spill"]], writes=[ht[1]])
                        pr.op("scalar", lambda eng, j=j, ht=ht: eng.activation(
                            v1[0][:, :], ht[0][:, :], ACT.Identity, bias=0.0,
                            scale=lncolt[0][:, j:j + 1]),
                            reads=[ht[1], lncolt[1]], writes=[v1[1]])
                        pr.op("vector", lambda eng: eng.tensor_tensor(
                            v1[0][:, :], v1[0][:, :], cbc[0][:, :], ALU.mult),
                            reads=[v1[1], cbc[1]], writes=[v1[1]])
                        emit_round(arena[0][:, j * 512:(j + 1) * 512], arena[1],
                                   v1[0][:, :], v1[1], r1, r2, 512)
                    for dc in range(NTC):
                        for j in range(NJ0):
                            w = wdt[j % 2]
                            pr.dma("sync", lambda eng, dc=dc, j=j, w=w: eng.dma_start(
                                out=w[0][:, :],
                                in_=tern_d[dc * 512:(dc + 1) * 512,
                                           j * 128:(j + 1) * 128], transpose=True),
                                reads=[BC["tern_d"]], writes=[w[1]])
                            for i in range(4):
                                pr.op("tensor", lambda eng, i=i, j=j, w=w: eng.matmul(
                                    dps[i][0][:, :],
                                    arena[0][:, j * 512 + i * 128: j * 512 + (i + 1) * 128],
                                    w[0][:, :],
                                    start=(j == 0), stop=(j == NJ0 - 1)),
                                    reads=[arena[1], w[1]], writes=[dps[i][1]])
                        for i in range(4):
                            ts = tc * 4 + i
                            xr = xres[i % 2]
                            pr.dma("sync", lambda eng, ts=ts, dc=dc, xr=xr: eng.dma_start(
                                out=xr[0][:, :],
                                in_=xA[ts * 128:(ts + 1) * 128,
                                       dc * 512:(dc + 1) * 512]),
                                reads=[BC["xA"]], writes=[xr[1]])
                            pr.op("scalar", lambda eng, i=i, ts=ts: eng.activation(
                                oacc[0][:, :], dps[i][0][:, :], ACT.Identity,
                                bias=0.0, scale=hstat[0][:, 3 * NTS + ts:3 * NTS + ts + 1]),
                                reads=[dps[i][1], hstat[1]], writes=[oacc[1]])
                            pr.op("vector", lambda eng, xr=xr: eng.tensor_tensor(
                                oacc[0][:, :], oacc[0][:, :], xr[0][:, :], ALU.add),
                                reads=[oacc[1], xr[1]], writes=[oacc[1]])
                            pr.dma("sync", lambda eng, ts=ts, dc=dc: eng.dma_start(
                                out=xB[ts * 128:(ts + 1) * 128,
                                       dc * 512:(dc + 1) * 512], in_=oacc[0][:, :]),
                                reads=[oacc[1]], writes=[BC["xB"]])


        def emit_l1f():
            with pr.phase("l1f_combine"):
                cols1 = make_cols(False)
                ptps = pr.ps("ptp", [128, 128], BF16)
                emit_ln_to_fm(xC, BC["xC"], False, cols1, ptps)
                gwb = pr.sb("gwb2", [15, 2048], BF16)
                gwt = pr.sb("gwt2", [128, NCH * 15], BF16)
                gtp = pr.ps("gtp2", [128, 15], BF16)
                lps = pr.ps("lps2", [128, 15], F32)
                lg = pr.sb("lg2", [128, 15], F32)
                c1 = pr.sb("c1_2", [128, 15], F32)
                mskd = pr.sb("mskd2", [128, 15], F32)
                pcol = pr.sb("pcol2", [128, 4], F32)
                pr.dma("gpsimd", lambda eng: eng.dma_start(out=gwb[0][:, :],
                                                           in_=gatew),
                       writes=[gwb[1]])
                for c in range(NCH):
                    pr.op("tensor", lambda eng, c=c: eng.transpose(
                        gtp[0][:, :], gwb[0][:, c * 128:(c + 1) * 128],
                        ident[0][0:15, 0:15]),
                        reads=[gwb[1], ident[1]], writes=[gtp[1]])
                    pr.op("scalar", lambda eng, c=c: eng.copy(
                        gwt[0][:, c * 15:(c + 1) * 15], gtp[0][:, :]),
                        reads=[gtp[1]], writes=[gwt[1]])
                pr.op("vector", lambda eng: eng.memset(
                    cmb[0][:, :], 1.0), writes=[cmb[1]])
                for ts in range(NTS):
                    for c in range(NCH):
                        pr.op("tensor", lambda eng, ts=ts, c=c: eng.matmul(
                            lps[0][:, :], A(c, ts * 128, 128),
                            gwt[0][:, c * 15:(c + 1) * 15],
                            start=(c == 0), stop=(c == NCH - 1)),
                            reads=[arena[1], gwt[1]], writes=[lps[1]])
                    pr.op("vector", lambda eng: eng.reduce_max(
                        pcol[0][:, 0:1], lps[0][:, :], axis=AX.X),
                        reads=[lps[1]], writes=[pcol[1]])
                    pr.op("vector", lambda eng: eng.tensor_single_scalar(
                        pcol[0][:, 1:2], pcol[0][:, 0:1], -1.0, ALU.mult),
                        reads=[pcol[1]], writes=[pcol[1]])
                    pr.op("scalar", lambda eng: eng.activation(
                        lg[0][:, :], lps[0][:, :], ACT.Exp,
                        bias=pcol[0][:, 1:2], scale=1.0, accum_out=pcol[0][:, 2:3]),
                        reads=[lps[1], pcol[1]], writes=[lg[1], pcol[1]])
                    pr.op("vector", lambda eng: eng.reciprocal(
                        pcol[0][:, 3:4], pcol[0][:, 2:3]),
                        reads=[pcol[1]], writes=[pcol[1]])
                    pr.op("scalar", lambda eng: eng.activation(
                        lg[0][:, :], lg[0][:, :], ACT.Identity, bias=0.0,
                        scale=pcol[0][:, 3:4]), reads=[lg[1], pcol[1]], writes=[lg[1]])
                    pr.op("vector", lambda eng: eng.reduce_max(
                        pcol[0][:, 0:1], lg[0][:, :], axis=AX.X),
                        reads=[lg[1]], writes=[pcol[1]])
                    pr.op("vector", lambda eng: eng.tensor_scalar(
                        c1[0][:, :], lg[0][:, :], pcol[0][:, 0:1], NEG,
                        ALU.is_ge, ALU.mult), reads=[lg[1], pcol[1]], writes=[c1[1]])
                    pr.op("vector", lambda eng: eng.tensor_tensor(
                        mskd[0][:, :], lg[0][:, :], c1[0][:, :], ALU.add),
                        reads=[lg[1], c1[1]], writes=[mskd[1]])
                    pr.op("vector", lambda eng: eng.reduce_max(
                        pcol[0][:, 1:2], mskd[0][:, :], axis=AX.X),
                        reads=[mskd[1]], writes=[pcol[1]])
                    pr.op("vector", lambda eng: eng.tensor_scalar(
                        c1[0][:, :], lg[0][:, :], pcol[0][:, 1:2], 0.0,
                        ALU.is_ge, ALU.add), reads=[lg[1], pcol[1]], writes=[c1[1]])
                    pr.op("vector", lambda eng, ts=ts: eng.tensor_tensor(
                        cmb[0][:, ts * 16: ts * 16 + 15], c1[0][:, :],
                        lg[0][:, :], ALU.mult), reads=[c1[1], lg[1]], writes=[cmb[1]])

            with pr.phase("l1f_moe"):
                NQ = 4
                TQ = S // NQ
                NTSQ = TQ // 128
                yacc = pr.sb("yacc", [128, NTSQ * DM], F32)
                w13t = [pr.sb("w13t", [128, INTER1], BF16) for _ in range(NCH)]
                sgst = pr.sb("sgst", [128, NTSQ * INTER1], BF16)
                hhst = pr.sb("hhst", [128, NTSQ * INTER1], BF16)
                hfm = pr.sb("hfm", [128, NM1 * TQ], BF16)
                w2t = [pr.sb("w2t", [128, 512], BF16) for _ in range(2)]
                hps = pr.ps("hps", [128, 512], F32)
                ptps = pr.ps("ptp", [128, 128], BF16)
                dps = [pr.ps("dps", [128, 512], F32) for _ in range(4)]
                xres = [pr.sb("xres", [128, 512], F32)]
                oacc = pr.sb("oacc", [128, 512], F32)
                exps = [(w1_bf[i], w3_bf[i], w2_bf[i], BC["w1_bf"], BC["w3_bf"],
                         BC["w2_bf"], i) for i in range(NEXP)]
                exps.append((sw1_bf, sw3_bf, sw2_bf, BC["sw1_bf"], BC["sw3_bf"],
                             BC["sw2_bf"], 15))
                for q in range(NQ):
                    pr.op("vector", lambda eng: eng.memset(yacc[0][:, :], 0.0),
                          writes=[yacc[1]])
                    for (e1, e3, e2, b1, b3, b2, ei) in exps:
                        for c in range(NCH):
                            pr.dma("sync", lambda eng, c=c, e1=e1: eng.dma_start(
                                out=w13t[c][0][:, :],
                                in_=e1[:, c * 128:(c + 1) * 128], transpose=True),
                                reads=[b1], writes=[w13t[c][1]])
                        for i in range(NTSQ):
                            ts = q * NTSQ + i
                            off = 0
                            for jw in L1JC:
                                for c in range(NCH):
                                    pr.op("tensor", lambda eng, c=c, ts=ts, off=off, jw=jw: eng.matmul(
                                        hps[0][:, 0:jw], A(c, ts * 128, 128),
                                        w13t[c][0][:, off:off + jw],
                                        start=(c == 0), stop=(c == NCH - 1)),
                                        reads=[arena[1], w13t[c][1]], writes=[hps[1]])
                                pr.op("scalar", lambda eng, i=i, off=off, jw=jw: eng.activation(
                                    sgst[0][:, i * INTER1 + off: i * INTER1 + off + jw],
                                    hps[0][:, 0:jw], ACT.Silu, bias=0.0, scale=1.0),
                                    reads=[hps[1]], writes=[sgst[1]])
                                off += jw
                        for c in range(NCH):
                            pr.dma("sync", lambda eng, c=c, e3=e3: eng.dma_start(
                                out=w13t[c][0][:, :],
                                in_=e3[:, c * 128:(c + 1) * 128], transpose=True),
                                reads=[b3], writes=[w13t[c][1]])
                        for i in range(NTSQ):
                            ts = q * NTSQ + i
                            off = 0
                            for jw in L1JC:
                                for c in range(NCH):
                                    pr.op("tensor", lambda eng, c=c, ts=ts, off=off, jw=jw: eng.matmul(
                                        hps[0][:, 0:jw], A(c, ts * 128, 128),
                                        w13t[c][0][:, off:off + jw],
                                        start=(c == 0), stop=(c == NCH - 1)),
                                        reads=[arena[1], w13t[c][1]], writes=[hps[1]])
                                pr.op("vector", lambda eng, i=i, ts=ts, off=off, jw=jw, ei=ei: eng.scalar_tensor_tensor(
                                    hhst[0][:, i * INTER1 + off: i * INTER1 + off + jw],
                                    sgst[0][:, i * INTER1 + off: i * INTER1 + off + jw],
                                    cmb[0][:, ts * 16 + ei: ts * 16 + ei + 1],
                                    hps[0][:, 0:jw], ALU.mult, ALU.mult),
                                    reads=[sgst[1], cmb[1], hps[1]], writes=[hhst[1]])
                                off += jw
                        for i in range(NTSQ):
                            for m in range(NM1):
                                emit_tr128(
                                    ptps,
                                    hfm[0][:, m * TQ + i * 128: m * TQ + (i + 1) * 128],
                                    hfm[1],
                                    hhst[0][:, i * INTER1 + m * 128:
                                            i * INTER1 + (m + 1) * 128], hhst[1])
                        for dc in range(NTC):
                            for m in range(NM1):
                                w = w2t[m % 2]
                                pr.dma("sync", lambda eng, dc=dc, m=m, w=w, e2=e2: eng.dma_start(
                                    out=w[0][:, :],
                                    in_=e2[dc * 512:(dc + 1) * 512,
                                           m * 128:(m + 1) * 128], transpose=True),
                                    reads=[b2], writes=[w[1]])
                                for i in range(NTSQ):
                                    pr.op("tensor", lambda eng, i=i, m=m, w=w: eng.matmul(
                                        dps[i][0][:, :],
                                        hfm[0][:, m * TQ + i * 128: m * TQ + (i + 1) * 128],
                                        w[0][:, :],
                                        start=(m == 0), stop=(m == NM1 - 1)),
                                        reads=[hfm[1], w[1]], writes=[dps[i][1]])
                            for i in range(NTSQ):
                                pr.op("vector", lambda eng, i=i, dc=dc: eng.tensor_tensor(
                                    yacc[0][:, i * DM + dc * 512: i * DM + (dc + 1) * 512],
                                    yacc[0][:, i * DM + dc * 512: i * DM + (dc + 1) * 512],
                                    dps[i][0][:, :], ALU.add),
                                    reads=[yacc[1], dps[i][1]], writes=[yacc[1]])
                    for i in range(NTSQ):
                        ts = q * NTSQ + i
                        for dc in range(NTC):
                            xr = xres[0]
                            pr.dma("sync", lambda eng, ts=ts, dc=dc, xr=xr: eng.dma_start(
                                out=xr[0][:, :],
                                in_=xC[ts * 128:(ts + 1) * 128,
                                       dc * 512:(dc + 1) * 512]),
                                reads=[BC["xC"]], writes=[xr[1]])
                            pr.op("vector", lambda eng, i=i, dc=dc, xr=xr: eng.tensor_tensor(
                                oacc[0][:, :],
                                yacc[0][:, i * DM + dc * 512: i * DM + (dc + 1) * 512],
                                xr[0][:, :], ALU.add),
                                reads=[yacc[1], xr[1]], writes=[oacc[1]])
                            pr.dma("sync", lambda eng, ts=ts, dc=dc: eng.dma_start(
                                out=out[ts * 128:(ts + 1) * 128,
                                        dc * 512:(dc + 1) * 512], in_=oacc[0][:, :]),
                                reads=[oacc[1]], writes=[BC["out"]])


        if STAGES >= 3:
            emit_l0f()
        if STAGES >= 4:
            attention_block("l1a", xB, BC["xB"], xC, BC["xC"], False)
        if STAGES >= 5:
            emit_l1f()

        with pr.phase("finish"):
            pr.barrier_wait_all("sync")

    build_program.trace = pr.trace
    return nc, pr.n_instr


def _host_consts():
    ident = np.eye(128, dtype=BF)
    identf = np.eye(128, dtype=np.float32)
    P = np.zeros((128, 128), dtype=np.float32)
    for d in range(64):
        P[d, d + 64] = -1.0
        P[64 + d, d] = 1.0
    ppermt = np.ascontiguousarray(P.T).astype(BF)
    masks = np.zeros((4 * 128, 512), dtype=np.float32)
    for m in range(4):
        p = np.arange(128)[:, None]
        f = np.arange(512)[None, :]
        masks[m * 128:(m + 1) * 128] = np.where(f > p + 128 * m, NEG, 0.0)
    masks = masks.astype(BF)
    ones = np.ones((1, 128), dtype=np.float32)
    return ident, identf, ppermt, masks, ones


_CACHE = {}


def kernel(L0_attn_x, L0_attn_Wq, L0_attn_Wk, L0_attn_Wv, L0_attn_Wo,
           L0_ffn_w_gate, L0_ffn_w_up, L0_ffn_w_down, L0_ffn_ln_weight,
           L1_attn_W_lat, L1_attn_b_lat, L1_attn_W_d, L1_attn_b_d,
           L1_attn_W_proj, L1_attn_b_proj, L1_ffn_gate_w, L1_ffn_w1,
           L1_ffn_w2, L1_ffn_w3, L1_ffn_sw1, L1_ffn_sw2, L1_ffn_sw3):
    if "nc" not in _CACHE:
        _CACHE["nc"], n = build_program()
        _CACHE["n_instr"] = n
    nc = _CACHE["nc"]

    ident, identf, ppermt, masks, ones = _host_consts()
    f32 = lambda a: np.ascontiguousarray(a, dtype=np.float32)
    shared = dict(
        wq=f32(L0_attn_Wq), wk=f32(L0_attn_Wk), wv=f32(L0_attn_Wv),
        wo=f32(L0_attn_Wo), wgate=f32(L0_ffn_w_gate), wup=f32(L0_ffn_w_up),
        wdown=f32(L0_ffn_w_down),
        lnw_row=f32(L0_ffn_ln_weight).reshape(1, INTER0),
        lnw_colt=np.ascontiguousarray(
            f32(L0_ffn_ln_weight).reshape(NJ0, 128).T),
        wlat=f32(L1_attn_W_lat), blat=f32(L1_attn_b_lat).reshape(128, 1),
        wd=f32(L1_attn_W_d),
        bdh=np.ascontiguousarray(f32(L1_attn_b_d).reshape(16, 128).T),
        wproj=f32(L1_attn_W_proj),
        bproj_row=f32(L1_attn_b_proj).reshape(1, DM),
        gatew=f32(L1_ffn_gate_w), w1=f32(L1_ffn_w1), w2=f32(L1_ffn_w2),
        w3=f32(L1_ffn_w3), sw1=f32(L1_ffn_sw1), sw2=f32(L1_ffn_sw2),
        sw3=f32(L1_ffn_sw3),
        ident_c=ident, identf_c=identf, ppermt_c=ppermt, masks_c=masks,
        ones_c=ones,
        csts_c=np.tile(np.array([0.25, math.pi / 2.0, 128.0, EPS_LN, EPS_RMS,
                                 1.0, 0.0, 0.0], np.float32), (128, 1)),
    )
    x = f32(L0_attn_x)
    in_maps = [dict(shared, x_in=np.ascontiguousarray(x[b]))
               for b in range(x.shape[0])]
    res = run_bass_kernel_spmd(nc, in_maps, core_ids=list(range(8)))
    outs = [np.asarray(r["out"], dtype=np.float32) for r in res.results]
    return np.stack(outs, axis=0)
